# version 03
# baseline (speedup 1.0000x reference)
import numpy as np
import ml_dtypes
from contextlib import ExitStack
import dataclasses
import concourse.bass as bass
import concourse.mybir as mybir
from concourse.bass_utils import run_bass_kernel_spmd
from concourse.expressions import RuntimeValue

F32 = mybir.dt.float32
BF16 = mybir.dt.bfloat16
U32 = mybir.dt.uint32
AF = mybir.ActivationFunctionType
ALU = mybir.AluOpType
E = mybir.EngineType
ds = bass.ds

NCORES = 8
SEQ = 16384
DM = 1024
PH = 16
PD = 64
PM = 256
FH = 8
FD = 64
FIN = 512
HID = 2730
WIN = 512
LN16 = float(np.log(16.0))

ST = 512
STD = 256

HALO_C = 2560
HALO_D1 = 1536
WU_D1 = 768
HALO_D2 = 1280
HALO_E1 = 256
WU_E1 = 768

OUT_ROWS = 4096
SPX1_COLS = 5120
DEBUG = False
XC_TOK = 6656
N_STG = SEQ // ST

HID_PAD = 2816
ALLE = [E.SP, E.Activation, E.DVE, E.PE, E.Pool]
COMPE = [E.SP, E.Activation, E.DVE, E.PE]

(CFG_B_C, CFG_N_LO, CFG_N_FEED, CFG_N_C, CFG_S, CFG_B_D1, CFG_N_D1W, CFG_N_D1,
 CFG_B_D2, CFG_N_D2, CFG_B_E1, CFG_N_E1W, CFG_N_E1, CFG_N_E2, CFG_B_D1W,
 CFG_B_E1W, CFG_T128, CFG_OFFX1) = range(18)



def plan_shards(Ts=None):
    if Ts is None:
        Ts = [SEQ // NCORES] * NCORES
    assert sum(Ts) == SEQ and all(t % ST == 0 for t in Ts)
    plans = []
    s = 0
    for c in range(NCORES):
        S, Eend = s, s + Ts[c]
        p = dict(S=S, E=Eend, T=Ts[c])
        p["B_C"] = max(0, S - HALO_C)
        p["B_D1"] = max(0, S - HALO_D1)
        p["B_D1W"] = max(0, S - HALO_D1 - WU_D1)
        p["B_D2"] = max(0, S - HALO_D2)
        p["B_E1"] = max(0, S - HALO_E1)
        p["B_E1W"] = max(0, S - HALO_E1 - WU_E1)
        p["n_feed"] = p["B_C"] // ST
        p["n_c"] = (Eend - p["B_C"]) // ST
        p["n_lo"] = (Eend - p["B_C"]) // 128
        p["n_d1w"] = (p["B_D1"] - p["B_D1W"]) // STD
        p["n_d1"] = (Eend - p["B_D1"]) // STD
        p["n_d2"] = (Eend - p["B_D2"]) // STD
        p["n_e1w"] = (p["B_E1"] - p["B_E1W"]) // STD
        p["n_e1"] = (Eend - p["B_E1"]) // STD
        p["n_e2"] = Ts[c] // STD
        plans.append(p)
        s = Eend
    return plans


def build_rotary_tables():
    n = SEQ
    half = np.arange(0, FD, 2, dtype=np.float64)
    inv_freq = 1.0 / 10000.0 ** (half / FD)
    t = np.arange(n, dtype=np.float64)
    freqs = t[:, None] * inv_freq[None, :]
    freqs = np.concatenate([freqs, freqs], axis=-1)
    base = (half + 0.4 * FD) / (1.4 * FD)
    power = (t - n // 2) / float(WIN // 2)
    scale = base[None, :] ** power[:, None]
    scale = np.concatenate([scale, scale], axis=-1)
    cos, sin = np.cos(freqs), np.sin(freqs)
    def two(x):
        xT = x.T.astype(np.float32)
        xT = np.concatenate([xT, xT], 0)
        return np.concatenate([xT, np.zeros((128, STD), np.float32)], 1)
    return (two(cos * scale), two(sin * scale),
            two(cos / scale), two(sin / scale))


def build_rotp():
    P = np.zeros((128, 128), np.float32)
    for hb in (0, 64):
        for d in range(32):
            P[hb + d + 32, hb + d] = -1.0
            P[hb + d, hb + d + 32] = 1.0
    return P


def build_w1_perm():
    perm = []
    tiles = []
    off = 0
    i = 0
    while i * 128 < HID:
        w = min(128, HID - i * 128)
        a_cols = list(range(i * 128, i * 128 + w))
        g_cols = list(range(HID + i * 128, HID + i * 128 + w))
        perm.extend(a_cols)
        perm.extend(g_cols)
        tiles.append((off, off + w, w))
        off += 2 * w
        i += 1
    return np.array(perm, np.int64), tiles


def col_layout(vec, dm=DM):
    return np.ascontiguousarray(np.asarray(vec, np.float32).reshape(dm // 128, 128).T)



class Ctr:

    def __init__(self, nc, name, deltas):
        self.h = nc.semaphore(name).__enter__()
        self.nc = nc
        self.name = name
        self.deltas = deltas
        self.static = 0
        self.cur = 0
        self.in_loop = False

    def inc(self, instr, amt=1):
        instr.then_inc(self.h, amt)
        if self.in_loop:
            self.cur += amt
        else:
            self.static += amt
        return instr

    @property
    def delta(self):
        return self.deltas.get(self.name, 1 << 18)

    def at(self, i):
        return i * self.delta + (self.static + self.cur)

    def start_of(self, i):
        return i * self.delta + self.static

    def begin_body(self):
        self.in_loop = True
        self.cur = 0

    def end_body(self):
        self.in_loop = False
        d = self.deltas.get(self.name)
        if d is None:
            self.deltas[self.name] = self.cur
        else:
            assert d == self.cur, f"{self.name}: delta {d} -> {self.cur}"
        self.cur = 0


class MMRot:

    def __init__(self, p, banks, cur):
        self.p = p
        self.banks = banks
        self.cur = cur
        self.idx = 0
        self.guard = {b: None for b in banks}

    def group(self):
        b = self.banks[self.idx % len(self.banks)]
        self.idx += 1
        g = self.guard[b]
        if g is not None:
            ctr, K = g
            self.cur.wait(E.PE, ctr, K)
        return self.p.ps[b], b

    def done(self, b, ctr):
        self.guard[b] = (ctr, ctr.static + ctr.cur)


class DmaOff:

    def __init__(self, p, eng, maxv, name=None):
        self.p, self.eng, self.maxv = p, eng, maxv
        self.reg = p.nc.engines[eng].alloc_register(name)

    def init_to(self, expr, const=0):
        self.p.nc.engines[self.eng].reg_alu(self.reg, expr, const, ALU.add)

    def step(self, amt):
        self.p.nc.engines[self.eng].reg_alu(self.reg, self.reg, amt, ALU.add)

    def val(self):
        return RuntimeValue(val=self.reg, min_val=0, max_val=self.maxv)

    def ap(self, static_ap):
        return dataclasses.replace(static_ap, offset=self.val())

    def free(self):
        self.p.nc.engines[self.eng].free_register(self.reg)


class Cursors:

    def __init__(self, p, i_val, ctrs):
        self.p = p
        self.i = i_val
        self.state = {}

    def wait(self, eng, ctr, K=None):
        nc = self.p.nc
        if K is None:
            K = ctr.static + ctr.cur
        key = (eng, ctr.name)
        st = self.state.get(key)
        e = nc.engines[eng]
        if st is None:
            reg = e.alloc_register(None)
            st = [reg, None]
            self.state[key] = st
        reg, maxK = st
        if maxK is not None and K <= maxK:
            return
        if maxK is None:
            ih = self.i.val[eng]
            e.reg_alu(reg, ih, ctr.delta, ALU.mult)
            if K:
                e.reg_alu(reg, reg, K, ALU.add)
        else:
            e.reg_alu(reg, reg, K - maxK, ALU.add)
        e.wait_ge(ctr.h, reg)
        st[1] = K

    def finish(self):
        for (eng, _), (reg, _m) in self.state.items():
            self.p.nc.engines[eng].free_register(reg)


class P:
    pass



class Stream:

    def __init__(self, p, eng, maxv):
        self.p, self.eng, self.maxv = p, eng, maxv
        self.reg = p.nc.engines[eng].alloc_register(None)

    def mk(self):
        return RuntimeValue(val=self.reg, min_val=0, max_val=self.maxv)

    def setv(self, imm=0, add=(), sub=()):
        e = self.p.nc.engines[self.eng]

        def h(x):
            if isinstance(x, RuntimeValue):
                try:
                    return x.val[self.eng]
                except TypeError:
                    return x.val
            return x

        init = False
        for x in add:
            if init:
                e.reg_alu(self.reg, self.reg, h(x), ALU.add)
            else:
                e.reg_alu(self.reg, h(x), 0, ALU.add)
                init = True
        for x in sub:
            assert init
            e.reg_alu(self.reg, self.reg, h(x), ALU.subtract)
        if init:
            if imm:
                e.reg_alu(self.reg, self.reg, imm, ALU.add)
        else:
            e.reg_alu(self.reg, self.p.zero_regs[self.eng], imm, ALU.add)

    def bump(self, step):
        self.p.nc.engines[self.eng].reg_alu(self.reg, self.reg, step, ALU.add)


def stream(p, eng, maxv):
    key = (eng, maxv)
    st = p.streams.get(key)
    if st is None:
        st = Stream(p, eng, maxv)
        p.streams[key] = st
    return st


def rcalc(p, eng, maxv, imm=0, mul=None, add=(), sub=()):
    nc = p.nc
    e = nc.engines[eng]
    r = e.alloc_register(None)

    def h(x):
        if isinstance(x, RuntimeValue):
            try:
                return x.val[eng]
            except TypeError:
                return x.val
        return x

    init = False
    if mul is not None:
        e.reg_alu(r, h(mul[0]), mul[1], ALU.mult)
        init = True
    for x in add:
        if init:
            e.reg_alu(r, r, h(x), ALU.add)
        else:
            e.reg_alu(r, h(x), 0, ALU.add)
            init = True
    assert init
    for x in sub:
        e.reg_alu(r, r, h(x), ALU.subtract)
    if imm:
        e.reg_alu(r, r, imm, ALU.add)
    lst = getattr(p, "_snap_regs", None)
    if lst is not None:
        lst.append(r)
    else:
        p._pro_regs.append(r)
    return RuntimeValue(val=r, min_val=0, max_val=maxv)


def snapo(p, expr, maxv, engines=(E.SP,)):
    v = p.nc.snap(expr, engines=list(engines), min_val=0, max_val=maxv)
    try:
        regs = v.val
        lst = getattr(p, "_snap_regs", None)
        if lst is not None:
            for eng in engines:
                lst.append(regs[eng])
    except Exception:
        pass
    return v


def begin_snaps(p):
    p._snap_regs = []


def end_snaps(p):
    for r in p._snap_regs:
        p.nc.engines[r.engine].free_register(r)
    p._snap_regs = None


from contextlib import contextmanager


@contextmanager
def kfori(p, n, engines=ALLE):
    nc = p.nc
    loop_id = nc.next_id()
    loop_name = f"kfori_{loop_id}"
    loop_start = f"{loop_name}_loop"
    loop_end = f"{loop_name}_end"
    registers = nc.alloc_registers(f"{loop_name}_i", engines=engines)
    nc.regs_mov(registers, 0)
    nc.br(loop_start, engines=engines)
    max_val = getattr(n, "max_val", None)
    if max_val is not None:
        max_val -= 1
    snap_rv = None
    with nc.body(loop_start, valid_engines=engines):
        snap_rv = nc.snap(registers, min_val=0, max_val=max_val)
        yield snap_rv
        nc.regs_alu(registers, registers, 1, op=ALU.add)
        nc.br_lt(registers, n, on_true=loop_start, on_false=loop_end,
                 engines=engines)
    nc.switch_bb(loop_end)
    for e in engines:
        nc.engines[e].free_register(registers[e])
        try:
            nc.engines[e].free_register(snap_rv.val[e])
        except Exception:
            pass


def pro_mark(p):
    return len(p._pro_regs)


def pro_free(p, mark):
    for r in p._pro_regs[mark:]:
        p.nc.engines[r.engine].free_register(r)
    del p._pro_regs[mark:]

def drain_all(p, ctrs, n, engines=ALLE):
    nc = p.nc
    for e in engines:
        eng = nc.engines[e]
        r = eng.alloc_register(None)
        for c in ctrs:
            eng.reg_alu(r, n.val[e], c.delta, ALU.mult)
            if c.static:
                eng.reg_alu(r, r, c.static, ALU.add)
            eng.wait_ge(c.h, r)
        eng.free_register(r)


def body_start_waits(cur, ctrs, engines=COMPE):
    for e in engines:
        for c in ctrs:
            cur.wait(e, c, K=c.static)



def build_program(deltas):
    p = P()
    nc = bass.Bass("TRN2", target_bir_lowering=False)
    p.nc = nc
    p.deltas = deltas

    def din(name, shape, dt=F32):
        return nc.dram_tensor(name, shape, dt, kind="ExternalInput").ap()

    p.x0 = din("L0_attn_x", [SEQ + XC_TOK - ST, DM])
    p.q_w = din("L0_attn_q_w", [DM, DM])
    p.q_b = din("L0_attn_q_b", [1, DM])
    p.k_w = din("L0_attn_k_w", [DM, DM])
    p.k_b = din("L0_attn_k_b", [1, DM])
    p.v_w = din("L0_attn_v_w", [DM, DM])
    p.v_b = din("L0_attn_v_b", [1, DM])
    p.out_w = din("L0_attn_out_w", [DM, DM])
    p.out_b = din("L0_attn_out_b", [1, DM])
    p.projT_aug = din("projT_aug", [65, PM])
    p.gamma = din("L0_attn_gamma", [128, 8])
    p.beta = din("L0_attn_beta", [128, 8])
    p.wqkv = din("L0_ffn_wqkv", [2, DM, 3 * FIN])
    p.wo = din("L0_ffn_wo", [2, FIN, DM])
    p.w1 = din("L0_ffn_w1", [2, DM, 2 * HID])
    p.b1 = din("L0_ffn_b1", [2, 2 * HID])
    p.w2 = din("L0_ffn_w2", [2, HID_PAD, DM])
    p.b2 = din("L0_ffn_b2", [2, DM])

    p.cfg = din("cfg", [1, 32], U32)
    p.negi = din("negi", [128, ST])
    p.alen = din("alen", [128, 4])
    p.rot_all = din("rot_all", [128, 4 * (SEQ + STD)])
    p.rotp = din("rotp", [128, 128])
    p.ident = din("ident", [128, 128])
    p.rowp1 = din("rowp1", [128, 1])
    p.ffn_scl = din("ffn_scl", [128, 4])
    p.ffn_ln = din("ffn_ln", [128, 32])
    p.ffn_b1c = din("ffn_b1c", [128, 88])
    p.ffn_b2c = din("ffn_b2c", [128, 16])
    p.sel2m = din("sel2m", [2, 128])
    p.qm_d1 = din("qm_d1", [128, 32])
    p.qm_e1 = din("qm_e1", [128, 24])

    p.out = nc.dram_tensor("out", [OUT_ROWS, DM], F32, kind="ExternalOutput").ap()

    p.x0b = nc.dram_tensor("x0b", [SEQ + XC_TOK - ST, DM], BF16).ap()
    p.x0bT_d = nc.dram_tensor("x0bT_d", [DM, SEQ + ST], BF16).ap()
    p.xc_f32 = nc.dram_tensor("xc_f32", [XC_TOK, DM], F32).ap()
    p.xcb = nc.dram_tensor("xcb", [XC_TOK, DM], BF16).ap()
    p.lo_d = nc.dram_tensor("lo_d", [XC_TOK, DM], BF16).ap()
    p.loT_d = nc.dram_tensor("loT_d", [DM, XC_TOK], BF16).ap()
    p.sp_x1 = nc.dram_tensor("sp_x1", [DM, SPX1_COLS + ST], F32).ap()
    p.sp_bm = nc.dram_tensor("sp_bm", [DM, 4096 + STD], F32).ap()
    p.sp_l0 = nc.dram_tensor("sp_l0", [DM, 4096 + STD], F32).ap()
    p.sp_l1 = nc.dram_tensor("sp_l1", [DM, 3072 + STD], F32).ap()
    p.dbg1 = nc.dram_tensor("dbg1", [128, 12 * 4096], F32).ap()
    p.dbg2 = nc.dram_tensor("dbg2", [128, 12 * 4096], F32).ap()
    p.dbg3 = nc.dram_tensor("dbg3", [128, 12 * 4096], F32).ap()
    p.dbg4 = nc.dram_tensor("dbg4", [128, 12 * 4096], F32).ap()
    p.sp_oh = nc.dram_tensor("sp_oh", [DM, OUT_ROWS], BF16).ap()
    p.sp_ol = nc.dram_tensor("sp_ol", [DM, OUT_ROWS], BF16).ap()

    p.bar_sem = nc.semaphore("bar").__enter__()
    p.bar_cnt = 0

    def barrier():
        p.bar_cnt += 1
        for e in ALLE:
            nc.engines[e].nop().then_inc(p.bar_sem, 1)
        for e in ALLE:
            nc.engines[e].wait_ge(p.bar_sem, p.bar_cnt * len(ALLE))

    p.barrier = barrier

    p.cfg_s = nc.alloc_sbuf_tensor("cfg_s", [1, 32], U32)
    p.ones_b = nc.alloc_sbuf_tensor("ones_b", [128, 1], BF16)
    p.ones_r = nc.alloc_sbuf_tensor("ones_r", [1, ST], BF16)
    p.ones_r32 = nc.alloc_sbuf_tensor("ones_r32", [1, 128], F32)
    p.ps_t = nc.alloc_psum_tensor("ps_t", [128, 4096], F32)
    p.ps = [p.ps_t[:, i * 512:(i + 1) * 512] for i in range(8)]
    p.ps_bt = p.ps_t.bitcast(BF16)

    CONSTS = [1e-5, 1e-6, -0.5, 0.5, LN16, 0.125, 8.0, -80.0, 1.0 / DM]
    p.cst = nc.alloc_sbuf_tensor("cstp", [128, len(CONSTS)], F32)
    for j, v in enumerate(CONSTS):
        nc.const_aps.register(F32, float(v), p.cst[:, j:j + 1])

    sem_init = nc.semaphore("init").__enter__()
    nc.sync.dma_start(out=p.cfg_s[:, :], in_=p.cfg).then_inc(sem_init, 16)
    nc.gpsimd.memset(p.ones_b[:, :], 1.0).then_inc(sem_init, 1)
    nc.gpsimd.memset(p.ones_r[:, :], 1.0).then_inc(sem_init, 1)
    nc.gpsimd.memset(p.ones_r32[:, :], 1.0).then_inc(sem_init, 1)
    for j, v in enumerate(CONSTS):
        nc.gpsimd.memset(p.cst[:, j:j + 1], float(v)).then_inc(sem_init, 1)
    for e in ALLE:
        nc.engines[e].wait_ge(sem_init, 19 + len(CONSTS))

    p._pro_regs = []
    p.streams = {}
    p.zero_regs = {}
    for e in ALLE:
        z = nc.engines[e].alloc_register(f"zero_{e}")
        nc.engines[e].reg_mov(z, 0)
        p.zero_regs[e] = z

    def cfgv(idx, engines, maxv):
        v = nc.values_load(p.cfg_s[0:1, idx:idx + 1], engines=engines,
                           min_val=0, max_val=maxv,
                           skip_runtime_bounds_check=True)
        try:
            for e_ in engines:
                p._pro_regs.append(v.val[e_])
        except Exception:
            pass
        return v

    p.cfgv = cfgv

    import os
    _kph = os.environ.get("KPH", "all")
    phase_A(p)
    for e in ALLE:
        nc.engines[e].wait_ge(p.s_pad, 64)
    barrier()
    phase_BC(p)
    barrier()
    if _kph in ("all", "fa0", "fm0", "fa1", "fm1"):
        phase_FA(p, 0)
    if _kph in ("all", "fm0", "fa1", "fm1"):
        phase_FM(p, 0)
    if _kph in ("all", "fa1", "fm1"):
        phase_FA(p, 1)
    if _kph in ("all", "fm1"):
        phase_FM(p, 1)
    phase_F(p)
    barrier()
    mybir.codegen_inst_isa_subclasses(nc)
    return p



def phase_A(p):
    nc = p.nc
    sA = nc.semaphore("aA").__enter__()
    sT = nc.semaphore("aT").__enter__()
    sL = nc.semaphore("aLl").__enter__()
    sV = nc.semaphore("aVv").__enter__()
    p.stg_sem = nc.semaphore("stg").__enter__()

    s_cast = nc.semaphore("acst").__enter__()
    nc.gpsimd.dma_start(out=p.x0b[0:SEQ, :], in_=p.x0[0:SEQ, :]).then_inc(s_cast, 16)
    p.s_pad = nc.semaphore("apad").__enter__()
    s_padm = nc.semaphore("apadm").__enter__()
    with nc.sbuf_tensor("a_zpad", [128, 1024], F32) as zt:
        nc.gpsimd.memset(zt[:, :], 0.0).then_inc(s_padm, 1)
        nc.gpsimd.wait_ge(s_padm, 1)
        zt3 = zt[:, :].rearrange("q (dt n) -> q dt n", n=128)
        for sp in (p.sp_x1, p.sp_bm, p.sp_l0, p.sp_l1):
            dstp = sp[:, 0:128].rearrange("(dt q) n -> q dt n", q=128)
            nc.gpsimd.dma_start(out=dstp, in_=zt3).then_inc(p.s_pad, 16)
        for e in ALLE:
            nc.engines[e].wait_ge(p.s_pad, 64)
    bc = p.cfgv(CFG_B_C, [E.Pool], SEQ)
    bcs = nc.snap(bc * DM, engines=[E.Pool], min_val=0, max_val=SEQ * DM)
    ap_x0 = dataclasses.replace(p.x0[0:XC_TOK - ST, :], offset=bcs)
    nc.gpsimd.dma_start(out=p.xc_f32[0:XC_TOK - ST, :], in_=ap_x0).then_inc(sA, 16)
    nc.gpsimd.wait_ge(s_cast, 16)
    ap_x0b = dataclasses.replace(p.x0b[0:XC_TOK - ST, :], offset=bcs)
    nc.gpsimd.dma_start(out=p.xcb[0:XC_TOK - ST, :], in_=ap_x0b).then_inc(sA, 16)

    with (
        nc.sbuf_tensor("a_tt0", [128, 8 * ST], BF16) as tt0,
        nc.sbuf_tensor("a_tt1", [128, 8 * ST], BF16) as tt1,
        nc.sbuf_tensor("a_f32", [128, DM], F32) as tf,
        nc.sbuf_tensor("a_b16", [128, DM], BF16) as tb,
        nc.sbuf_tensor("a_lo", [128, DM], BF16) as tl,
    ):
        nc.sync.wait_ge(s_cast, 16)
        for j in range(N_STG):
            nc.sync.wait_ge(sT, 144 * j)
            nc.sync.nop().then_inc(p.stg_sem, 1)
            t = tt0 if j % 2 == 0 else tt1
            for dt in range(8):
                nc.sync.dma_start(out=t[:, dt * ST:(dt + 1) * ST],
                                  in_=p.x0b[j * ST:(j + 1) * ST, dt * 128:(dt + 1) * 128],
                                  transpose=True).then_inc(sT, 16)
            nc.sync.wait_ge(sT, 144 * j + 128)
            nc.sync.dma_start(
                out=p.x0bT_d.rearrange("(dt q) n -> q dt n", q=128)[:, :, j * ST:(j + 1) * ST],
                in_=t[:, :].rearrange("q (dt n) -> q dt n", n=ST)).then_inc(sT, 16)
        nc.sync.wait_ge(sT, 144 * N_STG)
        nc.sync.nop().then_inc(p.stg_sem, 1)

        nc.scalar.wait_ge(sA, 32)
        nlo = (XC_TOK - ST) // 128
        for j in range(nlo):
            if j >= 1:
                nc.scalar.wait_ge(sV, j)
            nc.scalar.dma_start(out=tf[:, :], in_=p.xc_f32[j * 128:(j + 1) * 128, :]).then_inc(sL, 16)
            nc.scalar.dma_start(out=tb[:, :], in_=p.xcb[j * 128:(j + 1) * 128, :]).then_inc(sL, 16)
            nc.vector.wait_ge(sL, j * 48 + 32)
            nc.vector.tensor_sub(tl[:, :], tf[:, :], tb[:, :]).then_inc(sV, 1)
            nc.scalar.wait_ge(sV, j + 1)
            nc.scalar.dma_start(out=p.lo_d[j * 128:(j + 1) * 128, :], in_=tl[:, :]).then_inc(sL, 16)

        nstt = (XC_TOK - ST) // ST
        base = N_STG * 144
        nc.sync.wait_ge(sL, nlo * 48)
        for j in range(nstt):
            t = tt0 if j % 2 == 0 else tt1
            nc.sync.wait_ge(sT, base + 144 * j)
            for dt in range(8):
                nc.sync.dma_start(out=t[:, dt * ST:(dt + 1) * ST],
                                  in_=p.lo_d[j * ST:(j + 1) * ST, dt * 128:(dt + 1) * 128],
                                  transpose=True).then_inc(sT, 16)
            nc.sync.wait_ge(sT, base + 144 * j + 128)
            nc.sync.dma_start(
                out=p.loT_d.rearrange("(dt q) n -> q dt n", q=128)[:, :, j * ST:(j + 1) * ST],
                in_=t[:, :].rearrange("q (dt n) -> q dt n", n=ST)).then_inc(sT, 16)
        nc.sync.wait_ge(sT, base + 144 * nstt)
        nc.sync.nop().then_inc(p.stg_sem, 1)
        p.stg_all = N_STG + 2



def alloc_perf(p, stk):
    nc = p.nc
    a = {}

    def sb(name, shape, dt):
        a[name] = stk.enter_context(nc.sbuf_tensor(name, shape, dt))

    sb("Wq", [128, 8 * DM], BF16)
    sb("Wk", [128, 8 * DM], BF16)
    sb("Wv", [128, 8 * DM], BF16)
    sb("Wo", [128, 8 * DM], BF16)
    sb("projT", [65, PM], BF16)
    sb("qbr", [1, DM], BF16)
    sb("kbr", [1, DM], BF16)
    sb("vb_row", [1, DM], BF16)
    sb("ob_row", [1, DM], BF16)
    sb("gam", [128, 8], F32)
    sb("bet", [128, 8], F32)
    sb("negi_s", [128, ST], F32)
    sb("alen_s", [128, 4], F32)
    sb("x0bT", [128, 8 * ST], BF16)
    sb("x0T", [128, 8 * ST], F32)
    sb("yln", [128, 8 * ST], F32)
    sb("gb1", [128, 8 * ST], F32)
    sb("ylnb", [128, 8 * ST], BF16)
    sb("sqb", [128, 8 * ST], BF16)
    sb("qhat", [128, 8 * ST], BF16)
    sb("khat", [128, 8 * ST], BF16)
    sb("vnat", [128, 4 * DM], BF16)
    sb("qa", [128, ST], BF16)
    sb("ka", [128, ST], BF16)
    sb("qsq", [128, ST], BF16)
    sb("qpT", [128, 2 * ST], BF16)
    sb("kpT", [128, 2 * ST], BF16)
    sb("kpn", [128, 4 * PM], BF16)
    sb("Am", [128, 4 * ST], BF16)
    sb("state", [128, PH * 2 * 65], F32)
    sb("stateb", [128, PH * 2 * 65], BF16)
    sb("dbc", [64, ST], F32)
    sb("r_mu", [1, ST], F32)
    sb("r_t1", [1, ST], F32)
    sb("r_t2", [1, ST], F32)
    sb("r_rstd", [1, ST], F32)
    assert nc.sbuf_bytes_remaining >= 0
    return a


def phase_BC(p):
    nc = p.nc
    with ExitStack() as stk:
        a = alloc_perf(p, stk)
        s_w = nc.semaphore("bc_w").__enter__()
        s_w2 = nc.semaphore("bc_w2").__enter__()
        g = nc.gpsimd
        g.dma_start(out=a["Wq"][:, :].rearrange("q (kt o) -> q kt o", kt=8), in_=p.q_w.rearrange("(kt q) o -> q kt o", q=128)).then_inc(s_w, 16)
        g.dma_start(out=a["Wk"][:, :].rearrange("q (kt o) -> q kt o", kt=8), in_=p.k_w.rearrange("(kt q) o -> q kt o", q=128)).then_inc(s_w, 16)
        g.dma_start(out=a["Wv"][:, :].rearrange("q (kt o) -> q kt o", kt=8), in_=p.v_w.rearrange("(kt q) o -> q kt o", q=128)).then_inc(s_w, 16)
        g.dma_start(out=a["Wo"][:, :].rearrange("q (kt o) -> q kt o", kt=8), in_=p.out_w.rearrange("(kt q) o -> q kt o", q=128)).then_inc(s_w, 16)
        g.dma_start(out=a["projT"][:, :], in_=p.projT_aug).then_inc(s_w, 16)
        g.dma_start(out=a["vb_row"][:, :], in_=p.v_b).then_inc(s_w, 16)
        g.dma_start(out=a["ob_row"][:, :], in_=p.out_b).then_inc(s_w, 16)
        nc.sync.dma_start(out=a["negi_s"][:, :], in_=p.negi).then_inc(s_w2, 16)
        nc.sync.dma_start(out=a["alen_s"][:, :], in_=p.alen).then_inc(s_w2, 16)
        g.dma_start(out=a["qbr"][:, :], in_=p.q_b).then_inc(s_w, 16)
        g.dma_start(out=a["kbr"][:, :], in_=p.k_b).then_inc(s_w, 16)
        nc.sync.dma_start(out=a["gam"][:, :], in_=p.gamma).then_inc(s_w2, 16)
        nc.sync.dma_start(out=a["bet"][:, :], in_=p.beta).then_inc(s_w2, 16)
        for e in ALLE:
            nc.engines[e].wait_ge(s_w, 9 * 16)
            nc.engines[e].wait_ge(s_w2, 4 * 16)
        s_ms = nc.semaphore("bc_ms").__enter__()
        nc.gpsimd.memset(a["state"][:, :], 0.0).then_inc(s_ms, 1)
        nc.gpsimd.memset(a["stateb"][:, :], 0.0).then_inc(s_ms, 1)
        for e in ALLE:
            nc.engines[e].wait_ge(s_ms, 2)
        p.barrier()

        emit_perf_loop(p, a, feed_only=True)
        p.barrier()
        emit_perf_loop(p, a, feed_only=False)
        p.barrier()


def emit_perf_loop(p, a, feed_only):
    nc = p.nc
    mark = pro_mark(p)
    tag = "f" if feed_only else "m"
    cL = Ctr(nc, f"p{tag}L", p.deltas)
    cS = Ctr(nc, f"p{tag}S", p.deltas)
    cV = Ctr(nc, f"p{tag}V", p.deltas)
    cT = Ctr(nc, f"p{tag}T", p.deltas)
    ctrs = (cL, cS, cV, cT)

    if feed_only:
        n = p.cfgv(CFG_N_FEED, ALLE, SEQ // ST)
        base = None
    else:
        n = p.cfgv(CFG_N_C, ALLE, SEQ // ST + 8)
        base = p.cfgv(CFG_B_C, ALLE, SEQ)

    x0bT_src = p.x0bT_d.rearrange("(dt q) n -> q dt n", q=128)
    loT_src = p.loT_d.rearrange("(dt q) n -> q dt n", q=128)
    spx_dst = p.sp_x1.rearrange("(dt q) n -> q dt n", q=128)
    x0bT_dst = a["x0bT"][:, :].rearrange("q (dt n) -> q dt n", n=ST)
    lo_dst = a["qhat"][:, :].rearrange("q (dt n) -> q dt n", n=ST)
    x0T_src3 = a["x0T"][:, :].rearrange("q (dt n) -> q dt n", n=ST)

    def load_x0bT(off_expr, stg_wait=None):
        if stg_wait is not None:
            nc.sync.wait_ge(p.stg_sem, stg_wait)
        cL.inc(nc.sync.dma_start(out=x0bT_dst, in_=x0bT_src[:, :, ds(off_expr, ST)]), 16)

    

    def load_loT(off_expr):
        cL.inc(nc.scalar.dma_start(out=lo_dst, in_=loT_src[:, :, ds(off_expr, ST)]), 16)

    p.load_x0bT, p.load_loT = load_x0bT, load_loT
    p.spx_dst, p.x0T_src3 = spx_dst, x0T_src3

    if feed_only:
        load_x0bT(0, stg_wait=2)
    else:
        nc.sync.wait_ge(p.stg_sem, p.stg_all)
        b0 = rcalc(p, E.SP, SEQ, add=(base,))
        p.c_base = b0
        load_x0bT(b0)
        load_loT(0)

    if DEBUG and not feed_only:
        p.sdbg_s = nc.semaphore("sdbgs").__enter__()
        p.sdbg_g = nc.semaphore("sdbgg").__enter__()
    from contextlib import nullcontext
    guard = nc.If(n > 0) if feed_only else nullcontext()
    with guard:
        with kfori(p, n) as i:
            for c in ctrs:
                c.begin_body()
            begin_snaps(p)
            cur = Cursors(p, i, ctrs)
            body_start_waits(cur, ctrs)
            emit_perf_body(p, a, i, feed_only, cL, cS, cV, cT, cur)
            for c in ctrs:
                c.end_body()
            cur.finish()
            end_snaps(p)
    drain_all(p, ctrs, n)
    pro_free(p, mark)
    if DEBUG and not feed_only:
        nc.sync.wait_ge(p.sdbg_s.h if hasattr(p.sdbg_s, "h") else p.sdbg_s, n * 32)
        nc.gpsimd.wait_ge(p.sdbg_g.h if hasattr(p.sdbg_g, "h") else p.sdbg_g, n * 32)


def emit_perf_body(p, a, i, feed_only, cL, cS, cV, cT, cur):
    nc = p.nc
    ps = p.ps
    rot = MMRot(p, [0, 1, 2, 3], cur)

    cur.wait(E.Activation, cL)
    cS.inc(nc.scalar.activation(a["sqb"][:, :], a["x0bT"][:, :], AF.Square))

    cur.wait(E.PE, cL)
    for kt in range(8):
        mm = nc.tensor.matmul(ps[6][0:1, :], p.ones_b[:, :],
                              a["x0bT"][:, kt * ST:(kt + 1) * ST],
                              start=(kt == 0), stop=(kt == 7))
    cT.inc(mm)
    cur.wait(E.PE, cS)
    for kt in range(8):
        mm = nc.tensor.matmul(ps[7][0:1, :], p.ones_b[:, :],
                              a["sqb"][:, kt * ST:(kt + 1) * ST],
                              start=(kt == 0), stop=(kt == 7))
    cT.inc(mm)

    cur.wait(E.DVE, cT)
    nc.vector.tensor_scalar_mul(a["r_mu"][:, :], ps[6][0:1, :], 1.0 / DM)
    nc.vector.tensor_scalar_mul(a["r_t1"][:, :], ps[7][0:1, :], 1.0 / DM)
    nc.vector.tensor_mul(a["r_t2"][:, :], a["r_mu"][:, :], a["r_mu"][:, :])
    cV.inc(nc.vector.tensor_sub(a["r_t1"][:, :], a["r_t1"][:, :], a["r_t2"][:, :]))
    cur.wait(E.Activation, cV)
    nc.scalar.activation(a["r_t2"][:, :], a["r_t1"][:, :], AF.Ln, bias=1e-5)
    cS.inc(nc.scalar.activation(a["r_rstd"][:, :], a["r_t2"][:, :], AF.Exp, scale=-0.5))

    cur.wait(E.PE, cS)
    cT.inc(nc.tensor.matmul(ps[4][:, :], p.ones_r32[:, :], a["r_mu"][:, :], start=True, stop=True))
    cT.inc(nc.tensor.matmul(ps[5][:, :], p.ones_r32[:, :], a["r_rstd"][:, :], start=True, stop=True))

    cur.wait(E.DVE, cT)
    if feed_only:
        for kt in range(8):
            nc.vector.tensor_sub(a["ylnb"][:, kt * ST:(kt + 1) * ST],
                                 a["x0bT"][:, kt * ST:(kt + 1) * ST], ps[4][:, :])
        for kt in range(8):
            v = nc.vector.tensor_mul(a["ylnb"][:, kt * ST:(kt + 1) * ST],
                                     a["ylnb"][:, kt * ST:(kt + 1) * ST], ps[5][:, :])
        cV.inc(v)
    else:
        for kt in range(8):
            nc.vector.tensor_add(a["x0T"][:, kt * ST:(kt + 1) * ST],
                                 a["x0bT"][:, kt * ST:(kt + 1) * ST],
                                 a["qhat"][:, kt * ST:(kt + 1) * ST])
        for kt in range(8):
            nc.vector.tensor_sub(a["yln"][:, kt * ST:(kt + 1) * ST],
                                 a["x0T"][:, kt * ST:(kt + 1) * ST], ps[4][:, :])
        for kt in range(8):
            nc.vector.tensor_mul(a["yln"][:, kt * ST:(kt + 1) * ST],
                                 a["yln"][:, kt * ST:(kt + 1) * ST], ps[5][:, :])
        cV.inc(nc.vector.tensor_copy(a["ylnb"][:, :], a["yln"][:, :]))

    def proj_block(Wname, bias_row, dst):
        cur.wait(E.PE, cV)
        for dt in range(8):
            bank, bi = rot.group()
            for kt in range(8):
                nc.tensor.matmul(bank[:, :],
                                 a[Wname][:, kt * DM + dt * 128: kt * DM + (dt + 1) * 128],
                                 a["ylnb"][:, kt * ST:(kt + 1) * ST],
                                 start=(kt == 0), stop=False)
            cT.inc(nc.tensor.matmul(bank[:, :], a[bias_row][0:1, dt * 128:(dt + 1) * 128],
                                    p.ones_r[0:1, :], start=False, stop=True))
            cur.wait(E.Activation, cT)
            cS.inc(nc.scalar.activation(dst[:, dt * ST:(dt + 1) * ST], bank[:, :],
                                        AF.Copy, scale=0.125))
            rot.done(bi, cS)

    proj_block("Wk", "kbr", a["khat"])
    if not feed_only:
        proj_block("Wq", "qbr", a["qhat"])

    cur.wait(E.PE, cV)
    for tt in range(4):
        for vc in range(2):
            bank, bi = rot.group()
            for kt in range(8):
                nc.tensor.matmul(bank[:, :],
                                 a["ylnb"][:, kt * ST + tt * 128: kt * ST + (tt + 1) * 128],
                                 a["Wv"][:, kt * DM + vc * 512: kt * DM + (vc + 1) * 512],
                                 start=(kt == 0), stop=False)
            cT.inc(nc.tensor.matmul(bank[:, :], p.ones_r[0:1, 0:128],
                                    a["vb_row"][0:1, vc * 512:(vc + 1) * 512],
                                    start=False, stop=True))
            cur.wait(E.Activation, cT)
            cS.inc(nc.scalar.copy(a["vnat"][:, tt * DM + vc * 512: tt * DM + (vc + 1) * 512],
                                  bank[:, :]))
            rot.done(bi, cS)

    for h in range(PH):
        dt, row = h // 2, (h % 2) * 64
        cur.wait(E.Activation, cV)
        cS.inc(nc.scalar.copy(a["ka"][0:64, :], a["khat"][row:row + 64, dt * ST:(dt + 1) * ST]))
        cS.inc(nc.scalar.activation(a["qsq"][0:64, :], a["ka"][0:64, :], AF.Square))
        cur.wait(E.PE, cS)
        cur.wait(E.PE, cV)
        cT.inc(nc.tensor.matmul(ps[6][0:1, :], p.ones_b[0:64, :], a["qsq"][0:64, :],
                                start=True, stop=True))
        cur.wait(E.Activation, cT)
        cS.inc(nc.scalar.activation(a["ka"][64:65, :], ps[6][0:1, :], AF.Copy,
                                    scale=0.5, bias=LN16))
        if not feed_only:
            cS.inc(nc.scalar.copy(a["qa"][0:64, :], a["qhat"][row:row + 64, dt * ST:(dt + 1) * ST]))
            cS.inc(nc.scalar.activation(a["qsq"][0:64, :], a["qa"][0:64, :], AF.Square))
            cur.wait(E.PE, cS)
            cT.inc(nc.tensor.matmul(ps[7][0:1, :], p.ones_b[0:64, :], a["qsq"][0:64, :],
                                    start=True, stop=True))
            cur.wait(E.Activation, cT)
            cS.inc(nc.scalar.activation(a["qa"][64:65, :], ps[7][0:1, :], AF.Copy,
                                        scale=0.5, bias=LN16))

        cur.wait(E.PE, cS)
        for tt in range(4):
            bank, bi = rot.group()
            cT.inc(nc.tensor.matmul(bank[:, 0:PM], a["ka"][0:65, tt * 128:(tt + 1) * 128],
                                    a["projT"][0:65, :], start=True, stop=True))
            cur.wait(E.Activation, cT)
            cS.inc(nc.scalar.activation(a["kpn"][:, tt * PM:(tt + 1) * PM], bank[:, 0:PM], AF.Exp))
            rot.done(bi, cS)
        if not feed_only:
            for mt in range(2):
                bank, bi = rot.group()
                cT.inc(nc.tensor.matmul(bank[:, :], a["projT"][0:65, mt * 128:(mt + 1) * 128],
                                        a["qa"][0:65, :], start=True, stop=True))
                cur.wait(E.Activation, cT)
                cS.inc(nc.scalar.activation(a["qpT"][:, mt * ST:(mt + 1) * ST], bank[:, :], AF.Exp))
                rot.done(bi, cS)
            for mt in range(2):
                bank, bi = rot.group()
                cT.inc(nc.tensor.matmul(bank[:, :], a["projT"][0:65, mt * 128:(mt + 1) * 128],
                                        a["ka"][0:65, :], start=True, stop=True))
                cur.wait(E.Activation, cT)
                cS.inc(nc.scalar.activation(a["kpT"][:, mt * ST:(mt + 1) * ST], bank[:, :], AF.Exp))
                rot.done(bi, cS)

            for jt in range(4):
                bank, bi = rot.group()
                cur.wait(E.PE, cS)
                for mt in range(2):
                    mm = nc.tensor.matmul(bank[:, :],
                                          a["kpT"][:, mt * ST + jt * 128: mt * ST + (jt + 1) * 128],
                                          a["qpT"][:, mt * ST:(mt + 1) * ST],
                                          start=(mt == 0), stop=(mt == 1))
                cT.inc(mm)
                cur.wait(E.DVE, cT)
                cV.inc(nc.vector.tensor_mask(a["Am"][:, jt * ST:(jt + 1) * ST], bank[:, :],
                                             a["alen_s"][:, jt:jt + 1], a["negi_s"][:, :], 0))
                rot.done(bi, cV)

            bank_n, bi_n = rot.group()
            cur.wait(E.PE, cV)
            for jt in range(4):
                nc.tensor.matmul(bank_n[0:64, :], a["vnat"][:, jt * DM + h * 64: jt * DM + (h + 1) * 64],
                                 a["Am"][:, jt * ST:(jt + 1) * ST], start=(jt == 0), stop=False)
            for mt in range(2):
                slot = (2 * h + mt) * 65
                nc.tensor.matmul(bank_n[0:64, :], a["stateb"][:, slot:slot + 64],
                                 a["qpT"][:, mt * ST:(mt + 1) * ST], start=False, stop=(mt == 1))
            for jt in range(4):
                nc.tensor.matmul(bank_n[64:65, :], p.ones_b[:, :],
                                 a["Am"][:, jt * ST:(jt + 1) * ST], start=(jt == 0), stop=False)
            for mt in range(2):
                slot = (2 * h + mt) * 65
                mm = nc.tensor.matmul(bank_n[64:65, :], a["stateb"][:, slot + 64:slot + 65],
                                      a["qpT"][:, mt * ST:(mt + 1) * ST], start=False, stop=(mt == 1))
            cT.inc(mm)

            cur.wait(E.DVE, cT)
            cV.inc(nc.vector.reciprocal(a["r_t2"][:, :], bank_n[64:65, :]))
            cur.wait(E.PE, cV)
            cT.inc(nc.tensor.matmul(ps[5][0:64, :], p.ones_r32[0:1, 0:64], a["r_t2"][:, :],
                                    start=True, stop=True))
            cur.wait(E.Activation, cT)
            cS.inc(nc.scalar.copy(a["dbc"][:, :], ps[5][0:64, :]))
            cur.wait(E.DVE, cS)
            cV.inc(nc.vector.tensor_mul(a["x0bT"][row:row + 64, dt * ST:(dt + 1) * ST],
                                        bank_n[0:64, :], a["dbc"][:, :]))
            rot.done(bi_n, cV)

        bank_u, bi_u = rot.group()
        cur.wait(E.PE, cS)
        first = True
        for c in range(4):
            for mt in range(2):
                nc.tensor.matmul(bank_u[:, mt * 65: mt * 65 + 64],
                                 a["kpn"][:, c * PM + mt * 128: c * PM + (mt + 1) * 128],
                                 a["vnat"][:, c * DM + h * 64: c * DM + (h + 1) * 64],
                                 start=first, stop=False)
                first = False
                mm = nc.tensor.matmul(bank_u[:, mt * 65 + 64: mt * 65 + 65],
                                      a["kpn"][:, c * PM + mt * 128: c * PM + (mt + 1) * 128],
                                      p.ones_b[:, :], start=False,
                                      stop=(c == 3 and mt == 1))
        cT.inc(mm)
        slot0 = 2 * h * 65
        cur.wait(E.DVE, cT)
        cV.inc(nc.vector.tensor_add(a["state"][:, slot0:slot0 + 130],
                                    a["state"][:, slot0:slot0 + 130], bank_u[:, 0:130]))
        cV.inc(nc.vector.tensor_copy(a["stateb"][:, slot0:slot0 + 130],
                                     a["state"][:, slot0:slot0 + 130]))
        rot.done(bi_u, cV)

    if feed_only:
        cur.wait(E.SP, cS)
        cur.wait(E.SP, cT)
        cur.wait(E.SP, cV)
        p.load_x0bT(rcalc(p, E.SP, SEQ, imm=ST, mul=(i, ST)), stg_wait=i + 3)
        return

    for dt in range(8):
        bank, bi = rot.group()
        cur.wait(E.PE, cV)
        for et in range(8):
            nc.tensor.matmul(bank[:, :], a["Wo"][:, et * DM + dt * 128: et * DM + (dt + 1) * 128],
                             a["x0bT"][:, et * ST:(et + 1) * ST], start=(et == 0), stop=False)
        cT.inc(nc.tensor.matmul(bank[:, :], a["ob_row"][0:1, dt * 128:(dt + 1) * 128],
                                p.ones_r[0:1, :], start=False, stop=True))
        cur.wait(E.DVE, cT)
        cV.inc(nc.vector.tensor_add(a["gb1"][:, dt * ST:(dt + 1) * ST], bank[:, :],
                                    a["yln"][:, dt * ST:(dt + 1) * ST]))
        rot.done(bi, cV)

    emit_ln_tail(p, a, i, cS, cV, cT, cur, src="gb1", scratch="yln", dst="gb1",
                 eps=1e-6, affine=True)
    emit_ln_tail(p, a, i, cS, cV, cT, cur, src="gb1", scratch="yln", dst=None,
                 eps=1e-5, affine=False)
    cur.wait(E.DVE, cT)
    for kt in range(8):
        v = nc.vector.tensor_add(a["x0T"][:, kt * ST:(kt + 1) * ST],
                                 a["x0T"][:, kt * ST:(kt + 1) * ST],
                                 a["yln"][:, kt * ST:(kt + 1) * ST])
    cV.inc(v)

    if DEBUG:
        dco = snapo(p, i * 4096, 11 * 4096)
        dcp = snapo(p, i * 4096, 11 * 4096, engines=(E.Pool,))
        cur.wait(E.SP, cV)
        cur.wait(E.SP, cS)
        nc.sync.dma_start(out=p.dbg1[:, ds(dco, 4096)], in_=a["yln"][:, :]).then_inc(p.sdbg_s, 16)
        nc.gpsimd.dma_start(out=p.dbg2[:, ds(dcp, 4096)], in_=a["qhat"][:, :]).then_inc(p.sdbg_g, 16)
        nc.gpsimd.dma_start(out=p.dbg3[:, ds(dcp, 4096)], in_=a["x0bT"][:, :]).then_inc(p.sdbg_g, 16)
        nc.sync.dma_start(out=p.dbg4[:, ds(dco, 4096)], in_=a["gb1"][:, :]).then_inc(p.sdbg_s, 16)

    cur.wait(E.SP, cV)
    cur.wait(E.SP, cS)
    cur.wait(E.SP, cT)
    cur.wait(E.Activation, cV)
    spo = rcalc(p, E.Activation, SPX1_COLS, imm=128, mul=(i, ST))
    cL.inc(nc.scalar.dma_start(out=p.spx_dst[:, :, ds(spo, ST)], in_=p.x0T_src3), 16)
    p.load_x0bT(rcalc(p, E.SP, SEQ, imm=ST, mul=(i, ST), add=(p.c_base,)))
    p.load_loT(rcalc(p, E.Activation, XC_TOK - ST, imm=ST, mul=(i, ST)))


def emit_ln_tail(p, a, i, cS, cV, cT, cur, src, scratch, dst, eps, affine):
    nc = p.nc
    ps = p.ps
    cur.wait(E.Activation, cV)
    cS.inc(nc.scalar.copy(a["sqb"][:, :], a[src][:, :]))
    cur.wait(E.PE, cS)
    cur.wait(E.PE, cV)
    for kt in range(8):
        mm = nc.tensor.matmul(ps[6][0:1, :], p.ones_b[:, :],
                              a["sqb"][:, kt * ST:(kt + 1) * ST],
                              start=(kt == 0), stop=(kt == 7))
    cT.inc(mm)
    cur.wait(E.Activation, cT)
    cS.inc(nc.scalar.activation(a["sqb"][:, :], a["sqb"][:, :], AF.Square))
    cur.wait(E.PE, cS)
    for kt in range(8):
        mm = nc.tensor.matmul(ps[7][0:1, :], p.ones_b[:, :],
                              a["sqb"][:, kt * ST:(kt + 1) * ST],
                              start=(kt == 0), stop=(kt == 7))
    cT.inc(mm)
    cur.wait(E.DVE, cT)
    nc.vector.tensor_scalar_mul(a["r_mu"][:, :], ps[6][0:1, :], 1.0 / DM)
    nc.vector.tensor_scalar_mul(a["r_t1"][:, :], ps[7][0:1, :], 1.0 / DM)
    nc.vector.tensor_mul(a["r_t2"][:, :], a["r_mu"][:, :], a["r_mu"][:, :])
    cV.inc(nc.vector.tensor_sub(a["r_t1"][:, :], a["r_t1"][:, :], a["r_t2"][:, :]))
    cur.wait(E.Activation, cV)
    nc.scalar.activation(a["r_t2"][:, :], a["r_t1"][:, :], AF.Ln, bias=eps)
    cS.inc(nc.scalar.activation(a["r_rstd"][:, :], a["r_t2"][:, :], AF.Exp, scale=-0.5))
    cur.wait(E.PE, cS)
    cur.wait(E.PE, cV)
    cT.inc(nc.tensor.matmul(ps[4][:, :], p.ones_r32[:, :], a["r_mu"][:, :], start=True, stop=True))
    cT.inc(nc.tensor.matmul(ps[5][:, :], p.ones_r32[:, :], a["r_rstd"][:, :], start=True, stop=True))
    cur.wait(E.DVE, cT)
    for kt in range(8):
        nc.vector.tensor_sub(a[scratch][:, kt * ST:(kt + 1) * ST],
                             a[src][:, kt * ST:(kt + 1) * ST], ps[4][:, :])
    for kt in range(8):
        v = nc.vector.tensor_mul(a[scratch][:, kt * ST:(kt + 1) * ST],
                                 a[scratch][:, kt * ST:(kt + 1) * ST], ps[5][:, :])
    cV.inc(v)
    if affine:
        for dt in range(8):
            v = nc.vector.tensor_scalar(a[dst][:, dt * ST:(dt + 1) * ST],
                                        a[scratch][:, dt * ST:(dt + 1) * ST],
                                        a["gam"][:, dt:dt + 1], a["bet"][:, dt:dt + 1],
                                        ALU.mult, ALU.add)
        cV.inc(v)



FA_SPAN = 8 * (STD + 1)


def alloc_fa(p, stk, with_ln, layer):
    nc = p.nc
    a = {}

    def sb(name, shape, dt):
        a[name] = stk.enter_context(nc.sbuf_tensor(f"fa{layer}_" + name, shape, dt))
        return a[name]

    sb("Wqkv", [128, 8 * 1536], BF16)
    sb("Wo", [128, 4 * DM], BF16)
    sb("rotpb", [128, 128], BF16)
    sb("identb", [128, 128], BF16)
    sb("xsp", [128, FA_SPAN], F32)
    sb("xb16", [128, FA_SPAN], BF16)
    sb("sqb", [128, FA_SPAN], BF16)
    if with_ln:
        sb("yln", [128, FA_SPAN], F32)
    sb("hb", [128, 8 * STD], BF16)
    sb("r_mu", [1, STD + 1], F32)
    sb("r_t1", [1, STD + 1], F32)
    sb("r_t2", [1, STD + 1], F32)
    sb("r_rstd", [1, STD + 1], F32)
    sb("sqt", [128, STD], BF16)
    sb("rr", [2, 2 * STD], F32)
    sb("qsb", [128, 4 * STD], BF16)
    sb("tr2", [128, STD], F32)
    sb("tr3", [128, STD], F32)
    sb("tr4", [128, STD], F32)
    sb("rsqs", [128, STD], F32)
    sb("qfb", [128, 4 * STD], BF16)
    sb("kwin", [128, 4 * 768], BF16)
    sb("vwin", [128, 6 * FIN], BF16)
    sb("rt4", [128, 4 * STD], F32)
    sb("simm", [128, 640], F32)
    sb("simx", [128, 640], F32)
    sb("attb", [128, 640], BF16)
    sb("attT", [128, 640], BF16)
    sb("mxc", [128, 1], F32)
    sb("nmx", [128, 1], F32)
    sb("denc", [128, 1], F32)
    sb("rden", [128, 1], F32)
    sb("qmc", [128, 2], F32)
    sb("e1c", [128, 1], F32)
    sb("z0c", [128, 1], F32)
    sb("rp1", [128, 1], F32)
    sb("sclq", [128, 1], F32)
    sb("sclk", [128, 1], F32)
    sb("bd2", [128, 2], BF16)
    sb("sel2", [2, 128], F32)
    sb("oTb", [128, 2 * FIN], BF16)
    sb("xout", [128, 8 * STD], F32)
    return a


def phase_FA(p, layer):
    nc = p.nc
    with_ln = (layer == 0)
    with ExitStack() as stk:
        a = alloc_fa(p, stk, with_ln, layer)
        s_w = nc.semaphore(f"fa{layer}_w").__enter__()
        s_w2 = nc.semaphore(f"fa{layer}_w2").__enter__()
        s_ms = nc.semaphore(f"fa{layer}_ms").__enter__()
        g = nc.gpsimd
        wq_src = p.wqkv[layer:layer + 1, :, :].rearrange("u (kt q) o -> q (u kt) o", q=128)
        g.dma_start(out=a["Wqkv"][:, :].rearrange("q (kt o) -> q kt o", o=1536),
                    in_=wq_src).then_inc(s_w, 16)
        wo_src = p.wo[layer:layer + 1, :, :].rearrange("u (kt q) o -> q (u kt) o", q=128)
        g.dma_start(out=a["Wo"][:, :].rearrange("q (kt o) -> q kt o", o=DM),
                    in_=wo_src).then_inc(s_w, 16)
        g.dma_start(out=a["rotpb"][:, :], in_=p.rotp).then_inc(s_w, 16)
        g.dma_start(out=a["identb"][:, :], in_=p.ident).then_inc(s_w, 16)
        nc.sync.dma_start(out=a["sclq"][:, :],
                          in_=p.ffn_scl[:, 2 * layer:2 * layer + 1], allow_slow_non_contiguous=True).then_inc(s_w2, 16)
        nc.sync.dma_start(out=a["sclk"][:, :],
                          in_=p.ffn_scl[:, 2 * layer + 1:2 * layer + 2], allow_slow_non_contiguous=True).then_inc(s_w2, 16)
        nc.sync.dma_start(out=a["rp1"][:, :], in_=p.rowp1, allow_slow_non_contiguous=True).then_inc(s_w2, 16)
        g.memset(a["bd2"][:, :], 0.0).then_inc(s_ms, 1)
        nc.sync.dma_start(out=a["sel2"][:, :], in_=p.sel2m).then_inc(s_w2, 16)
        g.memset(a["kwin"][:, :], 0.0).then_inc(s_ms, 1)
        g.memset(a["vwin"][:, :], 0.0).then_inc(s_ms, 1)
        g.memset(a["e1c"][:, :], 512.0).then_inc(s_ms, 1)
        g.memset(a["z0c"][:, :], 0.0).then_inc(s_ms, 1)
        g.wait_ge(s_ms, 5)
        g.memset(a["bd2"][0:64, 0:1], 1.0).then_inc(s_ms, 1)
        g.memset(a["bd2"][64:128, 1:2], 1.0).then_inc(s_ms, 1)
        for e in ALLE:
            nc.engines[e].wait_ge(s_w, 4 * 16)
            nc.engines[e].wait_ge(s_w2, 4 * 16)
            nc.engines[e].wait_ge(s_ms, 7)

        emit_fa_loop(p, a, layer, warm=True)
        p.barrier()
        emit_fa_loop(p, a, layer, warm=False)
        p.barrier()



def _spdbg(p, label):
    import os, itertools
    if not os.environ.get("REGDBG"):
        return
    if not hasattr(_spdbg, "c"):
        _spdbg.c = itertools.count()
    nc = p.nc
    got = []
    while True:
        try:
            got.append(nc.sync.alloc_register(f"dbg_{next(_spdbg.c)}"))
        except ValueError:
            break
    for r in got:
        nc.sync.free_register(r)
    print(f"[regdbg] {label}: SP free {len(got)}", flush=True)

def emit_fa_loop(p, a, layer, warm):
    nc = p.nc
    mark = pro_mark(p)
    tag = f"fa{layer}{'w' if warm else 'm'}"
    cL = Ctr(nc, tag + "L", p.deltas)
    cA = Ctr(nc, tag + "A", p.deltas)
    cG = Ctr(nc, tag + "G", p.deltas)
    cP = Ctr(nc, tag + "P", p.deltas)
    cS = Ctr(nc, tag + "S", p.deltas)
    cV = Ctr(nc, tag + "V", p.deltas)
    cT = Ctr(nc, tag + "T", p.deltas)
    ctrs = (cL, cA, cG, cP, cS, cV, cT)

    if layer == 0:
        n = p.cfgv(CFG_N_D1W if warm else CFG_N_D1, ALLE, 64)
        base = p.cfgv(CFG_B_D1W if warm else CFG_B_D1, [E.SP, E.Activation], SEQ)
        src_sp, src_base_cfg = p.sp_x1, CFG_B_C
        dst_sp = p.sp_bm
        qm_in, qm_max = p.qm_d1, 32
    else:
        n = p.cfgv(CFG_N_E1W if warm else CFG_N_E1, ALLE, 64)
        base = p.cfgv(CFG_B_E1W if warm else CFG_B_E1, [E.SP, E.Activation], SEQ)
        src_sp, src_base_cfg = p.sp_l0, CFG_B_D2
        dst_sp = p.sp_l1
        qm_in, qm_max = p.qm_e1, 24
    src_base = p.cfgv(src_base_cfg, [E.SP], SEQ)
    _spdbg(p, tag + " after cfgv")

    xsp3 = a["xsp"][:, :].rearrange("q (dt n) -> q dt n", n=STD + 1)
    src3 = src_sp.rearrange("(dt q) n -> q dt n", q=128)
    dst3 = dst_sp.rearrange("(dt q) n -> q dt n", q=128)
    xout3 = a["xout"][:, :].rearrange("q (dt n) -> q dt n", n=STD)
    sp_cols = {id(p.sp_x1): SPX1_COLS + ST, id(p.sp_bm): 4096 + STD,
               id(p.sp_l0): 4096 + STD, id(p.sp_l1): 3072 + STD}
    src_cols = sp_cols[id(src_sp)]

    xmax = src_cols - (STD + 1)
    smax = sp_cols[id(dst_sp)] - STD
    p._fa_smax = smax

    def load_iter(i_rv):
        if i_rv is None:
            offX = rcalc(p, E.SP, xmax, imm=127, add=(base,), sub=(src_base,))
            offR = rcalc(p, E.Activation, SEQ, add=(base,))
            offQ = 0
        else:
            offX = rcalc(p, E.SP, xmax, imm=127 + STD, mul=(i_rv, STD),
                         add=(base,), sub=(src_base,))
            offR = rcalc(p, E.Activation, SEQ, imm=STD, mul=(i_rv, STD), add=(base,))
            offQ = rcalc(p, E.SP, qm_max - 2, imm=2, mul=(i_rv, 2))
        cL.inc(nc.sync.dma_start(out=xsp3, in_=src3[:, :, ds(offX, STD + 1)],
                                  bounds_check="skip_entire_dma"), 16)
        rt43 = a["rt4"][:, :].rearrange("q (f n) -> q f n", n=STD)
        rall3 = p.rot_all.rearrange("q (f n) -> q f n", f=4)
        cA.inc(nc.scalar.dma_start(out=rt43, in_=rall3[:, :, ds(offR, STD)]), 16)
        if not warm:
            cL.inc(nc.sync.dma_start(out=a["qmc"][:, :], in_=qm_in[:, ds(offQ, 2)]), 16)

    from contextlib import nullcontext
    load_iter(None)
    _spdbg(p, tag + " after prologue")
    guard = nc.If(n > 0) if warm else nullcontext()
    with guard:
        with kfori(p, n) as i:
            _spdbg(p, tag + " in fori")
            for c in ctrs:
                c.begin_body()
            begin_snaps(p)
            cur = Cursors(p, i, ctrs)
            body_start_waits(cur, ctrs, engines=ALLE)
            _spdbg(p, tag + " after cursors")
            emit_fa_body(p, a, layer, warm, i, cL, cA, cG, cP, cS, cV, cT, cur,
                         load_iter, dst3)
            for c in ctrs:
                c.end_body()
            cur.finish()
            end_snaps(p)
            _spdbg(p, tag + " after body frees")
    drain_all(p, ctrs, n)
    pro_free(p, mark)
    _spdbg(p, tag + " after loop")


def emit_fa_body(p, a, layer, warm, i, cL, cA, cG, cP, cS, cV, cT, cur,
                 load_iter, dst3):
    nc = p.nc
    ps = p.ps
    ps_t = p.ps_t
    with_ln = (layer == 0)

    vw = a["vwin"]
    for mt in range(4):
        kw = a["kwin"][:, mt * 768:(mt + 1) * 768]
        cG.inc(nc.gpsimd.dma_start(out=kw[:, 0:256], in_=kw[:, 256:512]), 16)
    cG.inc(nc.gpsimd.dma_start(out=vw[:, 0:1024], in_=vw[:, 1024:2048]), 16)
    cur.wait(E.Pool, cG)
    for mt in range(4):
        kw = a["kwin"][:, mt * 768:(mt + 1) * 768]
        cG.inc(nc.gpsimd.dma_start(out=kw[:, 256:512], in_=kw[:, 512:768]), 16)
    cG.inc(nc.gpsimd.dma_start(out=vw[:, 1024:2048], in_=vw[:, 2048:3072]), 16)

    cur.wait(E.DVE, cL)
    cV.inc(nc.vector.tensor_copy(a["xb16"][:, :], a["xsp"][:, :]))
    if with_ln:
        cur.wait(E.Activation, cV)
        cS.inc(nc.scalar.activation(a["sqb"][:, :], a["xb16"][:, :], AF.Square))
        cur.wait(E.PE, cV)
        for dt in range(8):
            mm = nc.tensor.matmul(ps[6][0:1, 0:STD + 1], p.ones_b[:, :],
                                  a["xb16"][:, dt * 257:(dt + 1) * 257],
                                  start=(dt == 0), stop=(dt == 7))
        cT.inc(mm)
        cur.wait(E.PE, cS)
        for dt in range(8):
            mm = nc.tensor.matmul(ps[7][0:1, 0:STD + 1], p.ones_b[:, :],
                                  a["sqb"][:, dt * 257:(dt + 1) * 257],
                                  start=(dt == 0), stop=(dt == 7))
        cT.inc(mm)
        cur.wait(E.DVE, cT)
        nc.vector.tensor_scalar_mul(a["r_mu"][:, :], ps[6][0:1, 0:STD + 1], 1.0 / DM)
        nc.vector.tensor_scalar_mul(a["r_t1"][:, :], ps[7][0:1, 0:STD + 1], 1.0 / DM)
        nc.vector.tensor_mul(a["r_t2"][:, :], a["r_mu"][:, :], a["r_mu"][:, :])
        cV.inc(nc.vector.tensor_sub(a["r_t1"][:, :], a["r_t1"][:, :], a["r_t2"][:, :]))
        cur.wait(E.Activation, cV)
        nc.scalar.activation(a["r_t2"][:, :], a["r_t1"][:, :], AF.Ln, bias=1e-5)
        cS.inc(nc.scalar.activation(a["r_rstd"][:, :], a["r_t2"][:, :], AF.Exp, scale=-0.5))
        cur.wait(E.PE, cS)
        cT.inc(nc.tensor.matmul(ps[4][:, 0:STD + 1], p.ones_r32[:, :], a["r_mu"][:, :],
                                start=True, stop=True))
        cT.inc(nc.tensor.matmul(ps[5][:, 0:STD + 1], p.ones_r32[:, :], a["r_rstd"][:, :],
                                start=True, stop=True))
        cur.wait(E.DVE, cT)
        for dt in range(8):
            nc.vector.tensor_sub(a["yln"][:, dt * 257:(dt + 1) * 257],
                                 a["xsp"][:, dt * 257:(dt + 1) * 257],
                                 ps[4][:, 0:STD + 1])
        for dt in range(8):
            v = nc.vector.tensor_mul(a["yln"][:, dt * 257:(dt + 1) * 257],
                                     a["yln"][:, dt * 257:(dt + 1) * 257],
                                     ps[5][:, 0:STD + 1])
        cV.inc(v)
        cur.wait(E.Pool, cV)
        for dt in range(8):
            sl = 1 if dt < 4 else 0
            pp = nc.gpsimd.tensor_copy(a["hb"][:, dt * STD:(dt + 1) * STD],
                                       a["yln"][:, dt * 257 + sl: dt * 257 + sl + STD])
        cP.inc(pp)
    else:
        cur.wait(E.Pool, cV)
        for dt in range(8):
            sl = 1 if dt < 4 else 0
            pp = nc.gpsimd.tensor_copy(a["hb"][:, dt * STD:(dt + 1) * STD],
                                       a["xb16"][:, dt * 257 + sl: dt * 257 + sl + STD])
        cP.inc(pp)

    def qk_tiles(kind):
        is_q = (kind == "q")
        wbase = 0 if is_q else FIN
        scl = a["sclq"] if is_q else a["sclk"]
        sfac = 8.0 if is_q else 1.0
        rc = a["rt4"][:, 0 * STD:1 * STD] if is_q else a["rt4"][:, 2 * STD:3 * STD]
        rs = a["rt4"][:, 1 * STD:2 * STD] if is_q else a["rt4"][:, 3 * STD:4 * STD]
        for bb in range(2):
            mts = [2 * bb, 2 * bb + 1]
            for j, mt in enumerate(mts):
                bq = ps[j]
                cur.wait(E.PE, cP)
                cur.wait(E.PE, cS)
                cur.wait(E.PE, cV)
                for kt in range(8):
                    mm = nc.tensor.matmul(bq[:, 0:STD],
                                          a["Wqkv"][:, kt * 1536 + wbase + mt * 128:
                                                    kt * 1536 + wbase + (mt + 1) * 128],
                                          a["hb"][:, kt * STD:(kt + 1) * STD],
                                          start=(kt == 0), stop=(kt == 7))
                cT.inc(mm)
                cur.wait(E.Activation, cT)
                cS.inc(nc.scalar.activation(a["sqt"][:, :], bq[:, 0:STD], AF.Square))
                cur.wait(E.PE, cS)
                cT.inc(nc.tensor.matmul(ps_t[0:2, 2 * 512 + j * STD: 2 * 512 + (j + 1) * STD],
                                        a["bd2"][:, :], a["sqt"][:, :],
                                        start=True, stop=True))
            cur.wait(E.DVE, cT)
            nc.vector.tensor_scalar_max(a["rr"][:, 0:2 * STD],
                                        ps_t[0:2, 2 * 512: 2 * 512 + 2 * STD], 1e-24)
            cV.inc(nc.vector.reciprocal(a["rr"][:, 0:2 * STD], a["rr"][:, 0:2 * STD]))
            cur.wait(E.Activation, cV)
            cS.inc(nc.scalar.activation(a["rr"][:, 0:2 * STD], a["rr"][:, 0:2 * STD],
                                        AF.Sqrt))
            for j, mt in enumerate(mts):
                bq = ps[j]
                cur.wait(E.PE, cS)
                cur.wait(E.PE, cV)
                cT.inc(nc.tensor.matmul(ps[4 + j][:, 0:STD], a["sel2"][:, :],
                                        a["rr"][:, j * STD:(j + 1) * STD],
                                        start=True, stop=True))
                cur.wait(E.Activation, cT)
                cS.inc(nc.scalar.copy(a["rsqs"][:, :], ps[4 + j][:, 0:STD]))
                cur.wait(E.DVE, cS)
                nc.vector.tensor_mul(a["tr4"][:, :], bq[:, 0:STD], a["rsqs"][:, :])
                cV.inc(nc.vector.tensor_scalar(a["qsb"][:, mt * STD:(mt + 1) * STD],
                                               a["tr4"][:, :], scl[:, 0:1], sfac,
                                               ALU.mult, ALU.mult))
                cur.wait(E.PE, cV)
                cT.inc(nc.tensor.matmul(ps[6 + j][:, 0:STD], a["rotpb"][:, :],
                                        a["qsb"][:, mt * STD:(mt + 1) * STD],
                                        start=True, stop=True))
                cur.wait(E.DVE, cT)
                cur.wait(E.DVE, cP)
                cV.inc(nc.vector.tensor_mul(a["tr2"][:, :], ps[6 + j][:, 0:STD],
                                            rs))
                cur.wait(E.Pool, cV)
                cur.wait(E.Pool, cG)
                cP.inc(nc.gpsimd.tensor_mul(a["tr3"][:, :],
                                            a["qsb"][:, mt * STD:(mt + 1) * STD],
                                            rc))
                cur.wait(E.Pool, cP)
                if is_q:
                    dst = a["qfb"][:, mt * STD:(mt + 1) * STD]
                else:
                    dst = a["kwin"][:, mt * 768 + 512: mt * 768 + 768]
                cP.inc(nc.gpsimd.tensor_add(dst, a["tr3"][:, :], a["tr2"][:, :]))

    qk_tiles("k")
    if not warm:
        qk_tiles("q")

    for s in range(2):
        bv = ps[0]
        cur.wait(E.PE, cS)
        cur.wait(E.PE, cV)
        cur.wait(E.PE, cP)
        for kt in range(8):
            mm = nc.tensor.matmul(bv[:, :],
                                  a["hb"][:, kt * STD + s * 128: kt * STD + (s + 1) * 128],
                                  a["Wqkv"][:, kt * 1536 + 1024: kt * 1536 + 1536],
                                  start=(kt == 0), stop=(kt == 7))
        cT.inc(mm)
        cur.wait(E.Activation, cT)
        cur.wait(E.Activation, cG)
        cS.inc(nc.scalar.copy(a["vwin"][:, (4 + s) * FIN:(5 + s) * FIN], bv[:, :]))

    if warm:
        cur.wait(E.SP, cV)
        cur.wait(E.SP, cS)
        cur.wait(E.SP, cT)
        cur.wait(E.SP, cP)
        cur.wait(E.Activation, cP)
        load_iter(i)
        return

    for s in range(2):
        oT = ps[6 + s]
        for h in range(8):
            mt, hrow = h // 2, (h % 2) * 64
            simv = ps_t[:, (h % 2) * 1024: (h % 2) * 1024 + 640]
            sim1 = ps_t[:, (h % 2) * 1024: (h % 2) * 1024 + 512]
            sim2 = ps_t[:, (h % 2) * 1024 + 512: (h % 2) * 1024 + 640]
            lhs_q = a["qfb"][hrow:hrow + 64, mt * STD + s * 128: mt * STD + (s + 1) * 128]
            kw = a["kwin"][hrow:hrow + 64, mt * 768 + s * 128: mt * 768 + s * 128 + 640]
            cur.wait(E.PE, cP)
            cur.wait(E.PE, cG)
            cur.wait(E.PE, cV)
            cur.wait(E.PE, cS)
            cT.inc(nc.tensor.matmul(sim1, lhs_q, kw[:, 0:512], start=True, stop=True))
            cT.inc(nc.tensor.matmul(sim2, lhs_q, kw[:, 512:640], start=True, stop=True))
            cur.wait(E.DVE, cT)
            nc.vector.tensor_mask_reduce(a["simm"][:, 0:512], sim1,
                                         a["qmc"][:, s:s + 1], a["e1c"][:, 0:1],
                                         1.0, -3.0e38, ALU.max, accum_out=a["mxc"][:, :])
            cV.inc(nc.vector.tensor_mask_reduce(a["simm"][:, 512:640], sim2,
                                                a["z0c"][:, 0:1], a["rp1"][:, 0:1],
                                                1.0, a["mxc"][:, 0:1], ALU.max,
                                                accum_out=a["mxc"][:, :]))
            cV.inc(nc.vector.tensor_scalar(a["nmx"][:, :], a["mxc"][:, :], -1.0, 80.0,
                                           ALU.mult, ALU.min))
            cur.wait(E.Activation, cV)
            cS.inc(nc.scalar.activation(a["simx"][:, :], a["simm"][:, :], AF.Exp,
                                        bias=a["nmx"][:, 0:1], accum_out=a["denc"][:, :]))
            cur.wait(E.DVE, cS)
            cV.inc(nc.vector.reciprocal(a["rden"][:, :], a["denc"][:, :]))
            cur.wait(E.Pool, cV)
            cur.wait(E.Pool, cT)
            cP.inc(nc.gpsimd.tensor_scalar(a["attb"][:, :], a["simx"][:, :],
                                           a["rden"][:, 0:1], None, ALU.mult))
            cur.wait(E.PE, cP)
            cur.wait(E.PE, cV)
            for j in range(5):
                tt = nc.tensor.transpose(
                    p.ps_bt[:, 4 * 1024 + j * 128: 4 * 1024 + (j + 1) * 128],
                    a["attb"][:, j * 128:(j + 1) * 128], a["identb"][:, :])
            cT.inc(tt)
            cur.wait(E.DVE, cT)
            cV.inc(nc.vector.tensor_copy(a["attT"][:, :],
                                         p.ps_bt[:, 4 * 1024: 4 * 1024 + 640]))
            cur.wait(E.PE, cV)
            for j in range(5):
                mm = nc.tensor.matmul(
                    oT[hrow:hrow + 64, mt * 128:(mt + 1) * 128],
                    a["vwin"][:, (s + j) * FIN + h * 64: (s + j) * FIN + (h + 1) * 64],
                    a["attT"][:, j * 128:(j + 1) * 128],
                    start=(j == 0), stop=(j == 4))
            cT.inc(mm)
        cur.wait(E.DVE, cT)
        oTb3 = a["oTb"][:, :].rearrange("q (kt two n) -> q kt two n", two=2, n=128)
        cV.inc(nc.vector.tensor_copy(oTb3[:, :, s, :],
                                     oT[:, :].rearrange("q (kt n) -> q kt n", n=128)))

    for dt in range(8):
        bo = ps[dt % 4]
        cur.wait(E.PE, cV)
        for kt in range(4):
            mm = nc.tensor.matmul(bo[:, 0:STD],
                                  a["Wo"][:, kt * DM + dt * 128: kt * DM + (dt + 1) * 128],
                                  a["oTb"][:, kt * STD:(kt + 1) * STD],
                                  start=(kt == 0), stop=(kt == 3))
        cT.inc(mm)
        cur.wait(E.DVE, cT)
        resid = a["yln"] if with_ln else a["xsp"]
        cV.inc(nc.vector.tensor_add(a["xout"][:, dt * STD:(dt + 1) * STD],
                                    bo[:, 0:STD],
                                    resid[:, dt * 257 + 1:dt * 257 + 1 + STD]))

    cur.wait(E.Activation, cV)
    cur.wait(E.Activation, cP)
    spo = rcalc(p, E.Activation, p._fa_smax, imm=128, mul=(i, STD))
    cA.inc(nc.scalar.dma_start(out=dst3[:, :, ds(spo, STD)], in_=xout3_of(p, a)), 16)
    cur.wait(E.SP, cV)
    cur.wait(E.SP, cS)
    cur.wait(E.SP, cT)
    cur.wait(E.SP, cP)
    load_iter(i)


def xout3_of(p, a):
    return a["xout"][:, :].rearrange("q (dt n) -> q dt n", n=STD)



W1_TILES = build_w1_perm()[1]


def alloc_fm(p, stk, final, layer):
    nc = p.nc
    a = {}

    def sb(name, shape, dt):
        a[name] = stk.enter_context(nc.sbuf_tensor(f"fm{layer}_" + name, shape, dt))
        return a[name]

    sb("W1", [128, 8 * 2 * HID], BF16)
    sb("W2", [128, 22 * DM], BF16)
    sb("xsp", [128, FA_SPAN], F32)
    sb("xb16", [128, FA_SPAN], BF16)
    sb("sqb", [128, FA_SPAN], BF16)
    sb("zb", [128, 8 * STD], BF16)
    sb("r_mu", [1, STD], F32)
    sb("r_t1", [1, STD], F32)
    sb("r_t2", [1, STD], F32)
    sb("r_rstd", [1, STD], F32)
    sb("gam", [128, 8], F32)
    sb("bet", [128, 8], F32)
    sb("b1a", [128, 22], F32)
    sb("b1g", [128, 22], F32)
    sb("b2c", [128, 8], F32)
    sb("gel", [128, STD], F32)
    sb("glub", [128, 22 * STD], BF16)
    sb("xout", [128, 8 * STD], F32)
    if final:
        sb("x1sp", [128, 8 * STD], F32)
        sb("hi", [128, 8 * STD], BF16)
        sb("lo", [128, 8 * STD], BF16)
    return a


def phase_FM(p, layer):
    nc = p.nc
    final = (layer == 1)
    with ExitStack() as stk:
        a = alloc_fm(p, stk, final, layer)
        s_w = nc.semaphore(f"fm{layer}_w").__enter__()
        s_w2 = nc.semaphore(f"fm{layer}_w2").__enter__()
        g = nc.gpsimd
        w1_src = p.w1[layer:layer + 1, :, :].rearrange("u (kt q) o -> q (u kt) o", q=128)
        g.dma_start(out=a["W1"][:, :].rearrange("q (kt o) -> q kt o", o=2 * HID),
                    in_=w1_src).then_inc(s_w, 16)
        w2_src = p.w2[layer:layer + 1, :, :].rearrange("u (kt q) o -> q (u kt) o", q=128)
        g.dma_start(out=a["W2"][:, :].rearrange("q (kt o) -> q kt o", o=DM),
                    in_=w2_src).then_inc(s_w, 16)
        nc.sync.dma_start(out=a["gam"][:, :],
                          in_=p.ffn_ln[:, layer * 16:layer * 16 + 8]).then_inc(s_w2, 16)
        nc.sync.dma_start(out=a["bet"][:, :],
                          in_=p.ffn_ln[:, layer * 16 + 8:layer * 16 + 16]).then_inc(s_w2, 16)
        nc.sync.dma_start(out=a["b1a"][:, :],
                          in_=p.ffn_b1c[:, layer * 44:layer * 44 + 22]).then_inc(s_w2, 16)
        nc.sync.dma_start(out=a["b1g"][:, :],
                          in_=p.ffn_b1c[:, layer * 44 + 22:layer * 44 + 44]).then_inc(s_w2, 16)
        nc.sync.dma_start(out=a["b2c"][:, :],
                          in_=p.ffn_b2c[:, layer * 8:layer * 8 + 8]).then_inc(s_w2, 16)
        for e in ALLE:
            nc.engines[e].wait_ge(s_w, 2 * 16)
            nc.engines[e].wait_ge(s_w2, 5 * 16)
        emit_fm_loop(p, a, layer)
        p.barrier()


def emit_fm_loop(p, a, layer):
    nc = p.nc
    mark = pro_mark(p)
    final = (layer == 1)
    tag = f"fm{layer}"
    cL = Ctr(nc, tag + "L", p.deltas)
    cA = Ctr(nc, tag + "A", p.deltas)
    cS = Ctr(nc, tag + "S", p.deltas)
    cV = Ctr(nc, tag + "V", p.deltas)
    cT = Ctr(nc, tag + "T", p.deltas)
    cP = Ctr(nc, tag + "P", p.deltas)
    ctrs = (cL, cA, cS, cV, cT, cP)

    if layer == 0:
        n = p.cfgv(CFG_N_D2, ALLE, 64)
        base = p.cfgv(CFG_B_D2, [E.SP], SEQ)
        src_sp, src_base_cfg = p.sp_bm, CFG_B_D1
        src_cols = 4096 + STD
    else:
        n = p.cfgv(CFG_N_E2, ALLE, 64)
        base = p.cfgv(CFG_S, [E.SP], SEQ)
        src_sp, src_base_cfg = p.sp_l1, CFG_B_E1
        src_cols = 3072 + STD
    src_base = p.cfgv(src_base_cfg, [E.SP], SEQ)
    xsp3 = a["xsp"][:, :].rearrange("q (dt n) -> q dt n", n=STD + 1)
    src3 = src_sp.rearrange("(dt q) n -> q dt n", q=128)
    if layer == 0:
        dst3 = p.sp_l0.rearrange("(dt q) n -> q dt n", q=128)
    if final:
        x1b_cfg = p.cfgv(CFG_B_C, [E.SP], SEQ)
        x1sp3 = a["x1sp"][:, :].rearrange("q (dt n) -> q dt n", n=STD)
        x1src3 = p.sp_x1.rearrange("(dt q) n -> q dt n", q=128)
        oh3 = p.sp_oh.rearrange("(dt q) n -> q dt n", q=128)
        ol3 = p.sp_ol.rearrange("(dt q) n -> q dt n", q=128)

    xmax = src_cols - (STD + 1)

    def load_iter(i_rv):
        if i_rv is None:
            offX = rcalc(p, E.SP, xmax, imm=127, add=(base,), sub=(src_base,))
        else:
            offX = rcalc(p, E.SP, xmax, imm=127 + STD, mul=(i_rv, STD),
                         add=(base,), sub=(src_base,))
        cL.inc(nc.sync.dma_start(out=xsp3, in_=src3[:, :, ds(offX, STD + 1)],
                                  bounds_check="skip_entire_dma"), 16)
        if final:
            if i_rv is None:
                off1 = rcalc(p, E.SP, SPX1_COLS, imm=128, add=(base,), sub=(x1b_cfg,))
            else:
                off1 = rcalc(p, E.SP, SPX1_COLS, imm=128 + STD, mul=(i_rv, STD),
                             add=(base,), sub=(x1b_cfg,))
            cL.inc(nc.sync.dma_start(out=x1sp3, in_=x1src3[:, :, ds(off1, STD)],
                                      bounds_check="skip_entire_dma"), 16)

    load_iter(None)
    with kfori(p, n) as i:
        for c in ctrs:
            c.begin_body()
        begin_snaps(p)
        cur = Cursors(p, i, ctrs)
        body_start_waits(cur, ctrs, engines=ALLE)
        emit_fm_body(p, a, layer, i, cL, cA, cS, cV, cT, cP, cur, load_iter,
                     dst3 if layer == 0 else None,
                     (oh3, ol3) if final else None)
        for c in ctrs:
            c.end_body()
        cur.finish()
        end_snaps(p)
    drain_all(p, ctrs, n)
    pro_free(p, mark)


def emit_fm_body(p, a, layer, i, cL, cA, cS, cV, cT, cP, cur, load_iter,
                 dst3, ohl):
    nc = p.nc
    ps = p.ps
    final = (layer == 1)

    cur.wait(E.DVE, cL)
    cV.inc(nc.vector.tensor_copy(a["xb16"][:, :], a["xsp"][:, :]))
    cur.wait(E.Activation, cV)
    cS.inc(nc.scalar.activation(a["sqb"][:, :], a["xb16"][:, :], AF.Square))
    cur.wait(E.PE, cV)
    for dt in range(8):
        sl = 1 if dt < 4 else 0
        mm = nc.tensor.matmul(ps[6][0:1, 0:STD], p.ones_b[:, :],
                              a["xb16"][:, dt * 257 + sl: dt * 257 + sl + STD],
                              start=(dt == 0), stop=(dt == 7))
    cT.inc(mm)
    cur.wait(E.PE, cS)
    for dt in range(8):
        sl = 1 if dt < 4 else 0
        mm = nc.tensor.matmul(ps[7][0:1, 0:STD], p.ones_b[:, :],
                              a["sqb"][:, dt * 257 + sl: dt * 257 + sl + STD],
                              start=(dt == 0), stop=(dt == 7))
    cT.inc(mm)
    cur.wait(E.DVE, cT)
    nc.vector.tensor_scalar_mul(a["r_mu"][:, :], ps[6][0:1, 0:STD], 1.0 / DM)
    nc.vector.tensor_scalar_mul(a["r_t1"][:, :], ps[7][0:1, 0:STD], 1.0 / DM)
    nc.vector.tensor_mul(a["r_t2"][:, :], a["r_mu"][:, :], a["r_mu"][:, :])
    cV.inc(nc.vector.tensor_sub(a["r_t1"][:, :], a["r_t1"][:, :], a["r_t2"][:, :]))
    cur.wait(E.Activation, cV)
    nc.scalar.activation(a["r_t2"][:, :], a["r_t1"][:, :], AF.Ln, bias=1e-5)
    cS.inc(nc.scalar.activation(a["r_rstd"][:, :], a["r_t2"][:, :], AF.Exp, scale=-0.5))
    cur.wait(E.PE, cS)
    cT.inc(nc.tensor.matmul(ps[4][:, 0:STD], p.ones_r32[:, :], a["r_mu"][:, :],
                            start=True, stop=True))
    cT.inc(nc.tensor.matmul(ps[5][:, 0:STD], p.ones_r32[:, :], a["r_rstd"][:, :],
                            start=True, stop=True))
    cur.wait(E.DVE, cT)
    for dt in range(8):
        sl = 1 if dt < 4 else 0
        nc.vector.tensor_sub(a["zb"][:, dt * STD:(dt + 1) * STD].bitcast(F32)
                             if False else a["gel"][:, :],
                             a["xsp"][:, dt * 257 + sl: dt * 257 + sl + STD],
                             ps[4][:, 0:STD])
        nc.vector.tensor_mul(a["gel"][:, :], a["gel"][:, :], ps[5][:, 0:STD])
        v = nc.vector.tensor_scalar(a["zb"][:, dt * STD:(dt + 1) * STD],
                                    a["gel"][:, :], a["gam"][:, dt:dt + 1],
                                    a["bet"][:, dt:dt + 1], ALU.mult, ALU.add)
    cV.inc(v)

    for pi, (a_off, g_off, w) in enumerate(W1_TILES):
        bank = ps[pi % 4]
        cur.wait(E.PE, cV)
        cur.wait(E.PE, cS)
        for kt in range(8):
            nc.tensor.matmul(bank[0:w, 0:STD],
                             a["W1"][:, kt * 2 * HID + a_off: kt * 2 * HID + a_off + w],
                             a["zb"][:, kt * STD:(kt + 1) * STD],
                             start=(kt == 0), stop=False)
        for kt in range(8):
            mm = nc.tensor.matmul(bank[0:w, 256:256 + STD],
                                  a["W1"][:, kt * 2 * HID + g_off: kt * 2 * HID + g_off + w],
                                  a["zb"][:, kt * STD:(kt + 1) * STD],
                                  start=False, stop=(kt == 7))
        cT.inc(mm)
        cur.wait(E.Activation, cT)
        cS.inc(nc.scalar.activation(a["gel"][0:w, :], bank[0:w, 256:256 + STD],
                                    AF.Gelu, bias=a["b1g"][0:w, pi:pi + 1]))
        cur.wait(E.DVE, cS)
        cV.inc(nc.vector.scalar_tensor_tensor(
            a["glub"][0:w, pi * STD:(pi + 1) * STD],
            bank[0:w, 0:STD], a["b1a"][0:w, pi:pi + 1],
            a["gel"][0:w, :], ALU.add, ALU.mult))

    for dt in range(8):
        bank = ps[4 + dt % 4]
        cur.wait(E.PE, cV)
        for kt in range(22):
            kk = 128 if kt < 21 else HID - 21 * 128
            mm = nc.tensor.matmul(bank[:, 0:STD],
                                  a["W2"][0:kk, kt * DM + dt * 128: kt * DM + (dt + 1) * 128],
                                  a["glub"][0:kk, kt * STD:(kt + 1) * STD],
                                  start=(kt == 0), stop=(kt == 21))
        cT.inc(mm)
        cur.wait(E.DVE, cT)
        cV.inc(nc.vector.scalar_tensor_tensor(
            a["xout"][:, dt * STD:(dt + 1) * STD], bank[:, 0:STD],
            a["b2c"][:, dt:dt + 1],
            a["xsp"][:, dt * 257 + 1: dt * 257 + 1 + STD], ALU.add, ALU.add))

    if not final:
        cur.wait(E.Activation, cV)
        spo = rcalc(p, E.Activation, 4096, imm=128, mul=(i, STD))
        cA.inc(nc.scalar.dma_start(out=dst3[:, :, ds(spo, STD)],
                                   in_=xout3_of(p, a)), 16)
        cur.wait(E.SP, cV)
        cur.wait(E.SP, cS)
        cur.wait(E.SP, cT)
        load_iter(i)
        return

    cur.wait(E.DVE, cV)
    cur.wait(E.DVE, cT)
    cV.inc(nc.vector.tensor_copy(a["xb16"][:, 0:8 * STD], a["xout"][:, :]))
    cur.wait(E.Activation, cV)
    cS.inc(nc.scalar.activation(a["sqb"][:, 0:8 * STD], a["xb16"][:, 0:8 * STD],
                                AF.Square))
    cur.wait(E.PE, cV)
    for dt in range(8):
        mm = nc.tensor.matmul(ps[2][0:1, 0:STD], p.ones_b[:, :],
                              a["xb16"][:, dt * STD:(dt + 1) * STD],
                              start=(dt == 0), stop=(dt == 7))
    cT.inc(mm)
    cur.wait(E.PE, cS)
    for dt in range(8):
        mm = nc.tensor.matmul(ps[3][0:1, 0:STD], p.ones_b[:, :],
                              a["sqb"][:, dt * STD:(dt + 1) * STD],
                              start=(dt == 0), stop=(dt == 7))
    cT.inc(mm)
    cur.wait(E.DVE, cT)
    nc.vector.tensor_scalar_mul(a["r_mu"][:, :], ps[2][0:1, 0:STD], 1.0 / DM)
    nc.vector.tensor_scalar_mul(a["r_t1"][:, :], ps[3][0:1, 0:STD], 1.0 / DM)
    nc.vector.tensor_mul(a["r_t2"][:, :], a["r_mu"][:, :], a["r_mu"][:, :])
    cV.inc(nc.vector.tensor_sub(a["r_t1"][:, :], a["r_t1"][:, :], a["r_t2"][:, :]))
    cur.wait(E.Activation, cV)
    nc.scalar.activation(a["r_t2"][:, :], a["r_t1"][:, :], AF.Ln, bias=1e-5)
    cS.inc(nc.scalar.activation(a["r_rstd"][:, :], a["r_t2"][:, :], AF.Exp, scale=-0.5))
    cur.wait(E.PE, cS)
    cT.inc(nc.tensor.matmul(ps[0][:, 0:STD], p.ones_r32[:, :], a["r_mu"][:, :],
                            start=True, stop=True))
    cT.inc(nc.tensor.matmul(ps[1][:, 0:STD], p.ones_r32[:, :], a["r_rstd"][:, :],
                            start=True, stop=True))
    cur.wait(E.DVE, cT)
    for dt in range(8):
        d = a["xout"][:, dt * STD:(dt + 1) * STD]
        nc.vector.tensor_sub(d, d, ps[0][:, 0:STD])
        nc.vector.tensor_mul(d, d, ps[1][:, 0:STD])
        v = nc.vector.tensor_add(d, d, a["x1sp"][:, dt * STD:(dt + 1) * STD])
    cV.inc(v)
    nc.vector.tensor_copy(a["hi"][:, :], a["xout"][:, :])
    nc.vector.tensor_sub(a["xout"][:, :], a["xout"][:, :], a["hi"][:, :])
    cV.inc(nc.vector.tensor_copy(a["lo"][:, :], a["xout"][:, :]))
    cur.wait(E.SP, cV)
    oh3, ol3 = ohl
    oc = rcalc(p, E.SP, OUT_ROWS - STD, mul=(i, STD))
    cL.inc(nc.sync.dma_start(
        out=oh3[:, :, ds(oc, STD)],
        in_=a["hi"][:, :].rearrange("q (dt n) -> q dt n", n=STD)), 16)
    cL.inc(nc.sync.dma_start(
        out=ol3[:, :, ds(oc, STD)],
        in_=a["lo"][:, :].rearrange("q (dt n) -> q dt n", n=STD)), 16)
    cur.wait(E.SP, cS)
    cur.wait(E.SP, cT)
    load_iter(i)



def phase_F(p):
    nc = p.nc
    with (
        nc.sbuf_tensor("f_th", [128, DM], BF16) as fth,
        nc.sbuf_tensor("f_tl", [128, DM], BF16) as ftl,
        nc.sbuf_tensor("f_o", [128, DM], F32) as fo,
    ):
        cL2 = Ctr(nc, "f2L", p.deltas)
        cV2 = Ctr(nc, "f2V", p.deltas)
        nT128b = p.cfgv(CFG_T128, ALLE, SEQ // 128)

        with kfori(p, nT128b) as i:
            for c in (cL2, cV2):
                c.begin_body()
            cur2 = Cursors(p, i, (cL2, cV2))
            begin_snaps(p)
            body_start_waits(cur2, (cL2, cV2), engines=[E.Activation, E.DVE])
            oc2 = rcalc(p, E.Activation, OUT_ROWS - 128, mul=(i, 128))
            cL2.inc(nc.scalar.dma_start(out=fth[:, :], in_=p.sp_oh[:, ds(oc2, 128)],
                                      transpose=True), 16)
            cL2.inc(nc.scalar.dma_start(out=ftl[:, :], in_=p.sp_ol[:, ds(oc2, 128)],
                                      transpose=True), 16)
            cur2.wait(E.DVE, cL2)
            cV2.inc(nc.vector.tensor_add(fo[:, :], fth[:, :], ftl[:, :]))
            cur2.wait(E.Activation, cV2)
            cL2.inc(nc.scalar.dma_start(out=p.out[ds(oc2, 128), :], in_=fo[:, :]), 16)
            for c in (cL2, cV2):
                c.end_body()
            cur2.finish()
            end_snaps(p)
        nf = p.cfgv(CFG_T128, [E.Activation], SEQ // 128)
        nc.scalar.wait_ge(cL2.h, nf * cL2.delta + cL2.static)



def make_in_maps(inputs):
    x0 = np.ascontiguousarray(np.asarray(inputs["L0_attn_x"], np.float32).reshape(SEQ, DM))
    x0 = np.concatenate([x0, np.zeros((XC_TOK - ST, DM), np.float32)], 0)
    plans = plan_shards()
    rqc, rqs, rkc, rks = build_rotary_tables()
    rot_all = np.ascontiguousarray(np.concatenate([rqc, rqs, rkc, rks], axis=1))
    rotp = build_rotp()
    w1perm, _w1tiles = build_w1_perm()
    ident = np.eye(128, dtype=np.float32)
    projT_aug = np.concatenate([
        np.ascontiguousarray(np.asarray(inputs["L0_attn_proj"], np.float32).T),
        -np.ones((1, PM), np.float32)], 0)
    negi = np.broadcast_to(-np.arange(ST, dtype=np.float32)[None, :], (128, ST)).copy()
    alen = np.zeros((128, 4), np.float32)
    for j in range(4):
        alen[:, j] = 1.0 - (j * 128 + np.arange(128))
    rowp1 = (np.arange(128, dtype=np.float32) + 1.0).reshape(128, 1)
    sel2m = np.zeros((2, 128), np.float32)
    sel2m[0, 0:64] = 1.0
    sel2m[1, 64:128] = 1.0
    qsc = np.asarray(inputs["L0_ffn_q_scale"], np.float32)
    ksc = np.asarray(inputs["L0_ffn_k_scale"], np.float32)
    ffn_scl = np.zeros((128, 4), np.float32)
    for l in range(2):
        ffn_scl[:, 2 * l + 0] = np.concatenate([qsc[l], qsc[l]])
        ffn_scl[:, 2 * l + 1] = np.concatenate([ksc[l], ksc[l]])
    lg = np.asarray(inputs["L0_ffn_ln_g"], np.float32)
    lb = np.asarray(inputs["L0_ffn_ln_b"], np.float32)
    ffn_ln = np.zeros((128, 32), np.float32)
    for l in range(2):
        ffn_ln[:, l * 16:l * 16 + 8] = col_layout(lg[l])
        ffn_ln[:, l * 16 + 8:l * 16 + 16] = col_layout(lb[l])
    w1p = np.ascontiguousarray(np.asarray(inputs["L0_ffn_w1"], np.float32)[:, :, w1perm])
    b1p = np.ascontiguousarray(np.asarray(inputs["L0_ffn_b1"], np.float32)[:, w1perm])
    w2_ = np.asarray(inputs["L0_ffn_w2"], np.float32)
    w2p = np.zeros((2, HID_PAD, DM), np.float32)
    w2p[:, :HID, :] = w2_
    b2_ = np.asarray(inputs["L0_ffn_b2"], np.float32)
    ffn_b1c = np.zeros((128, 88), np.float32)
    for l in range(2):
        for pi, (a_off, g_off, w) in enumerate(_w1tiles):
            ffn_b1c[:w, l * 44 + pi] = b1p[l, a_off:a_off + w]
            ffn_b1c[:w, l * 44 + 22 + pi] = b1p[l, g_off:g_off + w]
    ffn_b2c = np.zeros((128, 16), np.float32)
    for l in range(2):
        ffn_b2c[:, l * 8:l * 8 + 8] = col_layout(b2_[l])

    def qm_table(plan, base_key, ncols):
        t = np.zeros((128, ncols), np.float32)
        r = np.arange(128, dtype=np.float32)
        B = plan[base_key]
        for j in range(ncols):
            c0s = B + j * 128
            t[:, j] = np.maximum(r, 512.0 - c0s)
        return t

    in_maps = []
    for c, pl in enumerate(plans):
        cfg = np.zeros((1, 32), np.uint32)
        vals = {CFG_B_C: pl["B_C"], CFG_N_LO: pl["n_lo"], CFG_N_FEED: pl["n_feed"],
                CFG_N_C: pl["n_c"], CFG_S: pl["S"], CFG_B_D1: pl["B_D1"],
                CFG_N_D1W: pl["n_d1w"], CFG_N_D1: pl["n_d1"], CFG_B_D2: pl["B_D2"],
                CFG_N_D2: pl["n_d2"], CFG_B_E1: pl["B_E1"], CFG_N_E1W: pl["n_e1w"],
                CFG_N_E1: pl["n_e1"], CFG_N_E2: pl["n_e2"], CFG_B_D1W: pl["B_D1W"],
                CFG_B_E1W: pl["B_E1W"], CFG_T128: pl["T"] // 128,
                CFG_OFFX1: pl["S"] - pl["B_C"]}
        for k, v in vals.items():
            cfg[0, k] = v
        m = dict(
            L0_attn_x=x0,
            L0_attn_q_w=np.asarray(inputs["L0_attn_q_w"], np.float32),
            L0_attn_q_b=np.asarray(inputs["L0_attn_q_b"], np.float32).reshape(1, DM),
            L0_attn_k_w=np.asarray(inputs["L0_attn_k_w"], np.float32),
            L0_attn_k_b=np.asarray(inputs["L0_attn_k_b"], np.float32).reshape(1, DM),
            L0_attn_v_w=np.asarray(inputs["L0_attn_v_w"], np.float32),
            L0_attn_v_b=np.asarray(inputs["L0_attn_v_b"], np.float32).reshape(1, DM),
            L0_attn_out_w=np.asarray(inputs["L0_attn_out_w"], np.float32),
            L0_attn_out_b=np.asarray(inputs["L0_attn_out_b"], np.float32).reshape(1, DM),
            projT_aug=projT_aug,
            L0_attn_gamma=col_layout(inputs["L0_attn_gamma"]),
            L0_attn_beta=col_layout(inputs["L0_attn_beta"]),
            L0_ffn_wqkv=np.asarray(inputs["L0_ffn_wqkv"], np.float32),
            L0_ffn_wo=np.asarray(inputs["L0_ffn_wo"], np.float32),
            L0_ffn_w1=w1p, L0_ffn_b1=b1p, L0_ffn_w2=w2p,
            L0_ffn_b2=np.asarray(inputs["L0_ffn_b2"], np.float32),
            cfg=cfg, negi=negi, alen=alen,
            rot_all=rot_all, rotp=rotp, ident=ident,
            rowp1=rowp1, ffn_scl=ffn_scl, ffn_ln=ffn_ln, sel2m=sel2m,
            ffn_b1c=ffn_b1c, ffn_b2c=ffn_b2c,
            qm_d1=qm_table(pl, "B_D1", 32), qm_e1=qm_table(pl, "B_E1", 24),
        )
        in_maps.append(m)
    return in_maps, plans


def kernel(**inputs):
    in_maps, plans = make_in_maps(inputs)
    deltas = {}
    build_program(deltas)
    pr = build_program(deltas)
    res = run_bass_kernel_spmd(pr.nc, in_maps, core_ids=list(range(NCORES)))
    out = np.empty((SEQ, DM), np.float32)
    for c, pl in enumerate(plans):
        out[pl["S"]:pl["E"], :] = res.results[c]["out"][: pl["T"], :]
    return out.reshape(1, SEQ, DM)


# version 10
# speedup vs baseline: 1.0446x; 1.0446x over previous
import numpy as np
import ml_dtypes
from contextlib import ExitStack
import dataclasses
import concourse.bass as bass
import concourse.mybir as mybir
from concourse.bass_utils import run_bass_kernel_spmd
from concourse.expressions import RuntimeValue

F32 = mybir.dt.float32
BF16 = mybir.dt.bfloat16
U32 = mybir.dt.uint32
AF = mybir.ActivationFunctionType
ALU = mybir.AluOpType
E = mybir.EngineType
ds = bass.ds

NCORES = 8
SEQ = 16384
DM = 1024
PH = 16
PD = 64
PM = 256
FH = 8
FD = 64
FIN = 512
HID = 2730
WIN = 512
LN16 = float(np.log(16.0))

ST = 512
STD = 256

HALO_C = 2560
HALO_D1 = 1536
WU_D1 = 768
HALO_D2 = 1280
HALO_E1 = 256
WU_E1 = 768

OUT_ROWS = 4096
SPX1_COLS = 5120
DEBUG = False
XC_TOK = 6656
N_STG = SEQ // ST

HID_PAD = 2816
ALLE = [E.SP, E.Activation, E.DVE, E.PE, E.Pool]
COMPE = [E.SP, E.Activation, E.DVE, E.PE]

(CFG_B_C, CFG_N_LO, CFG_N_FEED, CFG_N_C, CFG_S, CFG_B_D1, CFG_N_D1W, CFG_N_D1,
 CFG_B_D2, CFG_N_D2, CFG_B_E1, CFG_N_E1W, CFG_N_E1, CFG_N_E2, CFG_B_D1W,
 CFG_B_E1W, CFG_T128, CFG_OFFX1) = range(18)



def plan_shards(Ts=None):
    if Ts is None:
        Ts = [SEQ // NCORES] * NCORES
    assert sum(Ts) == SEQ and all(t % ST == 0 for t in Ts)
    plans = []
    s = 0
    for c in range(NCORES):
        S, Eend = s, s + Ts[c]
        p = dict(S=S, E=Eend, T=Ts[c])
        p["B_C"] = max(0, S - HALO_C)
        p["B_D1"] = max(0, S - HALO_D1)
        p["B_D1W"] = max(0, S - HALO_D1 - WU_D1)
        p["B_D2"] = max(0, S - HALO_D2)
        p["B_E1"] = max(0, S - HALO_E1)
        p["B_E1W"] = max(0, S - HALO_E1 - WU_E1)
        p["n_feed"] = p["B_C"] // ST
        p["n_c"] = (Eend - p["B_C"]) // ST
        p["n_lo"] = (Eend - p["B_C"]) // 128
        p["n_d1w"] = (p["B_D1"] - p["B_D1W"]) // STD
        p["n_d1"] = (Eend - p["B_D1"]) // STD
        p["n_d2"] = (Eend - p["B_D2"]) // STD
        p["n_e1w"] = (p["B_E1"] - p["B_E1W"]) // STD
        p["n_e1"] = (Eend - p["B_E1"]) // STD
        p["n_e2"] = Ts[c] // STD
        plans.append(p)
        s = Eend
    return plans


def build_rotary_tables():
    n = SEQ
    half = np.arange(0, FD, 2, dtype=np.float64)
    inv_freq = 1.0 / 10000.0 ** (half / FD)
    t = np.arange(n, dtype=np.float64)
    freqs = t[:, None] * inv_freq[None, :]
    freqs = np.concatenate([freqs, freqs], axis=-1)
    base = (half + 0.4 * FD) / (1.4 * FD)
    power = (t - n // 2) / float(WIN // 2)
    scale = base[None, :] ** power[:, None]
    scale = np.concatenate([scale, scale], axis=-1)
    cos, sin = np.cos(freqs), np.sin(freqs)
    def two(x):
        xT = x.T.astype(np.float32)
        xT = np.concatenate([xT, xT], 0)
        return np.concatenate([xT, np.zeros((128, STD), np.float32)], 1)
    return (two(cos * scale), two(sin * scale),
            two(cos / scale), two(sin / scale))


def build_rotp():
    P = np.zeros((128, 128), np.float32)
    for hb in (0, 64):
        for d in range(32):
            P[hb + d + 32, hb + d] = -1.0
            P[hb + d, hb + d + 32] = 1.0
    return P


def build_w1_perm():
    perm = []
    tiles = []
    off = 0
    i = 0
    while i * 128 < HID:
        w = min(128, HID - i * 128)
        a_cols = list(range(i * 128, i * 128 + w))
        g_cols = list(range(HID + i * 128, HID + i * 128 + w))
        perm.extend(a_cols)
        perm.extend(g_cols)
        tiles.append((off, off + w, w))
        off += 2 * w
        i += 1
    return np.array(perm, np.int64), tiles


def col_layout(vec, dm=DM):
    return np.ascontiguousarray(np.asarray(vec, np.float32).reshape(dm // 128, 128).T)



class Ctr:

    def __init__(self, nc, name, deltas):
        self.h = nc.semaphore(name).__enter__()
        self.nc = nc
        self.name = name
        self.deltas = deltas
        self.static = 0
        self.cur = 0
        self.in_loop = False

    def inc(self, instr, amt=1):
        instr.then_inc(self.h, amt)
        if self.in_loop:
            self.cur += amt
        else:
            self.static += amt
        return instr

    @property
    def delta(self):
        return self.deltas.get(self.name, 1 << 18)

    def at(self, i):
        return i * self.delta + (self.static + self.cur)

    def start_of(self, i):
        return i * self.delta + self.static

    def begin_body(self):
        self.in_loop = True
        self.cur = 0

    def end_body(self):
        self.in_loop = False
        d = self.deltas.get(self.name)
        if d is None:
            self.deltas[self.name] = self.cur
        else:
            assert d == self.cur, f"{self.name}: delta {d} -> {self.cur}"
        self.cur = 0


class MMRot:

    def __init__(self, p, banks, cur):
        self.p = p
        self.banks = banks
        self.cur = cur
        self.idx = 0
        self.guard = {b: None for b in banks}

    def group(self):
        b = self.banks[self.idx % len(self.banks)]
        self.idx += 1
        g = self.guard[b]
        if g is not None:
            ctr, K = g
            self.cur.wait(E.PE, ctr, K)
        return self.p.ps[b], b

    def done(self, b, ctr):
        self.guard[b] = (ctr, ctr.static + ctr.cur)


class DmaOff:

    def __init__(self, p, eng, maxv, name=None):
        self.p, self.eng, self.maxv = p, eng, maxv
        self.reg = p.nc.engines[eng].alloc_register(name)

    def init_to(self, expr, const=0):
        self.p.nc.engines[self.eng].reg_alu(self.reg, expr, const, ALU.add)

    def step(self, amt):
        self.p.nc.engines[self.eng].reg_alu(self.reg, self.reg, amt, ALU.add)

    def val(self):
        return RuntimeValue(val=self.reg, min_val=0, max_val=self.maxv)

    def ap(self, static_ap):
        return dataclasses.replace(static_ap, offset=self.val())

    def free(self):
        self.p.nc.engines[self.eng].free_register(self.reg)


class Cursors:

    def __init__(self, p, i_val, ctrs):
        self.p = p
        self.i = i_val
        self.state = {}

    def wait(self, eng, ctr, K=None):
        nc = self.p.nc
        if K is None:
            K = ctr.static + ctr.cur
        key = (eng, ctr.name)
        st = self.state.get(key)
        e = nc.engines[eng]
        if st is None:
            reg = e.alloc_register(None)
            st = [reg, None]
            self.state[key] = st
        reg, maxK = st
        if maxK is not None and K <= maxK:
            return
        if maxK is None:
            ih = self.i.val[eng]
            e.reg_alu(reg, ih, ctr.delta, ALU.mult)
            if K:
                e.reg_alu(reg, reg, K, ALU.add)
        else:
            e.reg_alu(reg, reg, K - maxK, ALU.add)
        e.wait_ge(ctr.h, reg)
        st[1] = K

    def finish(self):
        for (eng, _), (reg, _m) in self.state.items():
            self.p.nc.engines[eng].free_register(reg)


class P:
    pass



class Stream:

    def __init__(self, p, eng, maxv):
        self.p, self.eng, self.maxv = p, eng, maxv
        self.reg = p.nc.engines[eng].alloc_register(None)

    def mk(self):
        return RuntimeValue(val=self.reg, min_val=0, max_val=self.maxv)

    def setv(self, imm=0, add=(), sub=()):
        e = self.p.nc.engines[self.eng]

        def h(x):
            if isinstance(x, RuntimeValue):
                try:
                    return x.val[self.eng]
                except TypeError:
                    return x.val
            return x

        init = False
        for x in add:
            if init:
                e.reg_alu(self.reg, self.reg, h(x), ALU.add)
            else:
                e.reg_alu(self.reg, h(x), 0, ALU.add)
                init = True
        for x in sub:
            assert init
            e.reg_alu(self.reg, self.reg, h(x), ALU.subtract)
        if init:
            if imm:
                e.reg_alu(self.reg, self.reg, imm, ALU.add)
        else:
            e.reg_alu(self.reg, self.p.zero_regs[self.eng], imm, ALU.add)

    def bump(self, step):
        self.p.nc.engines[self.eng].reg_alu(self.reg, self.reg, step, ALU.add)


def stream(p, eng, maxv):
    key = (eng, maxv)
    st = p.streams.get(key)
    if st is None:
        st = Stream(p, eng, maxv)
        p.streams[key] = st
    return st


def rcalc(p, eng, maxv, imm=0, mul=None, add=(), sub=()):
    nc = p.nc
    e = nc.engines[eng]
    r = e.alloc_register(None)

    def h(x):
        if isinstance(x, RuntimeValue):
            try:
                return x.val[eng]
            except TypeError:
                return x.val
        return x

    init = False
    if mul is not None:
        e.reg_alu(r, h(mul[0]), mul[1], ALU.mult)
        init = True
    for x in add:
        if init:
            e.reg_alu(r, r, h(x), ALU.add)
        else:
            e.reg_alu(r, h(x), 0, ALU.add)
            init = True
    assert init
    for x in sub:
        e.reg_alu(r, r, h(x), ALU.subtract)
    if imm:
        e.reg_alu(r, r, imm, ALU.add)
    lst = getattr(p, "_snap_regs", None)
    if lst is not None:
        lst.append(r)
    else:
        p._pro_regs.append(r)
    return RuntimeValue(val=r, min_val=0, max_val=maxv)


def snapo(p, expr, maxv, engines=(E.SP,)):
    v = p.nc.snap(expr, engines=list(engines), min_val=0, max_val=maxv)
    try:
        regs = v.val
        lst = getattr(p, "_snap_regs", None)
        if lst is not None:
            for eng in engines:
                lst.append(regs[eng])
    except Exception:
        pass
    return v


def begin_snaps(p):
    p._snap_regs = []


def end_snaps(p):
    for r in p._snap_regs:
        p.nc.engines[r.engine].free_register(r)
    p._snap_regs = None


from contextlib import contextmanager


@contextmanager
def kfori(p, n, engines=ALLE):
    nc = p.nc
    loop_id = nc.next_id()
    loop_name = f"kfori_{loop_id}"
    loop_start = f"{loop_name}_loop"
    loop_end = f"{loop_name}_end"
    registers = nc.alloc_registers(f"{loop_name}_i", engines=engines)
    nc.regs_mov(registers, 0)
    nc.br(loop_start, engines=engines)
    max_val = getattr(n, "max_val", None)
    if max_val is not None:
        max_val -= 1
    snap_rv = None
    with nc.body(loop_start, valid_engines=engines):
        snap_rv = nc.snap(registers, min_val=0, max_val=max_val)
        yield snap_rv
        nc.regs_alu(registers, registers, 1, op=ALU.add)
        nc.br_lt(registers, n, on_true=loop_start, on_false=loop_end,
                 engines=engines)
    nc.switch_bb(loop_end)
    for e in engines:
        nc.engines[e].free_register(registers[e])
        try:
            nc.engines[e].free_register(snap_rv.val[e])
        except Exception:
            pass


def pro_mark(p):
    return len(p._pro_regs)


def pro_free(p, mark):
    for r in p._pro_regs[mark:]:
        p.nc.engines[r.engine].free_register(r)
    del p._pro_regs[mark:]

def drain_all(p, ctrs, n, engines=ALLE):
    nc = p.nc
    for e in engines:
        eng = nc.engines[e]
        r = eng.alloc_register(None)
        for c in ctrs:
            eng.reg_alu(r, n.val[e], c.delta, ALU.mult)
            if c.static:
                eng.reg_alu(r, r, c.static, ALU.add)
            eng.wait_ge(c.h, r)
        eng.free_register(r)


def body_start_waits(cur, ctrs, engines=COMPE):
    for e in engines:
        for c in ctrs:
            cur.wait(e, c, K=c.static)



def build_program(deltas):
    p = P()
    nc = bass.Bass("TRN2", target_bir_lowering=False)
    p.nc = nc
    p.deltas = deltas

    def din(name, shape, dt=F32):
        return nc.dram_tensor(name, shape, dt, kind="ExternalInput").ap()

    p.x0 = din("L0_attn_x", [SEQ + XC_TOK - ST, DM])
    p.q_w = din("L0_attn_q_w", [DM, DM])
    p.q_b = din("L0_attn_q_b", [1, DM])
    p.k_w = din("L0_attn_k_w", [DM, DM])
    p.k_b = din("L0_attn_k_b", [1, DM])
    p.v_w = din("L0_attn_v_w", [DM, DM])
    p.v_b = din("L0_attn_v_b", [1, DM])
    p.out_w = din("L0_attn_out_w", [DM, DM])
    p.out_b = din("L0_attn_out_b", [1, DM])
    p.projT_aug = din("projT_aug", [65, PM])
    p.gamma = din("L0_attn_gamma", [128, 8])
    p.beta = din("L0_attn_beta", [128, 8])
    p.wqkv = din("L0_ffn_wqkv", [2, DM, 3 * FIN])
    p.wo = din("L0_ffn_wo", [2, FIN, DM])
    p.w1 = din("L0_ffn_w1", [2, DM, 2 * HID])
    p.b1 = din("L0_ffn_b1", [2, 2 * HID])
    p.w2 = din("L0_ffn_w2", [2, HID_PAD, DM])
    p.b2 = din("L0_ffn_b2", [2, DM])

    p.cfg = din("cfg", [1, 32], U32)
    p.negi = din("negi", [128, ST])
    p.alen = din("alen", [128, 4])
    p.rot_all = din("rot_all", [128, 4 * (SEQ + STD)])
    p.rotp = din("rotp", [128, 128])
    p.ident = din("ident", [128, 128])
    p.rowp1 = din("rowp1", [128, 1])
    p.ffn_scl = din("ffn_scl", [128, 4])
    p.ffn_ln = din("ffn_ln", [128, 32])
    p.ffn_b1c = din("ffn_b1c", [128, 88])
    p.ffn_b2c = din("ffn_b2c", [128, 16])
    p.sel2m = din("sel2m", [2, 128])
    p.qm_d1 = din("qm_d1", [128, 32])
    p.qm_e1 = din("qm_e1", [128, 24])

    p.out = nc.dram_tensor("out", [OUT_ROWS, DM], F32, kind="ExternalOutput").ap()

    p.x0b = nc.dram_tensor("x0b", [SEQ + XC_TOK - ST, DM], BF16).ap()
    p.x0bT_d = nc.dram_tensor("x0bT_d", [DM, SEQ + ST], BF16).ap()
    p.xc_f32 = nc.dram_tensor("xc_f32", [XC_TOK, DM], F32).ap()
    p.xcb = nc.dram_tensor("xcb", [XC_TOK, DM], BF16).ap()
    p.lo_d = nc.dram_tensor("lo_d", [XC_TOK, DM], BF16).ap()
    p.loT_d = nc.dram_tensor("loT_d", [DM, XC_TOK], BF16).ap()
    p.sp_x1 = nc.dram_tensor("sp_x1", [DM, SPX1_COLS + ST], F32).ap()
    p.sp_bm = nc.dram_tensor("sp_bm", [DM, 4096 + STD], F32).ap()
    p.sp_l0 = nc.dram_tensor("sp_l0", [DM, 4096 + STD], F32).ap()
    p.sp_l1 = nc.dram_tensor("sp_l1", [DM, 3072 + STD], F32).ap()
    p.dbg1 = nc.dram_tensor("dbg1", [128, 12 * 4096], F32).ap()
    p.dbg2 = nc.dram_tensor("dbg2", [128, 12 * 4096], F32).ap()
    p.dbg3 = nc.dram_tensor("dbg3", [128, 12 * 4096], F32).ap()
    p.dbg4 = nc.dram_tensor("dbg4", [128, 12 * 4096], F32).ap()
    p.sp_oh = nc.dram_tensor("sp_oh", [DM, OUT_ROWS], BF16).ap()
    p.sp_ol = nc.dram_tensor("sp_ol", [DM, OUT_ROWS], BF16).ap()

    p.bar_sem = nc.semaphore("bar").__enter__()
    p.bar_cnt = 0

    def barrier():
        p.bar_cnt += 1
        for e in ALLE:
            nc.engines[e].nop().then_inc(p.bar_sem, 1)
        for e in ALLE:
            nc.engines[e].wait_ge(p.bar_sem, p.bar_cnt * len(ALLE))

    p.barrier = barrier

    p.cfg_s = nc.alloc_sbuf_tensor("cfg_s", [1, 32], U32)
    p.ones_b = nc.alloc_sbuf_tensor("ones_b", [128, 1], BF16)
    p.ones_r = nc.alloc_sbuf_tensor("ones_r", [1, ST], BF16)
    p.ones_r32 = nc.alloc_sbuf_tensor("ones_r32", [1, 128], F32)
    p.ps_t = nc.alloc_psum_tensor("ps_t", [128, 4096], F32)
    p.ps = [p.ps_t[:, i * 512:(i + 1) * 512] for i in range(8)]
    p.ps_bt = p.ps_t.bitcast(BF16)

    CONSTS = [1e-5, 1e-6, -0.5, 0.5, LN16, 0.125, 8.0, -80.0, 1.0 / DM]
    p.cst = nc.alloc_sbuf_tensor("cstp", [128, len(CONSTS)], F32)
    for j, v in enumerate(CONSTS):
        nc.const_aps.register(F32, float(v), p.cst[:, j:j + 1])

    sem_init = nc.semaphore("init").__enter__()
    nc.sync.dma_start(out=p.cfg_s[:, :], in_=p.cfg).then_inc(sem_init, 16)
    nc.gpsimd.memset(p.ones_b[:, :], 1.0).then_inc(sem_init, 1)
    nc.gpsimd.memset(p.ones_r[:, :], 1.0).then_inc(sem_init, 1)
    nc.gpsimd.memset(p.ones_r32[:, :], 1.0).then_inc(sem_init, 1)
    for j, v in enumerate(CONSTS):
        nc.gpsimd.memset(p.cst[:, j:j + 1], float(v)).then_inc(sem_init, 1)
    for e in ALLE:
        nc.engines[e].wait_ge(sem_init, 19 + len(CONSTS))

    p._pro_regs = []
    p.streams = {}
    p.zero_regs = {}
    for e in ALLE:
        z = nc.engines[e].alloc_register(f"zero_{e}")
        nc.engines[e].reg_mov(z, 0)
        p.zero_regs[e] = z

    def cfgv(idx, engines, maxv):
        v = nc.values_load(p.cfg_s[0:1, idx:idx + 1], engines=engines,
                           min_val=0, max_val=maxv,
                           skip_runtime_bounds_check=True)
        try:
            for e_ in engines:
                p._pro_regs.append(v.val[e_])
        except Exception:
            pass
        return v

    p.cfgv = cfgv

    import os
    _kph = os.environ.get("KPH", "all")
    phase_A(p)
    for e in ALLE:
        nc.engines[e].wait_ge(p.s_pad, 64)
    barrier()
    phase_BC(p)
    barrier()
    if _kph in ("all", "fa0", "fm0", "fa1", "fm1"):
        phase_FA(p, 0)
    if _kph in ("all", "fm0", "fa1", "fm1"):
        phase_FM(p, 0)
    if _kph in ("all", "fa1", "fm1"):
        phase_FA(p, 1)
    if _kph in ("all", "fm1"):
        phase_FM(p, 1)
    phase_F(p)
    barrier()
    mybir.codegen_inst_isa_subclasses(nc)
    return p



def phase_A(p):
    nc = p.nc
    sA = nc.semaphore("aA").__enter__()
    sT = nc.semaphore("aT").__enter__()
    sL = nc.semaphore("aLl").__enter__()
    sV = nc.semaphore("aVv").__enter__()
    p.stg_sem = nc.semaphore("stg").__enter__()

    s_cast = nc.semaphore("acst").__enter__()
    nc.gpsimd.dma_start(out=p.x0b[0:SEQ, :], in_=p.x0[0:SEQ, :]).then_inc(s_cast, 16)
    p.s_pad = nc.semaphore("apad").__enter__()
    s_padm = nc.semaphore("apadm").__enter__()
    with nc.sbuf_tensor("a_zpad", [128, 1024], F32) as zt:
        nc.gpsimd.memset(zt[:, :], 0.0).then_inc(s_padm, 1)
        nc.gpsimd.wait_ge(s_padm, 1)
        zt3 = zt[:, :].rearrange("q (dt n) -> q dt n", n=128)
        for sp in (p.sp_x1, p.sp_bm, p.sp_l0, p.sp_l1):
            dstp = sp[:, 0:128].rearrange("(dt q) n -> q dt n", q=128)
            nc.gpsimd.dma_start(out=dstp, in_=zt3).then_inc(p.s_pad, 16)
        for e in ALLE:
            nc.engines[e].wait_ge(p.s_pad, 64)
    bc = p.cfgv(CFG_B_C, [E.Pool], SEQ)
    bcs = nc.snap(bc * DM, engines=[E.Pool], min_val=0, max_val=SEQ * DM)
    ap_x0 = dataclasses.replace(p.x0[0:XC_TOK - ST, :], offset=bcs)
    nc.gpsimd.dma_start(out=p.xc_f32[0:XC_TOK - ST, :], in_=ap_x0).then_inc(sA, 16)
    nc.gpsimd.wait_ge(s_cast, 16)
    ap_x0b = dataclasses.replace(p.x0b[0:XC_TOK - ST, :], offset=bcs)
    nc.gpsimd.dma_start(out=p.xcb[0:XC_TOK - ST, :], in_=ap_x0b).then_inc(sA, 16)

    with (
        nc.sbuf_tensor("a_tt0", [128, 8 * ST], BF16) as tt0,
        nc.sbuf_tensor("a_tt1", [128, 8 * ST], BF16) as tt1,
        nc.sbuf_tensor("a_f32", [128, DM], F32) as tf,
        nc.sbuf_tensor("a_b16", [128, DM], BF16) as tb,
        nc.sbuf_tensor("a_lo", [128, DM], BF16) as tl,
    ):
        nc.sync.wait_ge(s_cast, 16)
        for j in range(N_STG):
            nc.sync.wait_ge(sT, 144 * j)
            nc.sync.nop().then_inc(p.stg_sem, 1)
            t = tt0 if j % 2 == 0 else tt1
            for dt in range(8):
                nc.sync.dma_start(out=t[:, dt * ST:(dt + 1) * ST],
                                  in_=p.x0b[j * ST:(j + 1) * ST, dt * 128:(dt + 1) * 128],
                                  transpose=True).then_inc(sT, 16)
            nc.sync.wait_ge(sT, 144 * j + 128)
            nc.sync.dma_start(
                out=p.x0bT_d.rearrange("(dt q) n -> q dt n", q=128)[:, :, j * ST:(j + 1) * ST],
                in_=t[:, :].rearrange("q (dt n) -> q dt n", n=ST)).then_inc(sT, 16)
        nc.sync.wait_ge(sT, 144 * N_STG)
        nc.sync.nop().then_inc(p.stg_sem, 1)

        nc.scalar.wait_ge(sA, 32)
        nlo = (XC_TOK - ST) // 128
        for j in range(nlo):
            if j >= 1:
                nc.scalar.wait_ge(sV, j)
            nc.scalar.dma_start(out=tf[:, :], in_=p.xc_f32[j * 128:(j + 1) * 128, :]).then_inc(sL, 16)
            nc.scalar.dma_start(out=tb[:, :], in_=p.xcb[j * 128:(j + 1) * 128, :]).then_inc(sL, 16)
            nc.vector.wait_ge(sL, j * 48 + 32)
            nc.vector.tensor_sub(tl[:, :], tf[:, :], tb[:, :]).then_inc(sV, 1)
            nc.scalar.wait_ge(sV, j + 1)
            nc.scalar.dma_start(out=p.lo_d[j * 128:(j + 1) * 128, :], in_=tl[:, :]).then_inc(sL, 16)

        nstt = (XC_TOK - ST) // ST
        base = N_STG * 144
        nc.sync.wait_ge(sL, nlo * 48)
        for j in range(nstt):
            t = tt0 if j % 2 == 0 else tt1
            nc.sync.wait_ge(sT, base + 144 * j)
            for dt in range(8):
                nc.sync.dma_start(out=t[:, dt * ST:(dt + 1) * ST],
                                  in_=p.lo_d[j * ST:(j + 1) * ST, dt * 128:(dt + 1) * 128],
                                  transpose=True).then_inc(sT, 16)
            nc.sync.wait_ge(sT, base + 144 * j + 128)
            nc.sync.dma_start(
                out=p.loT_d.rearrange("(dt q) n -> q dt n", q=128)[:, :, j * ST:(j + 1) * ST],
                in_=t[:, :].rearrange("q (dt n) -> q dt n", n=ST)).then_inc(sT, 16)
        nc.sync.wait_ge(sT, base + 144 * nstt)
        nc.sync.nop().then_inc(p.stg_sem, 1)
        p.stg_all = N_STG + 2



def alloc_perf(p, stk):
    nc = p.nc
    a = {}

    def sb(name, shape, dt):
        a[name] = stk.enter_context(nc.sbuf_tensor(name, shape, dt))

    sb("Wq", [128, 8 * DM], BF16)
    sb("Wk", [128, 8 * DM], BF16)
    sb("Wv", [128, 8 * DM], BF16)
    sb("Wo", [128, 8 * DM], BF16)
    sb("projT", [65, PM], BF16)
    sb("qbr", [1, DM], BF16)
    sb("kbr", [1, DM], BF16)
    sb("vb_row", [1, DM], BF16)
    sb("ob_row", [1, DM], BF16)
    sb("gam", [128, 8], F32)
    sb("bet", [128, 8], F32)
    sb("negi_s", [128, ST], F32)
    sb("alen_s", [128, 4], F32)
    sb("x0bT", [128, 8 * ST], BF16)
    sb("x0T", [128, 8 * ST], F32)
    sb("yln", [128, 8 * ST], F32)
    sb("gb1", [128, 8 * ST], F32)
    sb("ylnb", [128, 8 * ST], BF16)
    sb("sqb", [128, 8 * ST], BF16)
    sb("qhat", [128, 8 * ST], BF16)
    sb("khat", [128, 8 * ST], BF16)
    sb("vnat", [128, 4 * DM], BF16)
    sb("qa", [128, ST], BF16)
    sb("ka", [128, ST], BF16)
    sb("qsq", [128, ST], BF16)
    sb("qpT", [128, 2 * ST], BF16)
    sb("kpT", [128, 2 * ST], BF16)
    sb("kpn", [128, 4 * PM], BF16)
    sb("Am", [128, 4 * ST], BF16)
    sb("state", [128, PH * 2 * 65], F32)
    sb("stateb", [128, PH * 2 * 65], BF16)
    sb("dbc", [64, ST], F32)
    sb("r_mu", [1, ST], F32)
    sb("r_t1", [1, ST], F32)
    sb("r_t2", [1, ST], F32)
    sb("r_rstd", [1, ST], F32)
    assert nc.sbuf_bytes_remaining >= 0
    return a


def phase_BC(p):
    nc = p.nc
    with ExitStack() as stk:
        a = alloc_perf(p, stk)
        s_w = nc.semaphore("bc_w").__enter__()
        s_w2 = nc.semaphore("bc_w2").__enter__()
        g = nc.gpsimd
        g.dma_start(out=a["Wq"][:, :].rearrange("q (kt o) -> q kt o", kt=8), in_=p.q_w.rearrange("(kt q) o -> q kt o", q=128)).then_inc(s_w, 16)
        g.dma_start(out=a["Wk"][:, :].rearrange("q (kt o) -> q kt o", kt=8), in_=p.k_w.rearrange("(kt q) o -> q kt o", q=128)).then_inc(s_w, 16)
        g.dma_start(out=a["Wv"][:, :].rearrange("q (kt o) -> q kt o", kt=8), in_=p.v_w.rearrange("(kt q) o -> q kt o", q=128)).then_inc(s_w, 16)
        g.dma_start(out=a["Wo"][:, :].rearrange("q (kt o) -> q kt o", kt=8), in_=p.out_w.rearrange("(kt q) o -> q kt o", q=128)).then_inc(s_w, 16)
        g.dma_start(out=a["projT"][:, :], in_=p.projT_aug).then_inc(s_w, 16)
        g.dma_start(out=a["vb_row"][:, :], in_=p.v_b).then_inc(s_w, 16)
        g.dma_start(out=a["ob_row"][:, :], in_=p.out_b).then_inc(s_w, 16)
        nc.sync.dma_start(out=a["negi_s"][:, :], in_=p.negi).then_inc(s_w2, 16)
        nc.sync.dma_start(out=a["alen_s"][:, :], in_=p.alen).then_inc(s_w2, 16)
        g.dma_start(out=a["qbr"][:, :], in_=p.q_b).then_inc(s_w, 16)
        g.dma_start(out=a["kbr"][:, :], in_=p.k_b).then_inc(s_w, 16)
        nc.sync.dma_start(out=a["gam"][:, :], in_=p.gamma).then_inc(s_w2, 16)
        nc.sync.dma_start(out=a["bet"][:, :], in_=p.beta).then_inc(s_w2, 16)
        for e in ALLE:
            nc.engines[e].wait_ge(s_w, 9 * 16)
            nc.engines[e].wait_ge(s_w2, 4 * 16)
        s_ms = nc.semaphore("bc_ms").__enter__()
        nc.gpsimd.memset(a["state"][:, :], 0.0).then_inc(s_ms, 1)
        nc.gpsimd.memset(a["stateb"][:, :], 0.0).then_inc(s_ms, 1)
        for e in ALLE:
            nc.engines[e].wait_ge(s_ms, 2)
        p.barrier()

        emit_perf_loop(p, a, feed_only=True)
        p.barrier()
        emit_perf_loop(p, a, feed_only=False)
        p.barrier()


def emit_perf_loop(p, a, feed_only):
    nc = p.nc
    mark = pro_mark(p)
    tag = "f" if feed_only else "m"
    cL = Ctr(nc, f"p{tag}L", p.deltas)
    cS = Ctr(nc, f"p{tag}S", p.deltas)
    cV = Ctr(nc, f"p{tag}V", p.deltas)
    cT = Ctr(nc, f"p{tag}T", p.deltas)
    ctrs = (cL, cS, cV, cT)

    if feed_only:
        n = p.cfgv(CFG_N_FEED, ALLE, SEQ // ST)
        base = None
    else:
        n = p.cfgv(CFG_N_C, ALLE, SEQ // ST + 8)
        base = p.cfgv(CFG_B_C, ALLE, SEQ)

    x0bT_src = p.x0bT_d.rearrange("(dt q) n -> q dt n", q=128)
    loT_src = p.loT_d.rearrange("(dt q) n -> q dt n", q=128)
    spx_dst = p.sp_x1.rearrange("(dt q) n -> q dt n", q=128)
    x0bT_dst = a["x0bT"][:, :].rearrange("q (dt n) -> q dt n", n=ST)
    lo_dst = a["qhat"][:, :].rearrange("q (dt n) -> q dt n", n=ST)
    x0T_src3 = a["x0T"][:, :].rearrange("q (dt n) -> q dt n", n=ST)

    def load_x0bT(off_expr, stg_wait=None):
        if stg_wait is not None:
            nc.sync.wait_ge(p.stg_sem, stg_wait)
        cL.inc(nc.sync.dma_start(out=x0bT_dst, in_=x0bT_src[:, :, ds(off_expr, ST)]), 16)

    

    def load_loT(off_expr):
        cL.inc(nc.scalar.dma_start(out=lo_dst, in_=loT_src[:, :, ds(off_expr, ST)]), 16)

    p.load_x0bT, p.load_loT = load_x0bT, load_loT
    p.spx_dst, p.x0T_src3 = spx_dst, x0T_src3

    if feed_only:
        load_x0bT(0, stg_wait=2)
    else:
        nc.sync.wait_ge(p.stg_sem, p.stg_all)
        b0 = rcalc(p, E.SP, SEQ, add=(base,))
        p.c_base = b0
        load_x0bT(b0)
        load_loT(0)

    if DEBUG and not feed_only:
        p.sdbg_s = nc.semaphore("sdbgs").__enter__()
        p.sdbg_g = nc.semaphore("sdbgg").__enter__()
    from contextlib import nullcontext
    guard = nc.If(n > 0) if feed_only else nullcontext()
    with guard:
        with kfori(p, n) as i:
            for c in ctrs:
                c.begin_body()
            begin_snaps(p)
            cur = Cursors(p, i, ctrs)
            body_start_waits(cur, ctrs)
            emit_perf_body(p, a, i, feed_only, cL, cS, cV, cT, cur)
            for c in ctrs:
                c.end_body()
            cur.finish()
            end_snaps(p)
    drain_all(p, ctrs, n)
    pro_free(p, mark)
    if DEBUG and not feed_only:
        nc.sync.wait_ge(p.sdbg_s.h if hasattr(p.sdbg_s, "h") else p.sdbg_s, n * 32)
        nc.gpsimd.wait_ge(p.sdbg_g.h if hasattr(p.sdbg_g, "h") else p.sdbg_g, n * 32)


def emit_perf_body(p, a, i, feed_only, cL, cS, cV, cT, cur):
    nc = p.nc
    ps = p.ps
    rot = MMRot(p, [0, 1, 2, 3], cur)

    cur.wait(E.Activation, cL)
    cS.inc(nc.scalar.activation(a["sqb"][:, :], a["x0bT"][:, :], AF.Square))

    cur.wait(E.PE, cL)
    for kt in range(8):
        mm = nc.tensor.matmul(ps[6][0:1, :], p.ones_b[:, :],
                              a["x0bT"][:, kt * ST:(kt + 1) * ST],
                              start=(kt == 0), stop=(kt == 7))
    cT.inc(mm)
    cur.wait(E.PE, cS)
    for kt in range(8):
        mm = nc.tensor.matmul(ps[7][0:1, :], p.ones_b[:, :],
                              a["sqb"][:, kt * ST:(kt + 1) * ST],
                              start=(kt == 0), stop=(kt == 7))
    cT.inc(mm)

    cur.wait(E.DVE, cT)
    nc.vector.tensor_scalar_mul(a["r_mu"][:, :], ps[6][0:1, :], 1.0 / DM)
    nc.vector.tensor_scalar_mul(a["r_t1"][:, :], ps[7][0:1, :], 1.0 / DM)
    nc.vector.tensor_mul(a["r_t2"][:, :], a["r_mu"][:, :], a["r_mu"][:, :])
    cV.inc(nc.vector.tensor_sub(a["r_t1"][:, :], a["r_t1"][:, :], a["r_t2"][:, :]))
    cur.wait(E.Activation, cV)
    nc.scalar.activation(a["r_t2"][:, :], a["r_t1"][:, :], AF.Ln, bias=1e-5)
    cS.inc(nc.scalar.activation(a["r_rstd"][:, :], a["r_t2"][:, :], AF.Exp, scale=-0.5))

    cur.wait(E.PE, cS)
    cT.inc(nc.tensor.matmul(ps[4][:, :], p.ones_r32[:, :], a["r_mu"][:, :], start=True, stop=True))
    cT.inc(nc.tensor.matmul(ps[5][:, :], p.ones_r32[:, :], a["r_rstd"][:, :], start=True, stop=True))

    cur.wait(E.DVE, cT)
    if feed_only:
        for kt in range(8):
            nc.vector.tensor_sub(a["ylnb"][:, kt * ST:(kt + 1) * ST],
                                 a["x0bT"][:, kt * ST:(kt + 1) * ST], ps[4][:, :])
        for kt in range(8):
            v = nc.vector.tensor_mul(a["ylnb"][:, kt * ST:(kt + 1) * ST],
                                     a["ylnb"][:, kt * ST:(kt + 1) * ST], ps[5][:, :])
        cV.inc(v)
    else:
        for kt in range(8):
            nc.vector.tensor_add(a["x0T"][:, kt * ST:(kt + 1) * ST],
                                 a["x0bT"][:, kt * ST:(kt + 1) * ST],
                                 a["qhat"][:, kt * ST:(kt + 1) * ST])
        for kt in range(8):
            nc.vector.tensor_sub(a["yln"][:, kt * ST:(kt + 1) * ST],
                                 a["x0T"][:, kt * ST:(kt + 1) * ST], ps[4][:, :])
        for kt in range(8):
            nc.vector.tensor_mul(a["yln"][:, kt * ST:(kt + 1) * ST],
                                 a["yln"][:, kt * ST:(kt + 1) * ST], ps[5][:, :])
        cV.inc(nc.vector.tensor_copy(a["ylnb"][:, :], a["yln"][:, :]))

    def proj_block(Wname, bias_row, dst):
        cur.wait(E.PE, cV)
        for dt in range(8):
            bank, bi = rot.group()
            for kt in range(8):
                nc.tensor.matmul(bank[:, :],
                                 a[Wname][:, kt * DM + dt * 128: kt * DM + (dt + 1) * 128],
                                 a["ylnb"][:, kt * ST:(kt + 1) * ST],
                                 start=(kt == 0), stop=False)
            cT.inc(nc.tensor.matmul(bank[:, :], a[bias_row][0:1, dt * 128:(dt + 1) * 128],
                                    p.ones_r[0:1, :], start=False, stop=True))
            cur.wait(E.Activation, cT)
            cS.inc(nc.scalar.activation(dst[:, dt * ST:(dt + 1) * ST], bank[:, :],
                                        AF.Copy, scale=0.125))
            rot.done(bi, cS)

    proj_block("Wk", "kbr", a["khat"])
    if not feed_only:
        proj_block("Wq", "qbr", a["qhat"])

    cur.wait(E.PE, cV)
    for tt in range(4):
        for vc in range(2):
            bank, bi = rot.group()
            for kt in range(8):
                nc.tensor.matmul(bank[:, :],
                                 a["ylnb"][:, kt * ST + tt * 128: kt * ST + (tt + 1) * 128],
                                 a["Wv"][:, kt * DM + vc * 512: kt * DM + (vc + 1) * 512],
                                 start=(kt == 0), stop=False)
            cT.inc(nc.tensor.matmul(bank[:, :], p.ones_r[0:1, 0:128],
                                    a["vb_row"][0:1, vc * 512:(vc + 1) * 512],
                                    start=False, stop=True))
            cur.wait(E.Activation, cT)
            cS.inc(nc.scalar.copy(a["vnat"][:, tt * DM + vc * 512: tt * DM + (vc + 1) * 512],
                                  bank[:, :]))
            rot.done(bi, cS)

    for h in range(PH):
        dt, row = h // 2, (h % 2) * 64
        cur.wait(E.Activation, cV)
        cS.inc(nc.scalar.copy(a["ka"][0:64, :], a["khat"][row:row + 64, dt * ST:(dt + 1) * ST]))
        cS.inc(nc.scalar.activation(a["qsq"][0:64, :], a["ka"][0:64, :], AF.Square))
        cur.wait(E.PE, cS)
        cur.wait(E.PE, cV)
        cT.inc(nc.tensor.matmul(ps[6][0:1, :], p.ones_b[0:64, :], a["qsq"][0:64, :],
                                start=True, stop=True))
        cur.wait(E.Activation, cT)
        cS.inc(nc.scalar.activation(a["ka"][64:65, :], ps[6][0:1, :], AF.Copy,
                                    scale=0.5, bias=LN16))
        if not feed_only:
            cS.inc(nc.scalar.copy(a["qa"][0:64, :], a["qhat"][row:row + 64, dt * ST:(dt + 1) * ST]))
            cS.inc(nc.scalar.activation(a["qsq"][0:64, :], a["qa"][0:64, :], AF.Square))
            cur.wait(E.PE, cS)
            cT.inc(nc.tensor.matmul(ps[7][0:1, :], p.ones_b[0:64, :], a["qsq"][0:64, :],
                                    start=True, stop=True))
            cur.wait(E.Activation, cT)
            cS.inc(nc.scalar.activation(a["qa"][64:65, :], ps[7][0:1, :], AF.Copy,
                                        scale=0.5, bias=LN16))

        cur.wait(E.PE, cS)
        for tt in range(4):
            bank, bi = rot.group()
            cT.inc(nc.tensor.matmul(bank[:, 0:PM], a["ka"][0:65, tt * 128:(tt + 1) * 128],
                                    a["projT"][0:65, :], start=True, stop=True))
            cur.wait(E.Activation, cT)
            cS.inc(nc.scalar.activation(a["kpn"][:, tt * PM:(tt + 1) * PM], bank[:, 0:PM], AF.Exp))
            rot.done(bi, cS)
        if not feed_only:
            for mt in range(2):
                bank, bi = rot.group()
                cT.inc(nc.tensor.matmul(bank[:, :], a["projT"][0:65, mt * 128:(mt + 1) * 128],
                                        a["qa"][0:65, :], start=True, stop=True))
                cur.wait(E.Activation, cT)
                cS.inc(nc.scalar.activation(a["qpT"][:, mt * ST:(mt + 1) * ST], bank[:, :], AF.Exp))
                rot.done(bi, cS)
            for mt in range(2):
                bank, bi = rot.group()
                cT.inc(nc.tensor.matmul(bank[:, :], a["projT"][0:65, mt * 128:(mt + 1) * 128],
                                        a["ka"][0:65, :], start=True, stop=True))
                cur.wait(E.Activation, cT)
                cS.inc(nc.scalar.activation(a["kpT"][:, mt * ST:(mt + 1) * ST], bank[:, :], AF.Exp))
                rot.done(bi, cS)

            for jt in range(4):
                bank, bi = rot.group()
                cur.wait(E.PE, cS)
                for mt in range(2):
                    mm = nc.tensor.matmul(bank[:, :],
                                          a["kpT"][:, mt * ST + jt * 128: mt * ST + (jt + 1) * 128],
                                          a["qpT"][:, mt * ST:(mt + 1) * ST],
                                          start=(mt == 0), stop=(mt == 1))
                cT.inc(mm)
                cur.wait(E.DVE, cT)
                cV.inc(nc.vector.tensor_mask(a["Am"][:, jt * ST:(jt + 1) * ST], bank[:, :],
                                             a["alen_s"][:, jt:jt + 1], a["negi_s"][:, :], 0))
                rot.done(bi, cV)

            bank_n, bi_n = rot.group()
            cur.wait(E.PE, cV)
            for jt in range(4):
                nc.tensor.matmul(bank_n[0:64, :], a["vnat"][:, jt * DM + h * 64: jt * DM + (h + 1) * 64],
                                 a["Am"][:, jt * ST:(jt + 1) * ST], start=(jt == 0), stop=False)
            for mt in range(2):
                slot = (2 * h + mt) * 65
                nc.tensor.matmul(bank_n[0:64, :], a["stateb"][:, slot:slot + 64],
                                 a["qpT"][:, mt * ST:(mt + 1) * ST], start=False, stop=(mt == 1))
            for jt in range(4):
                nc.tensor.matmul(bank_n[64:65, :], p.ones_b[:, :],
                                 a["Am"][:, jt * ST:(jt + 1) * ST], start=(jt == 0), stop=False)
            for mt in range(2):
                slot = (2 * h + mt) * 65
                mm = nc.tensor.matmul(bank_n[64:65, :], a["stateb"][:, slot + 64:slot + 65],
                                      a["qpT"][:, mt * ST:(mt + 1) * ST], start=False, stop=(mt == 1))
            cT.inc(mm)

            cur.wait(E.DVE, cT)
            cV.inc(nc.vector.reciprocal(a["r_t2"][:, :], bank_n[64:65, :]))
            cur.wait(E.PE, cV)
            cT.inc(nc.tensor.matmul(ps[5][0:64, :], p.ones_r32[0:1, 0:64], a["r_t2"][:, :],
                                    start=True, stop=True))
            cur.wait(E.Activation, cT)
            cS.inc(nc.scalar.copy(a["dbc"][:, :], ps[5][0:64, :]))
            cur.wait(E.DVE, cS)
            cV.inc(nc.vector.tensor_mul(a["x0bT"][row:row + 64, dt * ST:(dt + 1) * ST],
                                        bank_n[0:64, :], a["dbc"][:, :]))
            rot.done(bi_n, cV)

        bank_u, bi_u = rot.group()
        cur.wait(E.PE, cS)
        first = True
        for c in range(4):
            for mt in range(2):
                nc.tensor.matmul(bank_u[:, mt * 65: mt * 65 + 64],
                                 a["kpn"][:, c * PM + mt * 128: c * PM + (mt + 1) * 128],
                                 a["vnat"][:, c * DM + h * 64: c * DM + (h + 1) * 64],
                                 start=first, stop=False)
                first = False
                mm = nc.tensor.matmul(bank_u[:, mt * 65 + 64: mt * 65 + 65],
                                      a["kpn"][:, c * PM + mt * 128: c * PM + (mt + 1) * 128],
                                      p.ones_b[:, :], start=False,
                                      stop=(c == 3 and mt == 1))
        cT.inc(mm)
        slot0 = 2 * h * 65
        cur.wait(E.DVE, cT)
        cV.inc(nc.vector.tensor_add(a["state"][:, slot0:slot0 + 130],
                                    a["state"][:, slot0:slot0 + 130], bank_u[:, 0:130]))
        cV.inc(nc.vector.tensor_copy(a["stateb"][:, slot0:slot0 + 130],
                                     a["state"][:, slot0:slot0 + 130]))
        rot.done(bi_u, cV)

    if feed_only:
        cur.wait(E.SP, cS)
        cur.wait(E.SP, cT)
        cur.wait(E.SP, cV)
        p.load_x0bT(rcalc(p, E.SP, SEQ, imm=ST, mul=(i, ST)), stg_wait=i + 3)
        return

    for dt in range(8):
        bank, bi = rot.group()
        cur.wait(E.PE, cV)
        for et in range(8):
            nc.tensor.matmul(bank[:, :], a["Wo"][:, et * DM + dt * 128: et * DM + (dt + 1) * 128],
                             a["x0bT"][:, et * ST:(et + 1) * ST], start=(et == 0), stop=False)
        cT.inc(nc.tensor.matmul(bank[:, :], a["ob_row"][0:1, dt * 128:(dt + 1) * 128],
                                p.ones_r[0:1, :], start=False, stop=True))
        cur.wait(E.DVE, cT)
        cV.inc(nc.vector.tensor_add(a["gb1"][:, dt * ST:(dt + 1) * ST], bank[:, :],
                                    a["yln"][:, dt * ST:(dt + 1) * ST]))
        rot.done(bi, cV)

    emit_ln_tail(p, a, i, cS, cV, cT, cur, src="gb1", scratch="yln", dst="gb1",
                 eps=1e-6, affine=True)
    emit_ln_tail(p, a, i, cS, cV, cT, cur, src="gb1", scratch="yln", dst=None,
                 eps=1e-5, affine=False)
    cur.wait(E.DVE, cT)
    for kt in range(8):
        v = nc.vector.tensor_add(a["x0T"][:, kt * ST:(kt + 1) * ST],
                                 a["x0T"][:, kt * ST:(kt + 1) * ST],
                                 a["yln"][:, kt * ST:(kt + 1) * ST])
    cV.inc(v)

    if DEBUG:
        dco = snapo(p, i * 4096, 11 * 4096)
        dcp = snapo(p, i * 4096, 11 * 4096, engines=(E.Pool,))
        cur.wait(E.SP, cV)
        cur.wait(E.SP, cS)
        nc.sync.dma_start(out=p.dbg1[:, ds(dco, 4096)], in_=a["yln"][:, :]).then_inc(p.sdbg_s, 16)
        nc.gpsimd.dma_start(out=p.dbg2[:, ds(dcp, 4096)], in_=a["qhat"][:, :]).then_inc(p.sdbg_g, 16)
        nc.gpsimd.dma_start(out=p.dbg3[:, ds(dcp, 4096)], in_=a["x0bT"][:, :]).then_inc(p.sdbg_g, 16)
        nc.sync.dma_start(out=p.dbg4[:, ds(dco, 4096)], in_=a["gb1"][:, :]).then_inc(p.sdbg_s, 16)

    cur.wait(E.SP, cV)
    cur.wait(E.SP, cS)
    cur.wait(E.SP, cT)
    cur.wait(E.Activation, cV)
    spo = rcalc(p, E.Activation, SPX1_COLS, imm=128, mul=(i, ST))
    cL.inc(nc.scalar.dma_start(out=p.spx_dst[:, :, ds(spo, ST)], in_=p.x0T_src3), 16)
    p.load_x0bT(rcalc(p, E.SP, SEQ, imm=ST, mul=(i, ST), add=(p.c_base,)))
    p.load_loT(rcalc(p, E.Activation, XC_TOK - ST, imm=ST, mul=(i, ST)))


def emit_ln_tail(p, a, i, cS, cV, cT, cur, src, scratch, dst, eps, affine):
    nc = p.nc
    ps = p.ps
    cur.wait(E.Activation, cV)
    cS.inc(nc.scalar.copy(a["sqb"][:, :], a[src][:, :]))
    cur.wait(E.PE, cS)
    cur.wait(E.PE, cV)
    for kt in range(8):
        mm = nc.tensor.matmul(ps[6][0:1, :], p.ones_b[:, :],
                              a["sqb"][:, kt * ST:(kt + 1) * ST],
                              start=(kt == 0), stop=(kt == 7))
    cT.inc(mm)
    cur.wait(E.Activation, cT)
    cS.inc(nc.scalar.activation(a["sqb"][:, :], a["sqb"][:, :], AF.Square))
    cur.wait(E.PE, cS)
    for kt in range(8):
        mm = nc.tensor.matmul(ps[7][0:1, :], p.ones_b[:, :],
                              a["sqb"][:, kt * ST:(kt + 1) * ST],
                              start=(kt == 0), stop=(kt == 7))
    cT.inc(mm)
    cur.wait(E.DVE, cT)
    nc.vector.tensor_scalar_mul(a["r_mu"][:, :], ps[6][0:1, :], 1.0 / DM)
    nc.vector.tensor_scalar_mul(a["r_t1"][:, :], ps[7][0:1, :], 1.0 / DM)
    nc.vector.tensor_mul(a["r_t2"][:, :], a["r_mu"][:, :], a["r_mu"][:, :])
    cV.inc(nc.vector.tensor_sub(a["r_t1"][:, :], a["r_t1"][:, :], a["r_t2"][:, :]))
    cur.wait(E.Activation, cV)
    nc.scalar.activation(a["r_t2"][:, :], a["r_t1"][:, :], AF.Ln, bias=eps)
    cS.inc(nc.scalar.activation(a["r_rstd"][:, :], a["r_t2"][:, :], AF.Exp, scale=-0.5))
    cur.wait(E.PE, cS)
    cur.wait(E.PE, cV)
    cT.inc(nc.tensor.matmul(ps[4][:, :], p.ones_r32[:, :], a["r_mu"][:, :], start=True, stop=True))
    cT.inc(nc.tensor.matmul(ps[5][:, :], p.ones_r32[:, :], a["r_rstd"][:, :], start=True, stop=True))
    cur.wait(E.DVE, cT)
    for kt in range(8):
        nc.vector.tensor_sub(a[scratch][:, kt * ST:(kt + 1) * ST],
                             a[src][:, kt * ST:(kt + 1) * ST], ps[4][:, :])
    for kt in range(8):
        v = nc.vector.tensor_mul(a[scratch][:, kt * ST:(kt + 1) * ST],
                                 a[scratch][:, kt * ST:(kt + 1) * ST], ps[5][:, :])
    cV.inc(v)
    if affine:
        for dt in range(8):
            v = nc.vector.tensor_scalar(a[dst][:, dt * ST:(dt + 1) * ST],
                                        a[scratch][:, dt * ST:(dt + 1) * ST],
                                        a["gam"][:, dt:dt + 1], a["bet"][:, dt:dt + 1],
                                        ALU.mult, ALU.add)
        cV.inc(v)



FA_SPAN = 8 * (STD + 1)


def alloc_fa(p, stk, with_ln, layer):
    nc = p.nc
    a = {}

    def sb(name, shape, dt):
        a[name] = stk.enter_context(nc.sbuf_tensor(f"fa{layer}_" + name, shape, dt))
        return a[name]

    sb("Wqkv", [128, 8 * 1536], BF16)
    sb("Wo", [128, 4 * DM], BF16)
    sb("rotpb", [128, 128], BF16)
    sb("identb", [128, 128], BF16)
    sb("xsp", [128, FA_SPAN], F32)
    sb("xb16", [128, FA_SPAN], BF16)
    sb("sqb", [128, FA_SPAN], BF16)
    if with_ln:
        sb("yln", [128, FA_SPAN], F32)
    sb("hb", [128, 8 * STD], BF16)
    sb("r_mu", [1, STD + 1], F32)
    sb("r_t1", [1, STD + 1], F32)
    sb("r_t2", [1, STD + 1], F32)
    sb("r_rstd", [1, STD + 1], F32)
    sb("sqt", [128, STD], BF16)
    sb("rr", [2, 2 * STD], F32)
    sb("qsb", [128, 4 * STD], BF16)
    sb("tr2", [128, STD], F32)
    sb("tr3", [128, STD], F32)
    sb("tr4", [128, STD], F32)
    sb("rsqs", [128, STD], F32)
    sb("qfb", [128, 4 * STD], BF16)
    sb("kwin", [128, 4 * 768], BF16)
    sb("vwin", [128, 6 * FIN], BF16)
    sb("rt4", [128, 4 * STD], F32)
    sb("simm", [128, 640], F32)
    sb("simx", [128, 640], F32)
    sb("attb", [128, 640], BF16)
    sb("attT", [128, 640], BF16)
    sb("mxc", [128, 1], F32)
    sb("nmx", [128, 1], F32)
    sb("denc", [128, 1], F32)
    sb("rden", [128, 1], F32)
    sb("qmc", [128, 2], F32)
    sb("e1c", [128, 1], F32)
    sb("z0c", [128, 1], F32)
    sb("rp1", [128, 1], F32)
    sb("sclq", [128, 1], F32)
    sb("sclk", [128, 1], F32)
    sb("bd2", [128, 2], BF16)
    sb("sel2", [2, 128], F32)
    sb("oTb", [128, 2 * FIN], BF16)
    sb("xout", [128, 8 * STD], F32)
    return a


def phase_FA(p, layer):
    nc = p.nc
    with_ln = (layer == 0)
    with ExitStack() as stk:
        a = alloc_fa(p, stk, with_ln, layer)
        s_w = nc.semaphore(f"fa{layer}_w").__enter__()
        s_w2 = nc.semaphore(f"fa{layer}_w2").__enter__()
        s_ms = nc.semaphore(f"fa{layer}_ms").__enter__()
        g = nc.gpsimd
        wq_src = p.wqkv[layer:layer + 1, :, :].rearrange("u (kt q) o -> q (u kt) o", q=128)
        g.dma_start(out=a["Wqkv"][:, :].rearrange("q (kt o) -> q kt o", o=1536),
                    in_=wq_src).then_inc(s_w, 16)
        wo_src = p.wo[layer:layer + 1, :, :].rearrange("u (kt q) o -> q (u kt) o", q=128)
        g.dma_start(out=a["Wo"][:, :].rearrange("q (kt o) -> q kt o", o=DM),
                    in_=wo_src).then_inc(s_w, 16)
        g.dma_start(out=a["rotpb"][:, :], in_=p.rotp).then_inc(s_w, 16)
        g.dma_start(out=a["identb"][:, :], in_=p.ident).then_inc(s_w, 16)
        nc.sync.dma_start(out=a["sclq"][:, :],
                          in_=p.ffn_scl[:, 2 * layer:2 * layer + 1], allow_slow_non_contiguous=True).then_inc(s_w2, 16)
        nc.sync.dma_start(out=a["sclk"][:, :],
                          in_=p.ffn_scl[:, 2 * layer + 1:2 * layer + 2], allow_slow_non_contiguous=True).then_inc(s_w2, 16)
        nc.sync.dma_start(out=a["rp1"][:, :], in_=p.rowp1, allow_slow_non_contiguous=True).then_inc(s_w2, 16)
        g.memset(a["bd2"][:, :], 0.0).then_inc(s_ms, 1)
        nc.sync.dma_start(out=a["sel2"][:, :], in_=p.sel2m).then_inc(s_w2, 16)
        g.memset(a["kwin"][:, :], 0.0).then_inc(s_ms, 1)
        g.memset(a["vwin"][:, :], 0.0).then_inc(s_ms, 1)
        g.memset(a["e1c"][:, :], 512.0).then_inc(s_ms, 1)
        g.memset(a["z0c"][:, :], 0.0).then_inc(s_ms, 1)
        g.wait_ge(s_ms, 5)
        g.memset(a["bd2"][0:64, 0:1], 1.0).then_inc(s_ms, 1)
        g.memset(a["bd2"][64:128, 1:2], 1.0).then_inc(s_ms, 1)
        for e in ALLE:
            nc.engines[e].wait_ge(s_w, 4 * 16)
            nc.engines[e].wait_ge(s_w2, 4 * 16)
            nc.engines[e].wait_ge(s_ms, 7)

        emit_fa_loop(p, a, layer, warm=True)
        p.barrier()
        emit_fa_loop(p, a, layer, warm=False)
        p.barrier()



def _spdbg(p, label):
    import os, itertools
    if not os.environ.get("REGDBG"):
        return
    if not hasattr(_spdbg, "c"):
        _spdbg.c = itertools.count()
    nc = p.nc
    got = []
    while True:
        try:
            got.append(nc.sync.alloc_register(f"dbg_{next(_spdbg.c)}"))
        except ValueError:
            break
    for r in got:
        nc.sync.free_register(r)
    print(f"[regdbg] {label}: SP free {len(got)}", flush=True)

def emit_fa_loop(p, a, layer, warm):
    nc = p.nc
    mark = pro_mark(p)
    tag = f"fa{layer}{'w' if warm else 'm'}"
    cL = Ctr(nc, tag + "L", p.deltas)
    cA = Ctr(nc, tag + "A", p.deltas)
    cG = Ctr(nc, tag + "G", p.deltas)
    cP = Ctr(nc, tag + "P", p.deltas)
    cS = Ctr(nc, tag + "S", p.deltas)
    cV = Ctr(nc, tag + "V", p.deltas)
    cT = Ctr(nc, tag + "T", p.deltas)
    ctrs = (cL, cA, cG, cP, cS, cV, cT)

    if layer == 0:
        n = p.cfgv(CFG_N_D1W if warm else CFG_N_D1, ALLE, 64)
        base = p.cfgv(CFG_B_D1W if warm else CFG_B_D1, [E.SP, E.Activation], SEQ)
        src_sp, src_base_cfg = p.sp_x1, CFG_B_C
        dst_sp = p.sp_bm
        qm_in, qm_max = p.qm_d1, 32
    else:
        n = p.cfgv(CFG_N_E1W if warm else CFG_N_E1, ALLE, 64)
        base = p.cfgv(CFG_B_E1W if warm else CFG_B_E1, [E.SP, E.Activation], SEQ)
        src_sp, src_base_cfg = p.sp_l0, CFG_B_D2
        dst_sp = p.sp_l1
        qm_in, qm_max = p.qm_e1, 24
    src_base = p.cfgv(src_base_cfg, [E.SP], SEQ)
    _spdbg(p, tag + " after cfgv")

    xsp3 = a["xsp"][:, :].rearrange("q (dt n) -> q dt n", n=STD + 1)
    src3 = src_sp.rearrange("(dt q) n -> q dt n", q=128)
    dst3 = dst_sp.rearrange("(dt q) n -> q dt n", q=128)
    xout3 = a["xout"][:, :].rearrange("q (dt n) -> q dt n", n=STD)
    sp_cols = {id(p.sp_x1): SPX1_COLS + ST, id(p.sp_bm): 4096 + STD,
               id(p.sp_l0): 4096 + STD, id(p.sp_l1): 3072 + STD}
    src_cols = sp_cols[id(src_sp)]

    xmax = src_cols - (STD + 1)
    smax = sp_cols[id(dst_sp)] - STD
    p._fa_smax = smax

    def load_iter(i_rv):
        if i_rv is None:
            offX = rcalc(p, E.SP, xmax, imm=127, add=(base,), sub=(src_base,))
            offR = rcalc(p, E.Activation, SEQ, add=(base,))
            offQ = 0
        else:
            offX = rcalc(p, E.SP, xmax, imm=127 + STD, mul=(i_rv, STD),
                         add=(base,), sub=(src_base,))
            offR = rcalc(p, E.Activation, SEQ, imm=STD, mul=(i_rv, STD), add=(base,))
            offQ = rcalc(p, E.SP, qm_max - 2, imm=2, mul=(i_rv, 2))
        cL.inc(nc.sync.dma_start(out=xsp3, in_=src3[:, :, ds(offX, STD + 1)],
                                  bounds_check="skip_entire_dma"), 16)
        rt43 = a["rt4"][:, :].rearrange("q (f n) -> q f n", n=STD)
        rall3 = p.rot_all.rearrange("q (f n) -> q f n", f=4)
        cA.inc(nc.scalar.dma_start(out=rt43, in_=rall3[:, :, ds(offR, STD)]), 16)
        if not warm:
            cL.inc(nc.sync.dma_start(out=a["qmc"][:, :], in_=qm_in[:, ds(offQ, 2)]), 16)

    from contextlib import nullcontext
    load_iter(None)
    _spdbg(p, tag + " after prologue")
    guard = nc.If(n > 0) if warm else nullcontext()
    with guard:
        with kfori(p, n) as i:
            _spdbg(p, tag + " in fori")
            for c in ctrs:
                c.begin_body()
            begin_snaps(p)
            cur = Cursors(p, i, ctrs)
            body_start_waits(cur, ctrs, engines=ALLE)
            _spdbg(p, tag + " after cursors")
            emit_fa_body(p, a, layer, warm, i, cL, cA, cG, cP, cS, cV, cT, cur,
                         load_iter, dst3)
            for c in ctrs:
                c.end_body()
            cur.finish()
            end_snaps(p)
            _spdbg(p, tag + " after body frees")
    drain_all(p, ctrs, n)
    pro_free(p, mark)
    _spdbg(p, tag + " after loop")


def emit_fa_body(p, a, layer, warm, i, cL, cA, cG, cP, cS, cV, cT, cur,
                 load_iter, dst3):
    nc = p.nc
    ps = p.ps
    ps_t = p.ps_t
    with_ln = (layer == 0)

    vw = a["vwin"]
    for mt in range(4):
        kw = a["kwin"][:, mt * 768:(mt + 1) * 768]
        cG.inc(nc.gpsimd.dma_start(out=kw[:, 0:256], in_=kw[:, 256:512]), 16)
    cG.inc(nc.gpsimd.dma_start(out=vw[:, 0:1024], in_=vw[:, 1024:2048]), 16)
    cur.wait(E.Pool, cG)
    for mt in range(4):
        kw = a["kwin"][:, mt * 768:(mt + 1) * 768]
        cG.inc(nc.gpsimd.dma_start(out=kw[:, 256:512], in_=kw[:, 512:768]), 16)
    cG.inc(nc.gpsimd.dma_start(out=vw[:, 1024:2048], in_=vw[:, 2048:3072]), 16)

    cur.wait(E.DVE, cL)
    cV.inc(nc.vector.tensor_copy(a["xb16"][:, :], a["xsp"][:, :]))
    if with_ln:
        cur.wait(E.Activation, cV)
        cS.inc(nc.scalar.activation(a["sqb"][:, :], a["xb16"][:, :], AF.Square))
        cur.wait(E.PE, cV)
        for dt in range(8):
            mm = nc.tensor.matmul(ps[6][0:1, 0:STD + 1], p.ones_b[:, :],
                                  a["xb16"][:, dt * 257:(dt + 1) * 257],
                                  start=(dt == 0), stop=(dt == 7))
        cT.inc(mm)
        cur.wait(E.PE, cS)
        for dt in range(8):
            mm = nc.tensor.matmul(ps[7][0:1, 0:STD + 1], p.ones_b[:, :],
                                  a["sqb"][:, dt * 257:(dt + 1) * 257],
                                  start=(dt == 0), stop=(dt == 7))
        cT.inc(mm)
        cur.wait(E.DVE, cT)
        nc.vector.tensor_scalar_mul(a["r_mu"][:, :], ps[6][0:1, 0:STD + 1], 1.0 / DM)
        nc.vector.tensor_scalar_mul(a["r_t1"][:, :], ps[7][0:1, 0:STD + 1], 1.0 / DM)
        nc.vector.tensor_mul(a["r_t2"][:, :], a["r_mu"][:, :], a["r_mu"][:, :])
        cV.inc(nc.vector.tensor_sub(a["r_t1"][:, :], a["r_t1"][:, :], a["r_t2"][:, :]))
        cur.wait(E.Activation, cV)
        nc.scalar.activation(a["r_t2"][:, :], a["r_t1"][:, :], AF.Ln, bias=1e-5)
        cS.inc(nc.scalar.activation(a["r_rstd"][:, :], a["r_t2"][:, :], AF.Exp, scale=-0.5))
        cur.wait(E.PE, cS)
        cT.inc(nc.tensor.matmul(ps[4][:, 0:STD + 1], p.ones_r32[:, :], a["r_mu"][:, :],
                                start=True, stop=True))
        cT.inc(nc.tensor.matmul(ps[5][:, 0:STD + 1], p.ones_r32[:, :], a["r_rstd"][:, :],
                                start=True, stop=True))
        cur.wait(E.DVE, cT)
        for dt in range(8):
            nc.vector.tensor_sub(a["yln"][:, dt * 257:(dt + 1) * 257],
                                 a["xsp"][:, dt * 257:(dt + 1) * 257],
                                 ps[4][:, 0:STD + 1])
        for dt in range(8):
            v = nc.vector.tensor_mul(a["yln"][:, dt * 257:(dt + 1) * 257],
                                     a["yln"][:, dt * 257:(dt + 1) * 257],
                                     ps[5][:, 0:STD + 1])
        cV.inc(v)
        cur.wait(E.Pool, cV)
        for dt in range(8):
            sl = 1 if dt < 4 else 0
            pp = nc.gpsimd.tensor_copy(a["hb"][:, dt * STD:(dt + 1) * STD],
                                       a["yln"][:, dt * 257 + sl: dt * 257 + sl + STD])
        cP.inc(pp)
    else:
        cur.wait(E.Pool, cV)
        for dt in range(8):
            sl = 1 if dt < 4 else 0
            pp = nc.gpsimd.tensor_copy(a["hb"][:, dt * STD:(dt + 1) * STD],
                                       a["xb16"][:, dt * 257 + sl: dt * 257 + sl + STD])
        cP.inc(pp)

    def qk_tiles(kind):
        is_q = (kind == "q")
        wbase = 0 if is_q else FIN
        scl = a["sclq"] if is_q else a["sclk"]
        sfac = 8.0 if is_q else 1.0
        rc = a["rt4"][:, 0 * STD:1 * STD] if is_q else a["rt4"][:, 2 * STD:3 * STD]
        rs = a["rt4"][:, 1 * STD:2 * STD] if is_q else a["rt4"][:, 3 * STD:4 * STD]
        for bb in range(2):
            mts = [2 * bb, 2 * bb + 1]
            for j, mt in enumerate(mts):
                bq = ps[j]
                cur.wait(E.PE, cP)
                cur.wait(E.PE, cS)
                cur.wait(E.PE, cV)
                for kt in range(8):
                    mm = nc.tensor.matmul(bq[:, 0:STD],
                                          a["Wqkv"][:, kt * 1536 + wbase + mt * 128:
                                                    kt * 1536 + wbase + (mt + 1) * 128],
                                          a["hb"][:, kt * STD:(kt + 1) * STD],
                                          start=(kt == 0), stop=(kt == 7))
                cT.inc(mm)
                cur.wait(E.Activation, cT)
                cS.inc(nc.scalar.activation(a["sqt"][:, :], bq[:, 0:STD], AF.Square))
                cur.wait(E.PE, cS)
                cT.inc(nc.tensor.matmul(ps_t[0:2, 2 * 512 + j * STD: 2 * 512 + (j + 1) * STD],
                                        a["bd2"][:, :], a["sqt"][:, :],
                                        start=True, stop=True))
            cur.wait(E.DVE, cT)
            nc.vector.tensor_scalar_max(a["rr"][:, 0:2 * STD],
                                        ps_t[0:2, 2 * 512: 2 * 512 + 2 * STD], 1e-24)
            cV.inc(nc.vector.reciprocal(a["rr"][:, 0:2 * STD], a["rr"][:, 0:2 * STD]))
            cur.wait(E.Activation, cV)
            cS.inc(nc.scalar.activation(a["rr"][:, 0:2 * STD], a["rr"][:, 0:2 * STD],
                                        AF.Sqrt))
            for j, mt in enumerate(mts):
                bq = ps[j]
                cur.wait(E.PE, cS)
                cur.wait(E.PE, cV)
                cT.inc(nc.tensor.matmul(ps[4 + j][:, 0:STD], a["sel2"][:, :],
                                        a["rr"][:, j * STD:(j + 1) * STD],
                                        start=True, stop=True))
                cur.wait(E.Activation, cT)
                cS.inc(nc.scalar.copy(a["rsqs"][:, :], ps[4 + j][:, 0:STD]))
                cur.wait(E.DVE, cS)
                nc.vector.tensor_mul(a["tr4"][:, :], bq[:, 0:STD], a["rsqs"][:, :])
                cV.inc(nc.vector.tensor_scalar(a["qsb"][:, mt * STD:(mt + 1) * STD],
                                               a["tr4"][:, :], scl[:, 0:1], sfac,
                                               ALU.mult, ALU.mult))
                cur.wait(E.PE, cV)
                cT.inc(nc.tensor.matmul(ps[6 + j][:, 0:STD], a["rotpb"][:, :],
                                        a["qsb"][:, mt * STD:(mt + 1) * STD],
                                        start=True, stop=True))
                cur.wait(E.DVE, cT)
                cur.wait(E.DVE, cP)
                cV.inc(nc.vector.tensor_mul(a["tr2"][:, :], ps[6 + j][:, 0:STD],
                                            rs))
                cur.wait(E.Pool, cV)
                cur.wait(E.Pool, cG)
                cP.inc(nc.gpsimd.tensor_mul(a["tr3"][:, :],
                                            a["qsb"][:, mt * STD:(mt + 1) * STD],
                                            rc))
                cur.wait(E.Pool, cP)
                if is_q:
                    dst = a["qfb"][:, mt * STD:(mt + 1) * STD]
                else:
                    dst = a["kwin"][:, mt * 768 + 512: mt * 768 + 768]
                cP.inc(nc.gpsimd.tensor_add(dst, a["tr3"][:, :], a["tr2"][:, :]))

    qk_tiles("k")
    if not warm:
        qk_tiles("q")

    for s in range(2):
        bv = ps[0]
        cur.wait(E.PE, cS)
        cur.wait(E.PE, cV)
        cur.wait(E.PE, cP)
        for kt in range(8):
            mm = nc.tensor.matmul(bv[:, :],
                                  a["hb"][:, kt * STD + s * 128: kt * STD + (s + 1) * 128],
                                  a["Wqkv"][:, kt * 1536 + 1024: kt * 1536 + 1536],
                                  start=(kt == 0), stop=(kt == 7))
        cT.inc(mm)
        cur.wait(E.Activation, cT)
        cur.wait(E.Activation, cG)
        cS.inc(nc.scalar.copy(a["vwin"][:, (4 + s) * FIN:(5 + s) * FIN], bv[:, :]))

    if warm:
        cur.wait(E.SP, cV)
        cur.wait(E.SP, cS)
        cur.wait(E.SP, cT)
        cur.wait(E.SP, cP)
        cur.wait(E.Activation, cP)
        load_iter(i)
        return

    for s in range(2):
        oT = ps[6 + s]
        for h in range(8):
            mt, hrow = h // 2, (h % 2) * 64
            simv = ps_t[:, (h % 2) * 1024: (h % 2) * 1024 + 640]
            sim1 = ps_t[:, (h % 2) * 1024: (h % 2) * 1024 + 512]
            sim2 = ps_t[:, (h % 2) * 1024 + 512: (h % 2) * 1024 + 640]
            lhs_q = a["qfb"][hrow:hrow + 64, mt * STD + s * 128: mt * STD + (s + 1) * 128]
            kw = a["kwin"][hrow:hrow + 64, mt * 768 + s * 128: mt * 768 + s * 128 + 640]
            cur.wait(E.PE, cP)
            cur.wait(E.PE, cG)
            cur.wait(E.PE, cV)
            cur.wait(E.PE, cS)
            cT.inc(nc.tensor.matmul(sim1, lhs_q, kw[:, 0:512], start=True, stop=True))
            cT.inc(nc.tensor.matmul(sim2, lhs_q, kw[:, 512:640], start=True, stop=True))
            cur.wait(E.DVE, cT)
            nc.vector.tensor_mask_reduce(a["simm"][:, 0:512], sim1,
                                         a["qmc"][:, s:s + 1], a["e1c"][:, 0:1],
                                         1.0, -3.0e38, ALU.max, accum_out=a["mxc"][:, :])
            cV.inc(nc.vector.tensor_mask_reduce(a["simm"][:, 512:640], sim2,
                                                a["z0c"][:, 0:1], a["rp1"][:, 0:1],
                                                1.0, a["mxc"][:, 0:1], ALU.max,
                                                accum_out=a["mxc"][:, :]))
            cV.inc(nc.vector.tensor_scalar(a["nmx"][:, :], a["mxc"][:, :], -1.0, 80.0,
                                           ALU.mult, ALU.min))
            cur.wait(E.Activation, cV)
            cS.inc(nc.scalar.activation(a["simx"][:, :], a["simm"][:, :], AF.Exp,
                                        bias=a["nmx"][:, 0:1], accum_out=a["denc"][:, :]))
            cur.wait(E.DVE, cS)
            cV.inc(nc.vector.reciprocal(a["rden"][:, :], a["denc"][:, :]))
            cur.wait(E.Pool, cV)
            cur.wait(E.Pool, cT)
            cP.inc(nc.gpsimd.tensor_scalar(a["attb"][:, :], a["simx"][:, :],
                                           a["rden"][:, 0:1], None, ALU.mult))
            cur.wait(E.PE, cP)
            cur.wait(E.PE, cV)
            for j in range(5):
                tt = nc.tensor.transpose(
                    p.ps_bt[:, 4 * 1024 + j * 128: 4 * 1024 + (j + 1) * 128],
                    a["attb"][:, j * 128:(j + 1) * 128], a["identb"][:, :])
            cT.inc(tt)
            cur.wait(E.DVE, cT)
            cV.inc(nc.vector.tensor_copy(a["attT"][:, :],
                                         p.ps_bt[:, 4 * 1024: 4 * 1024 + 640]))
            cur.wait(E.PE, cV)
            for j in range(5):
                mm = nc.tensor.matmul(
                    oT[hrow:hrow + 64, mt * 128:(mt + 1) * 128],
                    a["vwin"][:, (s + j) * FIN + h * 64: (s + j) * FIN + (h + 1) * 64],
                    a["attT"][:, j * 128:(j + 1) * 128],
                    start=(j == 0), stop=(j == 4))
            cT.inc(mm)
        cur.wait(E.DVE, cT)
        oTb3 = a["oTb"][:, :].rearrange("q (kt two n) -> q kt two n", two=2, n=128)
        cV.inc(nc.vector.tensor_copy(oTb3[:, :, s, :],
                                     oT[:, :].rearrange("q (kt n) -> q kt n", n=128)))

    for dt in range(8):
        bo = ps[dt % 4]
        cur.wait(E.PE, cV)
        for kt in range(4):
            mm = nc.tensor.matmul(bo[:, 0:STD],
                                  a["Wo"][:, kt * DM + dt * 128: kt * DM + (dt + 1) * 128],
                                  a["oTb"][:, kt * STD:(kt + 1) * STD],
                                  start=(kt == 0), stop=(kt == 3))
        cT.inc(mm)
        cur.wait(E.DVE, cT)
        resid = a["yln"] if with_ln else a["xsp"]
        cV.inc(nc.vector.tensor_add(a["xout"][:, dt * STD:(dt + 1) * STD],
                                    bo[:, 0:STD],
                                    resid[:, dt * 257 + 1:dt * 257 + 1 + STD]))

    cur.wait(E.Activation, cV)
    cur.wait(E.Activation, cP)
    spo = rcalc(p, E.Activation, p._fa_smax, imm=128, mul=(i, STD))
    cA.inc(nc.scalar.dma_start(out=dst3[:, :, ds(spo, STD)], in_=xout3_of(p, a)), 16)
    cur.wait(E.SP, cV)
    cur.wait(E.SP, cS)
    cur.wait(E.SP, cT)
    cur.wait(E.SP, cP)
    load_iter(i)


def xout3_of(p, a):
    return a["xout"][:, :].rearrange("q (dt n) -> q dt n", n=STD)



W1_TILES = build_w1_perm()[1]


def alloc_fm(p, stk, final, layer):
    nc = p.nc
    a = {}

    def sb(name, shape, dt):
        a[name] = stk.enter_context(nc.sbuf_tensor(f"fm{layer}_" + name, shape, dt))
        return a[name]

    sb("W1", [128, 8 * 2 * HID], BF16)
    sb("W2", [128, 22 * DM], BF16)
    sb("xsp", [128, FA_SPAN], F32)
    sb("xb16", [128, FA_SPAN], BF16)
    sb("sqb", [128, FA_SPAN], BF16)
    sb("zb", [128, 8 * STD], BF16)
    sb("r_mu", [1, STD], F32)
    sb("r_t1", [1, STD], F32)
    sb("r_t2", [1, STD], F32)
    sb("r_rstd", [1, STD], F32)
    sb("gam", [128, 8], F32)
    sb("bet", [128, 8], F32)
    sb("b1a", [128, 22], F32)
    sb("b1g", [128, 22], F32)
    sb("b2c", [128, 8], F32)
    sb("gel", [128, 2 * STD], F32)
    sb("glub", [128, 22 * STD], BF16)
    sb("xout", [128, 8 * STD], F32)
    if final:
        sb("x1sp", [128, 8 * STD], F32)
        sb("hi", [128, 8 * STD], BF16)
        sb("lo", [128, 8 * STD], BF16)
    return a


def phase_FM(p, layer):
    nc = p.nc
    final = (layer == 1)
    with ExitStack() as stk:
        a = alloc_fm(p, stk, final, layer)
        s_w = nc.semaphore(f"fm{layer}_w").__enter__()
        s_w2 = nc.semaphore(f"fm{layer}_w2").__enter__()
        g = nc.gpsimd
        w1_src = p.w1[layer:layer + 1, :, :].rearrange("u (kt q) o -> q (u kt) o", q=128)
        g.dma_start(out=a["W1"][:, :].rearrange("q (kt o) -> q kt o", o=2 * HID),
                    in_=w1_src).then_inc(s_w, 16)
        w2_src = p.w2[layer:layer + 1, :, :].rearrange("u (kt q) o -> q (u kt) o", q=128)
        g.dma_start(out=a["W2"][:, :].rearrange("q (kt o) -> q kt o", o=DM),
                    in_=w2_src).then_inc(s_w, 16)
        nc.sync.dma_start(out=a["gam"][:, :],
                          in_=p.ffn_ln[:, layer * 16:layer * 16 + 8]).then_inc(s_w2, 16)
        nc.sync.dma_start(out=a["bet"][:, :],
                          in_=p.ffn_ln[:, layer * 16 + 8:layer * 16 + 16]).then_inc(s_w2, 16)
        nc.sync.dma_start(out=a["b1a"][:, :],
                          in_=p.ffn_b1c[:, layer * 44:layer * 44 + 22]).then_inc(s_w2, 16)
        nc.sync.dma_start(out=a["b1g"][:, :],
                          in_=p.ffn_b1c[:, layer * 44 + 22:layer * 44 + 44]).then_inc(s_w2, 16)
        nc.sync.dma_start(out=a["b2c"][:, :],
                          in_=p.ffn_b2c[:, layer * 8:layer * 8 + 8]).then_inc(s_w2, 16)
        for e in ALLE:
            nc.engines[e].wait_ge(s_w, 2 * 16)
            nc.engines[e].wait_ge(s_w2, 5 * 16)
        emit_fm_loop(p, a, layer)
        p.barrier()


def emit_fm_loop(p, a, layer):
    nc = p.nc
    mark = pro_mark(p)
    final = (layer == 1)
    tag = f"fm{layer}"
    cL = Ctr(nc, tag + "L", p.deltas)
    cA = Ctr(nc, tag + "A", p.deltas)
    cS = Ctr(nc, tag + "S", p.deltas)
    cV = Ctr(nc, tag + "V", p.deltas)
    cT = Ctr(nc, tag + "T", p.deltas)
    cP = Ctr(nc, tag + "P", p.deltas)
    ctrs = (cL, cA, cS, cV, cT, cP)

    if layer == 0:
        n = p.cfgv(CFG_N_D2, ALLE, 64)
        base = p.cfgv(CFG_B_D2, [E.SP], SEQ)
        src_sp, src_base_cfg = p.sp_bm, CFG_B_D1
        src_cols = 4096 + STD
    else:
        n = p.cfgv(CFG_N_E2, ALLE, 64)
        base = p.cfgv(CFG_S, [E.SP], SEQ)
        src_sp, src_base_cfg = p.sp_l1, CFG_B_E1
        src_cols = 3072 + STD
    src_base = p.cfgv(src_base_cfg, [E.SP], SEQ)
    xsp3 = a["xsp"][:, :].rearrange("q (dt n) -> q dt n", n=STD + 1)
    src3 = src_sp.rearrange("(dt q) n -> q dt n", q=128)
    if layer == 0:
        dst3 = p.sp_l0.rearrange("(dt q) n -> q dt n", q=128)
    if final:
        x1b_cfg = p.cfgv(CFG_B_C, [E.SP], SEQ)
        x1sp3 = a["x1sp"][:, :].rearrange("q (dt n) -> q dt n", n=STD)
        x1src3 = p.sp_x1.rearrange("(dt q) n -> q dt n", q=128)
        oh3 = p.sp_oh.rearrange("(dt q) n -> q dt n", q=128)
        ol3 = p.sp_ol.rearrange("(dt q) n -> q dt n", q=128)

    xmax = src_cols - (STD + 1)

    def load_iter(i_rv):
        if i_rv is None:
            offX = rcalc(p, E.SP, xmax, imm=127, add=(base,), sub=(src_base,))
        else:
            offX = rcalc(p, E.SP, xmax, imm=127 + STD, mul=(i_rv, STD),
                         add=(base,), sub=(src_base,))
        cL.inc(nc.sync.dma_start(out=xsp3, in_=src3[:, :, ds(offX, STD + 1)],
                                  bounds_check="skip_entire_dma"), 16)
        if final:
            if i_rv is None:
                off1 = rcalc(p, E.SP, SPX1_COLS, imm=128, add=(base,), sub=(x1b_cfg,))
            else:
                off1 = rcalc(p, E.SP, SPX1_COLS, imm=128 + STD, mul=(i_rv, STD),
                             add=(base,), sub=(x1b_cfg,))
            cL.inc(nc.sync.dma_start(out=x1sp3, in_=x1src3[:, :, ds(off1, STD)],
                                      bounds_check="skip_entire_dma"), 16)

    load_iter(None)
    with kfori(p, n) as i:
        for c in ctrs:
            c.begin_body()
        begin_snaps(p)
        cur = Cursors(p, i, ctrs)
        body_start_waits(cur, ctrs, engines=ALLE)
        emit_fm_body(p, a, layer, i, cL, cA, cS, cV, cT, cP, cur, load_iter,
                     dst3 if layer == 0 else None,
                     (oh3, ol3) if final else None)
        for c in ctrs:
            c.end_body()
        cur.finish()
        end_snaps(p)
    drain_all(p, ctrs, n)
    pro_free(p, mark)


def emit_fm_body(p, a, layer, i, cL, cA, cS, cV, cT, cP, cur, load_iter,
                 dst3, ohl):
    nc = p.nc
    ps = p.ps
    final = (layer == 1)

    cur.wait(E.DVE, cL)
    cV.inc(nc.vector.tensor_copy(a["xb16"][:, :], a["xsp"][:, :]))
    cur.wait(E.Activation, cV)
    cS.inc(nc.scalar.activation(a["sqb"][:, :], a["xb16"][:, :], AF.Square))
    cur.wait(E.PE, cV)
    for dt in range(8):
        sl = 1 if dt < 4 else 0
        mm = nc.tensor.matmul(ps[6][0:1, 0:STD], p.ones_b[:, :],
                              a["xb16"][:, dt * 257 + sl: dt * 257 + sl + STD],
                              start=(dt == 0), stop=(dt == 7))
    cT.inc(mm)
    cur.wait(E.PE, cS)
    for dt in range(8):
        sl = 1 if dt < 4 else 0
        mm = nc.tensor.matmul(ps[7][0:1, 0:STD], p.ones_b[:, :],
                              a["sqb"][:, dt * 257 + sl: dt * 257 + sl + STD],
                              start=(dt == 0), stop=(dt == 7))
    cT.inc(mm)
    cur.wait(E.DVE, cT)
    nc.vector.tensor_scalar_mul(a["r_mu"][:, :], ps[6][0:1, 0:STD], 1.0 / DM)
    nc.vector.tensor_scalar_mul(a["r_t1"][:, :], ps[7][0:1, 0:STD], 1.0 / DM)
    nc.vector.tensor_mul(a["r_t2"][:, :], a["r_mu"][:, :], a["r_mu"][:, :])
    cV.inc(nc.vector.tensor_sub(a["r_t1"][:, :], a["r_t1"][:, :], a["r_t2"][:, :]))
    cur.wait(E.Activation, cV)
    nc.scalar.activation(a["r_t2"][:, :], a["r_t1"][:, :], AF.Ln, bias=1e-5)
    cS.inc(nc.scalar.activation(a["r_rstd"][:, :], a["r_t2"][:, :], AF.Exp, scale=-0.5))
    cur.wait(E.PE, cS)
    cT.inc(nc.tensor.matmul(ps[4][:, 0:STD], p.ones_r32[:, :], a["r_mu"][:, :],
                            start=True, stop=True))
    cT.inc(nc.tensor.matmul(ps[5][:, 0:STD], p.ones_r32[:, :], a["r_rstd"][:, :],
                            start=True, stop=True))
    cur.wait(E.DVE, cT)
    for dt in range(8):
        sl = 1 if dt < 4 else 0
        g0 = a["gel"][:, 0:STD]
        nc.vector.tensor_sub(g0,
                             a["xsp"][:, dt * 257 + sl: dt * 257 + sl + STD],
                             ps[4][:, 0:STD])
        nc.vector.tensor_mul(g0, g0, ps[5][:, 0:STD])
        v = nc.vector.tensor_scalar(a["zb"][:, dt * STD:(dt + 1) * STD],
                                    g0, a["gam"][:, dt:dt + 1],
                                    a["bet"][:, dt:dt + 1], ALU.mult, ALU.add)
    cV.inc(v)

    rotW = MMRot(p, [0, 1, 2, 3], cur)
    gluK = {}
    for pi, (a_off, g_off, w) in enumerate(W1_TILES):
        bankap, bi = rotW.group()
        bank = ps[bi]
        if pi == 0:
            cur.wait(E.PE, cV)
            cur.wait(E.PE, cS)
        for kt in range(8):
            nc.tensor.matmul(bank[0:w, 0:STD],
                             a["W1"][:, kt * 2 * HID + a_off: kt * 2 * HID + a_off + w],
                             a["zb"][:, kt * STD:(kt + 1) * STD],
                             start=(kt == 0), stop=False)
        for kt in range(8):
            mm = nc.tensor.matmul(bank[0:w, 256:256 + STD],
                                  a["W1"][:, kt * 2 * HID + g_off: kt * 2 * HID + g_off + w],
                                  a["zb"][:, kt * STD:(kt + 1) * STD],
                                  start=False, stop=(kt == 7))
        cT.inc(mm)
        gph = a["gel"][0:w, (pi % 2) * STD:(pi % 2) * STD + STD]
        if pi >= 2:
            cur.wait(E.Activation, cV, K=gluK[pi - 2])
        cur.wait(E.Activation, cT)
        cS.inc(nc.scalar.activation(gph, bank[0:w, 256:256 + STD],
                                    AF.Gelu, bias=a["b1g"][0:w, pi:pi + 1]))
        cur.wait(E.DVE, cS)
        cV.inc(nc.vector.scalar_tensor_tensor(
            a["glub"][0:w, pi * STD:(pi + 1) * STD],
            bank[0:w, 0:STD], a["b1a"][0:w, pi:pi + 1],
            gph, ALU.add, ALU.mult))
        gluK[pi] = cV.static + cV.cur
        rotW.done(bi, cV)

    rot2 = MMRot(p, [4, 5, 6, 7], cur)
    for dt in range(8):
        bankap, bi2 = rot2.group()
        bank = ps[bi2]
        if dt == 0:
            cur.wait(E.PE, cV)
        for kt in range(22):
            kk = 128 if kt < 21 else HID - 21 * 128
            mm = nc.tensor.matmul(bank[:, 0:STD],
                                  a["W2"][0:kk, kt * DM + dt * 128: kt * DM + (dt + 1) * 128],
                                  a["glub"][0:kk, kt * STD:(kt + 1) * STD],
                                  start=(kt == 0), stop=(kt == 21))
        cT.inc(mm)
        cur.wait(E.DVE, cT)
        cV.inc(nc.vector.scalar_tensor_tensor(
            a["xout"][:, dt * STD:(dt + 1) * STD], bank[:, 0:STD],
            a["b2c"][:, dt:dt + 1],
            a["xsp"][:, dt * 257 + 1: dt * 257 + 1 + STD], ALU.add, ALU.add))
        rot2.done(bi2, cV)

    if not final:
        cur.wait(E.Activation, cV)
        spo = rcalc(p, E.Activation, 4096, imm=128, mul=(i, STD))
        cA.inc(nc.scalar.dma_start(out=dst3[:, :, ds(spo, STD)],
                                   in_=xout3_of(p, a)), 16)
        cur.wait(E.SP, cV)
        cur.wait(E.SP, cS)
        cur.wait(E.SP, cT)
        load_iter(i)
        return

    cur.wait(E.DVE, cV)
    cur.wait(E.DVE, cT)
    cV.inc(nc.vector.tensor_copy(a["xb16"][:, 0:8 * STD], a["xout"][:, :]))
    cur.wait(E.Activation, cV)
    cS.inc(nc.scalar.activation(a["sqb"][:, 0:8 * STD], a["xb16"][:, 0:8 * STD],
                                AF.Square))
    cur.wait(E.PE, cV)
    for dt in range(8):
        mm = nc.tensor.matmul(ps[2][0:1, 0:STD], p.ones_b[:, :],
                              a["xb16"][:, dt * STD:(dt + 1) * STD],
                              start=(dt == 0), stop=(dt == 7))
    cT.inc(mm)
    cur.wait(E.PE, cS)
    for dt in range(8):
        mm = nc.tensor.matmul(ps[3][0:1, 0:STD], p.ones_b[:, :],
                              a["sqb"][:, dt * STD:(dt + 1) * STD],
                              start=(dt == 0), stop=(dt == 7))
    cT.inc(mm)
    cur.wait(E.DVE, cT)
    nc.vector.tensor_scalar_mul(a["r_mu"][:, :], ps[2][0:1, 0:STD], 1.0 / DM)
    nc.vector.tensor_scalar_mul(a["r_t1"][:, :], ps[3][0:1, 0:STD], 1.0 / DM)
    nc.vector.tensor_mul(a["r_t2"][:, :], a["r_mu"][:, :], a["r_mu"][:, :])
    cV.inc(nc.vector.tensor_sub(a["r_t1"][:, :], a["r_t1"][:, :], a["r_t2"][:, :]))
    cur.wait(E.Activation, cV)
    nc.scalar.activation(a["r_t2"][:, :], a["r_t1"][:, :], AF.Ln, bias=1e-5)
    cS.inc(nc.scalar.activation(a["r_rstd"][:, :], a["r_t2"][:, :], AF.Exp, scale=-0.5))
    cur.wait(E.PE, cS)
    cT.inc(nc.tensor.matmul(ps[0][:, 0:STD], p.ones_r32[:, :], a["r_mu"][:, :],
                            start=True, stop=True))
    cT.inc(nc.tensor.matmul(ps[1][:, 0:STD], p.ones_r32[:, :], a["r_rstd"][:, :],
                            start=True, stop=True))
    cur.wait(E.DVE, cT)
    for dt in range(8):
        d = a["xout"][:, dt * STD:(dt + 1) * STD]
        nc.vector.tensor_sub(d, d, ps[0][:, 0:STD])
        nc.vector.tensor_mul(d, d, ps[1][:, 0:STD])
        v = nc.vector.tensor_add(d, d, a["x1sp"][:, dt * STD:(dt + 1) * STD])
    cV.inc(v)
    nc.vector.tensor_copy(a["hi"][:, :], a["xout"][:, :])
    nc.vector.tensor_sub(a["xout"][:, :], a["xout"][:, :], a["hi"][:, :])
    cV.inc(nc.vector.tensor_copy(a["lo"][:, :], a["xout"][:, :]))
    cur.wait(E.SP, cV)
    oh3, ol3 = ohl
    oc = rcalc(p, E.SP, OUT_ROWS - STD, mul=(i, STD))
    cL.inc(nc.sync.dma_start(
        out=oh3[:, :, ds(oc, STD)],
        in_=a["hi"][:, :].rearrange("q (dt n) -> q dt n", n=STD)), 16)
    cL.inc(nc.sync.dma_start(
        out=ol3[:, :, ds(oc, STD)],
        in_=a["lo"][:, :].rearrange("q (dt n) -> q dt n", n=STD)), 16)
    cur.wait(E.SP, cS)
    cur.wait(E.SP, cT)
    load_iter(i)



def phase_F(p):
    nc = p.nc
    with (
        nc.sbuf_tensor("f_th", [128, DM], BF16) as fth,
        nc.sbuf_tensor("f_tl", [128, DM], BF16) as ftl,
        nc.sbuf_tensor("f_o", [128, DM], F32) as fo,
    ):
        cL2 = Ctr(nc, "f2L", p.deltas)
        cV2 = Ctr(nc, "f2V", p.deltas)
        nT128b = p.cfgv(CFG_T128, ALLE, SEQ // 128)

        with kfori(p, nT128b) as i:
            for c in (cL2, cV2):
                c.begin_body()
            cur2 = Cursors(p, i, (cL2, cV2))
            begin_snaps(p)
            body_start_waits(cur2, (cL2, cV2), engines=[E.Activation, E.DVE])
            oc2 = rcalc(p, E.Activation, OUT_ROWS - 128, mul=(i, 128))
            cL2.inc(nc.scalar.dma_start(out=fth[:, :], in_=p.sp_oh[:, ds(oc2, 128)],
                                      transpose=True), 16)
            cL2.inc(nc.scalar.dma_start(out=ftl[:, :], in_=p.sp_ol[:, ds(oc2, 128)],
                                      transpose=True), 16)
            cur2.wait(E.DVE, cL2)
            cV2.inc(nc.vector.tensor_add(fo[:, :], fth[:, :], ftl[:, :]))
            cur2.wait(E.Activation, cV2)
            cL2.inc(nc.scalar.dma_start(out=p.out[ds(oc2, 128), :], in_=fo[:, :]), 16)
            for c in (cL2, cV2):
                c.end_body()
            cur2.finish()
            end_snaps(p)
        nf = p.cfgv(CFG_T128, [E.Activation], SEQ // 128)
        nc.scalar.wait_ge(cL2.h, nf * cL2.delta + cL2.static)



def make_in_maps(inputs):
    x0 = np.ascontiguousarray(np.asarray(inputs["L0_attn_x"], np.float32).reshape(SEQ, DM))
    x0 = np.concatenate([x0, np.zeros((XC_TOK - ST, DM), np.float32)], 0)
    plans = plan_shards()
    rqc, rqs, rkc, rks = build_rotary_tables()
    rot_all = np.ascontiguousarray(np.concatenate([rqc, rqs, rkc, rks], axis=1))
    rotp = build_rotp()
    w1perm, _w1tiles = build_w1_perm()
    ident = np.eye(128, dtype=np.float32)
    projT_aug = np.concatenate([
        np.ascontiguousarray(np.asarray(inputs["L0_attn_proj"], np.float32).T),
        -np.ones((1, PM), np.float32)], 0)
    negi = np.broadcast_to(-np.arange(ST, dtype=np.float32)[None, :], (128, ST)).copy()
    alen = np.zeros((128, 4), np.float32)
    for j in range(4):
        alen[:, j] = 1.0 - (j * 128 + np.arange(128))
    rowp1 = (np.arange(128, dtype=np.float32) + 1.0).reshape(128, 1)
    sel2m = np.zeros((2, 128), np.float32)
    sel2m[0, 0:64] = 1.0
    sel2m[1, 64:128] = 1.0
    qsc = np.asarray(inputs["L0_ffn_q_scale"], np.float32)
    ksc = np.asarray(inputs["L0_ffn_k_scale"], np.float32)
    ffn_scl = np.zeros((128, 4), np.float32)
    for l in range(2):
        ffn_scl[:, 2 * l + 0] = np.concatenate([qsc[l], qsc[l]])
        ffn_scl[:, 2 * l + 1] = np.concatenate([ksc[l], ksc[l]])
    lg = np.asarray(inputs["L0_ffn_ln_g"], np.float32)
    lb = np.asarray(inputs["L0_ffn_ln_b"], np.float32)
    ffn_ln = np.zeros((128, 32), np.float32)
    for l in range(2):
        ffn_ln[:, l * 16:l * 16 + 8] = col_layout(lg[l])
        ffn_ln[:, l * 16 + 8:l * 16 + 16] = col_layout(lb[l])
    w1p = np.ascontiguousarray(np.asarray(inputs["L0_ffn_w1"], np.float32)[:, :, w1perm])
    b1p = np.ascontiguousarray(np.asarray(inputs["L0_ffn_b1"], np.float32)[:, w1perm])
    w2_ = np.asarray(inputs["L0_ffn_w2"], np.float32)
    w2p = np.zeros((2, HID_PAD, DM), np.float32)
    w2p[:, :HID, :] = w2_
    b2_ = np.asarray(inputs["L0_ffn_b2"], np.float32)
    ffn_b1c = np.zeros((128, 88), np.float32)
    for l in range(2):
        for pi, (a_off, g_off, w) in enumerate(_w1tiles):
            ffn_b1c[:w, l * 44 + pi] = b1p[l, a_off:a_off + w]
            ffn_b1c[:w, l * 44 + 22 + pi] = b1p[l, g_off:g_off + w]
    ffn_b2c = np.zeros((128, 16), np.float32)
    for l in range(2):
        ffn_b2c[:, l * 8:l * 8 + 8] = col_layout(b2_[l])

    def qm_table(plan, base_key, ncols):
        t = np.zeros((128, ncols), np.float32)
        r = np.arange(128, dtype=np.float32)
        B = plan[base_key]
        for j in range(ncols):
            c0s = B + j * 128
            t[:, j] = np.maximum(r, 512.0 - c0s)
        return t

    in_maps = []
    for c, pl in enumerate(plans):
        cfg = np.zeros((1, 32), np.uint32)
        vals = {CFG_B_C: pl["B_C"], CFG_N_LO: pl["n_lo"], CFG_N_FEED: pl["n_feed"],
                CFG_N_C: pl["n_c"], CFG_S: pl["S"], CFG_B_D1: pl["B_D1"],
                CFG_N_D1W: pl["n_d1w"], CFG_N_D1: pl["n_d1"], CFG_B_D2: pl["B_D2"],
                CFG_N_D2: pl["n_d2"], CFG_B_E1: pl["B_E1"], CFG_N_E1W: pl["n_e1w"],
                CFG_N_E1: pl["n_e1"], CFG_N_E2: pl["n_e2"], CFG_B_D1W: pl["B_D1W"],
                CFG_B_E1W: pl["B_E1W"], CFG_T128: pl["T"] // 128,
                CFG_OFFX1: pl["S"] - pl["B_C"]}
        for k, v in vals.items():
            cfg[0, k] = v
        m = dict(
            L0_attn_x=x0,
            L0_attn_q_w=np.asarray(inputs["L0_attn_q_w"], np.float32),
            L0_attn_q_b=np.asarray(inputs["L0_attn_q_b"], np.float32).reshape(1, DM),
            L0_attn_k_w=np.asarray(inputs["L0_attn_k_w"], np.float32),
            L0_attn_k_b=np.asarray(inputs["L0_attn_k_b"], np.float32).reshape(1, DM),
            L0_attn_v_w=np.asarray(inputs["L0_attn_v_w"], np.float32),
            L0_attn_v_b=np.asarray(inputs["L0_attn_v_b"], np.float32).reshape(1, DM),
            L0_attn_out_w=np.asarray(inputs["L0_attn_out_w"], np.float32),
            L0_attn_out_b=np.asarray(inputs["L0_attn_out_b"], np.float32).reshape(1, DM),
            projT_aug=projT_aug,
            L0_attn_gamma=col_layout(inputs["L0_attn_gamma"]),
            L0_attn_beta=col_layout(inputs["L0_attn_beta"]),
            L0_ffn_wqkv=np.asarray(inputs["L0_ffn_wqkv"], np.float32),
            L0_ffn_wo=np.asarray(inputs["L0_ffn_wo"], np.float32),
            L0_ffn_w1=w1p, L0_ffn_b1=b1p, L0_ffn_w2=w2p,
            L0_ffn_b2=np.asarray(inputs["L0_ffn_b2"], np.float32),
            cfg=cfg, negi=negi, alen=alen,
            rot_all=rot_all, rotp=rotp, ident=ident,
            rowp1=rowp1, ffn_scl=ffn_scl, ffn_ln=ffn_ln, sel2m=sel2m,
            ffn_b1c=ffn_b1c, ffn_b2c=ffn_b2c,
            qm_d1=qm_table(pl, "B_D1", 32), qm_e1=qm_table(pl, "B_E1", 24),
        )
        in_maps.append(m)
    return in_maps, plans


def kernel(**inputs):
    in_maps, plans = make_in_maps(inputs)
    deltas = {}
    build_program(deltas)
    pr = build_program(deltas)
    res = run_bass_kernel_spmd(pr.nc, in_maps, core_ids=list(range(NCORES)))
    out = np.empty((SEQ, DM), np.float32)
    for c, pl in enumerate(plans):
        out[pl["S"]:pl["E"], :] = res.results[c]["out"][: pl["T"], :]
    return out.reshape(1, SEQ, DM)


# version 13 (final)
# speedup vs baseline: 1.2493x; 1.1959x over previous
import numpy as np
import ml_dtypes
from contextlib import ExitStack
import dataclasses
import concourse.bass as bass
import concourse.mybir as mybir
from concourse.bass_utils import run_bass_kernel_spmd
from concourse.expressions import RuntimeValue

F32 = mybir.dt.float32
BF16 = mybir.dt.bfloat16
U32 = mybir.dt.uint32
AF = mybir.ActivationFunctionType
ALU = mybir.AluOpType
E = mybir.EngineType
ds = bass.ds

NCORES = 8
SEQ = 16384
DM = 1024
PH = 16
PD = 64
PM = 256
FH = 8
FD = 64
FIN = 512
HID = 2730
WIN = 512
LN16 = float(np.log(16.0))

ST = 512
STD = 256

HALO_C = 2560
HALO_D1 = 1536
WU_D1 = 768
HALO_D2 = 1280
HALO_E1 = 256
WU_E1 = 768

OUT_ROWS = 4096
SPX1_COLS = 5120
DEBUG = False
XC_TOK = 6656
N_STG = SEQ // ST

HID_PAD = 2816
ALLE = [E.SP, E.Activation, E.DVE, E.PE, E.Pool]
COMPE = [E.SP, E.Activation, E.DVE, E.PE]

(CFG_B_C, CFG_N_LO, CFG_N_FEED, CFG_N_C, CFG_S, CFG_B_D1, CFG_N_D1W, CFG_N_D1,
 CFG_B_D2, CFG_N_D2, CFG_B_E1, CFG_N_E1W, CFG_N_E1, CFG_N_E2, CFG_B_D1W,
 CFG_B_E1W, CFG_T128, CFG_OFFX1) = range(18)



def plan_shards(Ts=None):
    if Ts is None:
        Ts = [SEQ // NCORES] * NCORES
    assert sum(Ts) == SEQ and all(t % ST == 0 for t in Ts)
    plans = []
    s = 0
    for c in range(NCORES):
        S, Eend = s, s + Ts[c]
        p = dict(S=S, E=Eend, T=Ts[c])
        p["B_C"] = max(0, S - HALO_C)
        p["B_D1"] = max(0, S - HALO_D1)
        p["B_D1W"] = max(0, S - HALO_D1 - WU_D1)
        p["B_D2"] = max(0, S - HALO_D2)
        p["B_E1"] = max(0, S - HALO_E1)
        p["B_E1W"] = max(0, S - HALO_E1 - WU_E1)
        p["n_feed"] = p["B_C"] // ST
        p["n_c"] = (Eend - p["B_C"]) // ST
        p["n_lo"] = (Eend - p["B_C"]) // 128
        p["n_d1w"] = (p["B_D1"] - p["B_D1W"]) // STD
        p["n_d1"] = (Eend - p["B_D1"]) // STD
        p["n_d2"] = (Eend - p["B_D2"]) // STD
        p["n_e1w"] = (p["B_E1"] - p["B_E1W"]) // STD
        p["n_e1"] = (Eend - p["B_E1"]) // STD
        p["n_e2"] = Ts[c] // STD
        plans.append(p)
        s = Eend
    return plans


def build_rotary_tables():
    n = SEQ
    half = np.arange(0, FD, 2, dtype=np.float64)
    inv_freq = 1.0 / 10000.0 ** (half / FD)
    t = np.arange(n, dtype=np.float64)
    freqs = t[:, None] * inv_freq[None, :]
    freqs = np.concatenate([freqs, freqs], axis=-1)
    base = (half + 0.4 * FD) / (1.4 * FD)
    power = (t - n // 2) / float(WIN // 2)
    scale = base[None, :] ** power[:, None]
    scale = np.concatenate([scale, scale], axis=-1)
    cos, sin = np.cos(freqs), np.sin(freqs)
    def two(x):
        xT = x.T.astype(np.float32)
        xT = np.concatenate([xT, xT], 0)
        return np.concatenate([xT, np.zeros((128, STD), np.float32)], 1)
    return (two(cos * scale), two(sin * scale),
            two(cos / scale), two(sin / scale))


def build_rotp():
    P = np.zeros((128, 128), np.float32)
    for hb in (0, 64):
        for d in range(32):
            P[hb + d + 32, hb + d] = -1.0
            P[hb + d, hb + d + 32] = 1.0
    return P


def build_w1_perm():
    perm = []
    tiles = []
    off = 0
    i = 0
    while i * 128 < HID:
        w = min(128, HID - i * 128)
        a_cols = list(range(i * 128, i * 128 + w))
        g_cols = list(range(HID + i * 128, HID + i * 128 + w))
        perm.extend(a_cols)
        perm.extend(g_cols)
        tiles.append((off, off + w, w))
        off += 2 * w
        i += 1
    return np.array(perm, np.int64), tiles


def col_layout(vec, dm=DM):
    return np.ascontiguousarray(np.asarray(vec, np.float32).reshape(dm // 128, 128).T)



class Ctr:

    def __init__(self, nc, name, deltas):
        self.h = nc.semaphore(name).__enter__()
        self.nc = nc
        self.name = name
        self.deltas = deltas
        self.static = 0
        self.cur = 0
        self.in_loop = False

    def inc(self, instr, amt=1):
        instr.then_inc(self.h, amt)
        if self.in_loop:
            self.cur += amt
        else:
            self.static += amt
        return instr

    @property
    def delta(self):
        return self.deltas.get(self.name, 1 << 18)

    def at(self, i):
        return i * self.delta + (self.static + self.cur)

    def start_of(self, i):
        return i * self.delta + self.static

    def begin_body(self):
        self.in_loop = True
        self.cur = 0

    def end_body(self):
        self.in_loop = False
        d = self.deltas.get(self.name)
        if d is None:
            self.deltas[self.name] = self.cur
        else:
            assert d == self.cur, f"{self.name}: delta {d} -> {self.cur}"
        self.cur = 0


class MMRot:

    def __init__(self, p, banks, cur):
        self.p = p
        self.banks = banks
        self.cur = cur
        self.idx = 0
        self.guard = {b: None for b in banks}

    def group(self):
        b = self.banks[self.idx % len(self.banks)]
        self.idx += 1
        g = self.guard[b]
        if g is not None:
            ctr, K = g
            self.cur.wait(E.PE, ctr, K)
        return self.p.ps[b], b

    def done(self, b, ctr):
        self.guard[b] = (ctr, ctr.static + ctr.cur)


class DmaOff:

    def __init__(self, p, eng, maxv, name=None):
        self.p, self.eng, self.maxv = p, eng, maxv
        self.reg = p.nc.engines[eng].alloc_register(name)

    def init_to(self, expr, const=0):
        self.p.nc.engines[self.eng].reg_alu(self.reg, expr, const, ALU.add)

    def step(self, amt):
        self.p.nc.engines[self.eng].reg_alu(self.reg, self.reg, amt, ALU.add)

    def val(self):
        return RuntimeValue(val=self.reg, min_val=0, max_val=self.maxv)

    def ap(self, static_ap):
        return dataclasses.replace(static_ap, offset=self.val())

    def free(self):
        self.p.nc.engines[self.eng].free_register(self.reg)


class Cursors:

    def __init__(self, p, i_val, ctrs):
        self.p = p
        self.i = i_val
        self.state = {}

    def wait(self, eng, ctr, K=None):
        nc = self.p.nc
        if K is None:
            K = ctr.static + ctr.cur
        key = (eng, ctr.name)
        st = self.state.get(key)
        e = nc.engines[eng]
        if st is None:
            reg = e.alloc_register(None)
            st = [reg, None]
            self.state[key] = st
        reg, maxK = st
        if maxK is not None and K <= maxK:
            return
        if maxK is None:
            ih = self.i.val[eng]
            e.reg_alu(reg, ih, ctr.delta, ALU.mult)
            if K:
                e.reg_alu(reg, reg, K, ALU.add)
        else:
            e.reg_alu(reg, reg, K - maxK, ALU.add)
        e.wait_ge(ctr.h, reg)
        st[1] = K

    def finish(self):
        for (eng, _), (reg, _m) in self.state.items():
            self.p.nc.engines[eng].free_register(reg)


class P:
    pass



class Stream:

    def __init__(self, p, eng, maxv):
        self.p, self.eng, self.maxv = p, eng, maxv
        self.reg = p.nc.engines[eng].alloc_register(None)

    def mk(self):
        return RuntimeValue(val=self.reg, min_val=0, max_val=self.maxv)

    def setv(self, imm=0, add=(), sub=()):
        e = self.p.nc.engines[self.eng]

        def h(x):
            if isinstance(x, RuntimeValue):
                try:
                    return x.val[self.eng]
                except TypeError:
                    return x.val
            return x

        init = False
        for x in add:
            if init:
                e.reg_alu(self.reg, self.reg, h(x), ALU.add)
            else:
                e.reg_alu(self.reg, h(x), 0, ALU.add)
                init = True
        for x in sub:
            assert init
            e.reg_alu(self.reg, self.reg, h(x), ALU.subtract)
        if init:
            if imm:
                e.reg_alu(self.reg, self.reg, imm, ALU.add)
        else:
            e.reg_alu(self.reg, self.p.zero_regs[self.eng], imm, ALU.add)

    def bump(self, step):
        self.p.nc.engines[self.eng].reg_alu(self.reg, self.reg, step, ALU.add)


def stream(p, eng, maxv):
    key = (eng, maxv)
    st = p.streams.get(key)
    if st is None:
        st = Stream(p, eng, maxv)
        p.streams[key] = st
    return st


def rcalc(p, eng, maxv, imm=0, mul=None, add=(), sub=()):
    nc = p.nc
    e = nc.engines[eng]
    r = e.alloc_register(None)

    def h(x):
        if isinstance(x, RuntimeValue):
            try:
                return x.val[eng]
            except TypeError:
                return x.val
        return x

    init = False
    if mul is not None:
        e.reg_alu(r, h(mul[0]), mul[1], ALU.mult)
        init = True
    for x in add:
        if init:
            e.reg_alu(r, r, h(x), ALU.add)
        else:
            e.reg_alu(r, h(x), 0, ALU.add)
            init = True
    assert init
    for x in sub:
        e.reg_alu(r, r, h(x), ALU.subtract)
    if imm:
        e.reg_alu(r, r, imm, ALU.add)
    lst = getattr(p, "_snap_regs", None)
    if lst is not None:
        lst.append(r)
    else:
        p._pro_regs.append(r)
    return RuntimeValue(val=r, min_val=0, max_val=maxv)


def snapo(p, expr, maxv, engines=(E.SP,)):
    v = p.nc.snap(expr, engines=list(engines), min_val=0, max_val=maxv)
    try:
        regs = v.val
        lst = getattr(p, "_snap_regs", None)
        if lst is not None:
            for eng in engines:
                lst.append(regs[eng])
    except Exception:
        pass
    return v


def begin_snaps(p):
    p._snap_regs = []


def end_snaps(p):
    for r in p._snap_regs:
        p.nc.engines[r.engine].free_register(r)
    p._snap_regs = None


from contextlib import contextmanager


@contextmanager
def kfori(p, n, engines=ALLE):
    nc = p.nc
    loop_id = nc.next_id()
    loop_name = f"kfori_{loop_id}"
    loop_start = f"{loop_name}_loop"
    loop_end = f"{loop_name}_end"
    registers = nc.alloc_registers(f"{loop_name}_i", engines=engines)
    nc.regs_mov(registers, 0)
    nc.br(loop_start, engines=engines)
    max_val = getattr(n, "max_val", None)
    if max_val is not None:
        max_val -= 1
    snap_rv = None
    with nc.body(loop_start, valid_engines=engines):
        snap_rv = nc.snap(registers, min_val=0, max_val=max_val)
        yield snap_rv
        nc.regs_alu(registers, registers, 1, op=ALU.add)
        nc.br_lt(registers, n, on_true=loop_start, on_false=loop_end,
                 engines=engines)
    nc.switch_bb(loop_end)
    for e in engines:
        nc.engines[e].free_register(registers[e])
        try:
            nc.engines[e].free_register(snap_rv.val[e])
        except Exception:
            pass


def pro_mark(p):
    return len(p._pro_regs)


def pro_free(p, mark):
    for r in p._pro_regs[mark:]:
        p.nc.engines[r.engine].free_register(r)
    del p._pro_regs[mark:]

def drain_all(p, ctrs, n, engines=ALLE):
    nc = p.nc
    for e in engines:
        eng = nc.engines[e]
        r = eng.alloc_register(None)
        for c in ctrs:
            eng.reg_alu(r, n.val[e], c.delta, ALU.mult)
            if c.static:
                eng.reg_alu(r, r, c.static, ALU.add)
            eng.wait_ge(c.h, r)
        eng.free_register(r)


def body_start_waits(cur, ctrs, engines=COMPE):
    for e in engines:
        for c in ctrs:
            cur.wait(e, c, K=c.static)



def build_program(deltas):
    p = P()
    nc = bass.Bass("TRN2", target_bir_lowering=False)
    p.nc = nc
    p.deltas = deltas

    def din(name, shape, dt=F32):
        return nc.dram_tensor(name, shape, dt, kind="ExternalInput").ap()

    p.x0 = din("L0_attn_x", [SEQ + XC_TOK - ST, DM])
    p.q_w = din("L0_attn_q_w", [DM, DM])
    p.q_b = din("L0_attn_q_b", [1, DM])
    p.k_w = din("L0_attn_k_w", [DM, DM])
    p.k_b = din("L0_attn_k_b", [1, DM])
    p.v_w = din("L0_attn_v_w", [DM, DM])
    p.v_b = din("L0_attn_v_b", [1, DM])
    p.out_w = din("L0_attn_out_w", [DM, DM])
    p.out_b = din("L0_attn_out_b", [1, DM])
    p.projT_aug = din("projT_aug", [65, PM])
    p.gamma = din("L0_attn_gamma", [128, 8])
    p.beta = din("L0_attn_beta", [128, 8])
    p.wqkv = din("L0_ffn_wqkv", [2, DM, 3 * FIN])
    p.wo = din("L0_ffn_wo", [2, FIN, DM])
    p.w1 = din("L0_ffn_w1", [2, DM, 2 * HID])
    p.b1 = din("L0_ffn_b1", [2, 2 * HID])
    p.w2 = din("L0_ffn_w2", [2, HID_PAD, DM])
    p.b2 = din("L0_ffn_b2", [2, DM])

    p.cfg = din("cfg", [1, 32], U32)
    p.negi = din("negi", [128, ST])
    p.alen = din("alen", [128, 4])
    p.rot_all = din("rot_all", [128, 4 * (SEQ + STD)])
    p.rotp = din("rotp", [128, 128])
    p.ident = din("ident", [128, 128])
    p.rowp1 = din("rowp1", [128, 1])
    p.ffn_scl = din("ffn_scl", [128, 4])
    p.ffn_ln = din("ffn_ln", [128, 32])
    p.ffn_b1c = din("ffn_b1c", [128, 88])
    p.ffn_b2c = din("ffn_b2c", [128, 16])
    p.sel2m = din("sel2m", [2, 128])
    p.qm_d1 = din("qm_d1", [128, 32])
    p.qm_e1 = din("qm_e1", [128, 24])

    p.out = nc.dram_tensor("out", [OUT_ROWS, DM], F32, kind="ExternalOutput").ap()

    p.x0b = nc.dram_tensor("x0b", [SEQ + XC_TOK - ST, DM], BF16).ap()
    p.x0bT_d = nc.dram_tensor("x0bT_d", [DM, SEQ + ST], BF16).ap()
    p.xc_f32 = nc.dram_tensor("xc_f32", [XC_TOK, DM], F32).ap()
    p.xcb = nc.dram_tensor("xcb", [XC_TOK, DM], BF16).ap()
    p.lo_d = nc.dram_tensor("lo_d", [XC_TOK, DM], BF16).ap()
    p.loT_d = nc.dram_tensor("loT_d", [DM, XC_TOK], BF16).ap()
    p.sp_x1 = nc.dram_tensor("sp_x1", [DM, SPX1_COLS + ST], F32).ap()
    p.sp_bm = nc.dram_tensor("sp_bm", [DM, 4096 + STD], F32).ap()
    p.sp_l0 = nc.dram_tensor("sp_l0", [DM, 4096 + STD], F32).ap()
    p.sp_l1 = nc.dram_tensor("sp_l1", [DM, 3072 + STD], F32).ap()
    p.dbg1 = nc.dram_tensor("dbg1", [128, 12 * 4096], F32).ap()
    p.dbg2 = nc.dram_tensor("dbg2", [128, 12 * 4096], F32).ap()
    p.dbg3 = nc.dram_tensor("dbg3", [128, 12 * 4096], F32).ap()
    p.dbg4 = nc.dram_tensor("dbg4", [128, 12 * 4096], F32).ap()
    p.sp_oh = nc.dram_tensor("sp_oh", [DM, OUT_ROWS], BF16).ap()
    p.sp_ol = nc.dram_tensor("sp_ol", [DM, OUT_ROWS], BF16).ap()

    p.bar_sem = nc.semaphore("bar").__enter__()
    p.bar_cnt = 0

    def barrier():
        p.bar_cnt += 1
        for e in ALLE:
            nc.engines[e].nop().then_inc(p.bar_sem, 1)
        for e in ALLE:
            nc.engines[e].wait_ge(p.bar_sem, p.bar_cnt * len(ALLE))

    p.barrier = barrier

    p.cfg_s = nc.alloc_sbuf_tensor("cfg_s", [1, 32], U32)
    p.ones_b = nc.alloc_sbuf_tensor("ones_b", [128, 1], BF16)
    p.ones_r = nc.alloc_sbuf_tensor("ones_r", [1, ST], BF16)
    p.ones_r32 = nc.alloc_sbuf_tensor("ones_r32", [1, 128], F32)
    p.ps_t = nc.alloc_psum_tensor("ps_t", [128, 4096], F32)
    p.ps = [p.ps_t[:, i * 512:(i + 1) * 512] for i in range(8)]
    p.ps_bt = p.ps_t.bitcast(BF16)

    CONSTS = [1e-5, 1e-6, -0.5, 0.5, LN16, 0.125, 8.0, -80.0, 1.0 / DM]
    p.cst = nc.alloc_sbuf_tensor("cstp", [128, len(CONSTS)], F32)
    for j, v in enumerate(CONSTS):
        nc.const_aps.register(F32, float(v), p.cst[:, j:j + 1])

    sem_init = nc.semaphore("init").__enter__()
    nc.sync.dma_start(out=p.cfg_s[:, :], in_=p.cfg).then_inc(sem_init, 16)
    nc.gpsimd.memset(p.ones_b[:, :], 1.0).then_inc(sem_init, 1)
    nc.gpsimd.memset(p.ones_r[:, :], 1.0).then_inc(sem_init, 1)
    nc.gpsimd.memset(p.ones_r32[:, :], 1.0).then_inc(sem_init, 1)
    for j, v in enumerate(CONSTS):
        nc.gpsimd.memset(p.cst[:, j:j + 1], float(v)).then_inc(sem_init, 1)
    for e in ALLE:
        nc.engines[e].wait_ge(sem_init, 19 + len(CONSTS))

    p._pro_regs = []
    p.streams = {}
    p.zero_regs = {}
    for e in ALLE:
        z = nc.engines[e].alloc_register(f"zero_{e}")
        nc.engines[e].reg_mov(z, 0)
        p.zero_regs[e] = z

    def cfgv(idx, engines, maxv):
        v = nc.values_load(p.cfg_s[0:1, idx:idx + 1], engines=engines,
                           min_val=0, max_val=maxv,
                           skip_runtime_bounds_check=True)
        try:
            for e_ in engines:
                p._pro_regs.append(v.val[e_])
        except Exception:
            pass
        return v

    p.cfgv = cfgv

    import os
    _kph = os.environ.get("KPH", "all")
    phase_A(p)
    for e in ALLE:
        nc.engines[e].wait_ge(p.s_pad, 64)
    barrier()
    phase_BC(p)
    barrier()
    if _kph in ("all", "fa0", "fm0", "fa1", "fm1"):
        phase_FA(p, 0)
    if _kph in ("all", "fm0", "fa1", "fm1"):
        phase_FM(p, 0)
    if _kph in ("all", "fa1", "fm1"):
        phase_FA(p, 1)
    if _kph in ("all", "fm1"):
        phase_FM(p, 1)
    phase_F(p)
    barrier()
    mybir.codegen_inst_isa_subclasses(nc)
    return p



def phase_A(p):
    nc = p.nc
    sA = nc.semaphore("aA").__enter__()
    sT = nc.semaphore("aT").__enter__()
    sL = nc.semaphore("aLl").__enter__()
    sV = nc.semaphore("aVv").__enter__()
    p.stg_sem = nc.semaphore("stg").__enter__()

    s_cast = nc.semaphore("acst").__enter__()
    nc.gpsimd.dma_start(out=p.x0b[0:SEQ, :], in_=p.x0[0:SEQ, :]).then_inc(s_cast, 16)
    p.s_pad = nc.semaphore("apad").__enter__()
    s_padm = nc.semaphore("apadm").__enter__()
    with nc.sbuf_tensor("a_zpad", [128, 1024], F32) as zt:
        nc.gpsimd.memset(zt[:, :], 0.0).then_inc(s_padm, 1)
        nc.gpsimd.wait_ge(s_padm, 1)
        zt3 = zt[:, :].rearrange("q (dt n) -> q dt n", n=128)
        for sp in (p.sp_x1, p.sp_bm, p.sp_l0, p.sp_l1):
            dstp = sp[:, 0:128].rearrange("(dt q) n -> q dt n", q=128)
            nc.gpsimd.dma_start(out=dstp, in_=zt3).then_inc(p.s_pad, 16)
        for e in ALLE:
            nc.engines[e].wait_ge(p.s_pad, 64)
    bc = p.cfgv(CFG_B_C, [E.Pool], SEQ)
    bcs = nc.snap(bc * DM, engines=[E.Pool], min_val=0, max_val=SEQ * DM)
    ap_x0 = dataclasses.replace(p.x0[0:XC_TOK - ST, :], offset=bcs)
    nc.gpsimd.dma_start(out=p.xc_f32[0:XC_TOK - ST, :], in_=ap_x0).then_inc(sA, 16)
    nc.gpsimd.wait_ge(s_cast, 16)
    ap_x0b = dataclasses.replace(p.x0b[0:XC_TOK - ST, :], offset=bcs)
    nc.gpsimd.dma_start(out=p.xcb[0:XC_TOK - ST, :], in_=ap_x0b).then_inc(sA, 16)

    with (
        nc.sbuf_tensor("a_tt0", [128, 8 * ST], BF16) as tt0,
        nc.sbuf_tensor("a_tt1", [128, 8 * ST], BF16) as tt1,
        nc.sbuf_tensor("a_f32", [128, DM], F32) as tf,
        nc.sbuf_tensor("a_b16", [128, DM], BF16) as tb,
        nc.sbuf_tensor("a_lo", [128, DM], BF16) as tl,
    ):
        nc.sync.wait_ge(s_cast, 16)
        for j in range(N_STG):
            nc.sync.wait_ge(sT, 144 * j)
            nc.sync.nop().then_inc(p.stg_sem, 1)
            t = tt0 if j % 2 == 0 else tt1
            for dt in range(8):
                nc.sync.dma_start(out=t[:, dt * ST:(dt + 1) * ST],
                                  in_=p.x0b[j * ST:(j + 1) * ST, dt * 128:(dt + 1) * 128],
                                  transpose=True).then_inc(sT, 16)
            nc.sync.wait_ge(sT, 144 * j + 128)
            nc.sync.dma_start(
                out=p.x0bT_d.rearrange("(dt q) n -> q dt n", q=128)[:, :, j * ST:(j + 1) * ST],
                in_=t[:, :].rearrange("q (dt n) -> q dt n", n=ST)).then_inc(sT, 16)
        nc.sync.wait_ge(sT, 144 * N_STG)
        nc.sync.nop().then_inc(p.stg_sem, 1)

        nc.scalar.wait_ge(sA, 32)
        nlo = (XC_TOK - ST) // 128
        for j in range(nlo):
            if j >= 1:
                nc.scalar.wait_ge(sV, j)
            nc.scalar.dma_start(out=tf[:, :], in_=p.xc_f32[j * 128:(j + 1) * 128, :]).then_inc(sL, 16)
            nc.scalar.dma_start(out=tb[:, :], in_=p.xcb[j * 128:(j + 1) * 128, :]).then_inc(sL, 16)
            nc.vector.wait_ge(sL, j * 48 + 32)
            nc.vector.tensor_sub(tl[:, :], tf[:, :], tb[:, :]).then_inc(sV, 1)
            nc.scalar.wait_ge(sV, j + 1)
            nc.scalar.dma_start(out=p.lo_d[j * 128:(j + 1) * 128, :], in_=tl[:, :]).then_inc(sL, 16)

        nstt = (XC_TOK - ST) // ST
        base = N_STG * 144
        nc.sync.wait_ge(sL, nlo * 48)
        for j in range(nstt):
            t = tt0 if j % 2 == 0 else tt1
            nc.sync.wait_ge(sT, base + 144 * j)
            for dt in range(8):
                nc.sync.dma_start(out=t[:, dt * ST:(dt + 1) * ST],
                                  in_=p.lo_d[j * ST:(j + 1) * ST, dt * 128:(dt + 1) * 128],
                                  transpose=True).then_inc(sT, 16)
            nc.sync.wait_ge(sT, base + 144 * j + 128)
            nc.sync.dma_start(
                out=p.loT_d.rearrange("(dt q) n -> q dt n", q=128)[:, :, j * ST:(j + 1) * ST],
                in_=t[:, :].rearrange("q (dt n) -> q dt n", n=ST)).then_inc(sT, 16)
        nc.sync.wait_ge(sT, base + 144 * nstt)
        nc.sync.nop().then_inc(p.stg_sem, 1)
        p.stg_all = N_STG + 2



def alloc_perf(p, stk):
    nc = p.nc
    a = {}

    def sb(name, shape, dt):
        a[name] = stk.enter_context(nc.sbuf_tensor(name, shape, dt))

    sb("Wq", [128, 8 * DM], BF16)
    sb("Wk", [128, 8 * DM], BF16)
    sb("Wv", [128, 8 * DM], BF16)
    sb("Wo", [128, 8 * DM], BF16)
    sb("projT", [65, PM], BF16)
    sb("qbr", [1, DM], BF16)
    sb("kbr", [1, DM], BF16)
    sb("vb_row", [1, DM], BF16)
    sb("ob_row", [1, DM], BF16)
    sb("gam", [128, 8], F32)
    sb("bet", [128, 8], F32)
    sb("negi_s", [128, ST], F32)
    sb("alen_s", [128, 4], F32)
    sb("x0bT", [128, 8 * ST], BF16)
    sb("x0T", [128, 8 * ST], F32)
    sb("yln", [128, 8 * ST], F32)
    sb("gb1", [128, 8 * ST], F32)
    sb("ylnb", [128, 8 * ST], BF16)
    sb("sqb", [128, 8 * ST], BF16)
    sb("qhat", [128, 8 * ST], BF16)
    sb("khat", [128, 8 * ST], BF16)
    sb("vnat", [128, 4 * DM], BF16)
    sb("qa", [128, ST], BF16)
    sb("ka", [128, ST], BF16)
    sb("qsq", [128, ST], BF16)
    sb("qpT", [128, 2 * ST], BF16)
    sb("kpT", [128, 2 * ST], BF16)
    sb("kpn", [128, 4 * PM], BF16)
    sb("Am", [128, 4 * ST], BF16)
    sb("state", [128, PH * 2 * 65], F32)
    sb("stateb", [128, PH * 2 * 65], BF16)
    sb("dbc", [64, ST], F32)
    sb("r_mu", [1, ST], F32)
    sb("r_t1", [1, ST], F32)
    sb("r_t2", [1, ST], F32)
    sb("r_rstd", [1, ST], F32)
    assert nc.sbuf_bytes_remaining >= 0
    return a


def phase_BC(p):
    nc = p.nc
    with ExitStack() as stk:
        a = alloc_perf(p, stk)
        s_w = nc.semaphore("bc_w").__enter__()
        s_w2 = nc.semaphore("bc_w2").__enter__()
        g = nc.gpsimd
        g.dma_start(out=a["Wq"][:, :].rearrange("q (kt o) -> q kt o", kt=8), in_=p.q_w.rearrange("(kt q) o -> q kt o", q=128)).then_inc(s_w, 16)
        g.dma_start(out=a["Wk"][:, :].rearrange("q (kt o) -> q kt o", kt=8), in_=p.k_w.rearrange("(kt q) o -> q kt o", q=128)).then_inc(s_w, 16)
        g.dma_start(out=a["Wv"][:, :].rearrange("q (kt o) -> q kt o", kt=8), in_=p.v_w.rearrange("(kt q) o -> q kt o", q=128)).then_inc(s_w, 16)
        g.dma_start(out=a["Wo"][:, :].rearrange("q (kt o) -> q kt o", kt=8), in_=p.out_w.rearrange("(kt q) o -> q kt o", q=128)).then_inc(s_w, 16)
        g.dma_start(out=a["projT"][:, :], in_=p.projT_aug).then_inc(s_w, 16)
        g.dma_start(out=a["vb_row"][:, :], in_=p.v_b).then_inc(s_w, 16)
        g.dma_start(out=a["ob_row"][:, :], in_=p.out_b).then_inc(s_w, 16)
        nc.sync.dma_start(out=a["negi_s"][:, :], in_=p.negi).then_inc(s_w2, 16)
        nc.sync.dma_start(out=a["alen_s"][:, :], in_=p.alen).then_inc(s_w2, 16)
        g.dma_start(out=a["qbr"][:, :], in_=p.q_b).then_inc(s_w, 16)
        g.dma_start(out=a["kbr"][:, :], in_=p.k_b).then_inc(s_w, 16)
        nc.sync.dma_start(out=a["gam"][:, :], in_=p.gamma).then_inc(s_w2, 16)
        nc.sync.dma_start(out=a["bet"][:, :], in_=p.beta).then_inc(s_w2, 16)
        for e in ALLE:
            nc.engines[e].wait_ge(s_w, 9 * 16)
            nc.engines[e].wait_ge(s_w2, 4 * 16)
        s_ms = nc.semaphore("bc_ms").__enter__()
        nc.gpsimd.memset(a["state"][:, :], 0.0).then_inc(s_ms, 1)
        nc.gpsimd.memset(a["stateb"][:, :], 0.0).then_inc(s_ms, 1)
        for e in ALLE:
            nc.engines[e].wait_ge(s_ms, 2)
        p.barrier()

        emit_perf_loop(p, a, feed_only=True)
        p.barrier()
        emit_perf_loop(p, a, feed_only=False)
        p.barrier()


def emit_perf_loop(p, a, feed_only):
    nc = p.nc
    mark = pro_mark(p)
    tag = "f" if feed_only else "m"
    cL = Ctr(nc, f"p{tag}L", p.deltas)
    cS = Ctr(nc, f"p{tag}S", p.deltas)
    cV = Ctr(nc, f"p{tag}V", p.deltas)
    cT = Ctr(nc, f"p{tag}T", p.deltas)
    ctrs = (cL, cS, cV, cT)

    if feed_only:
        n = p.cfgv(CFG_N_FEED, ALLE, SEQ // ST)
        base = None
    else:
        n = p.cfgv(CFG_N_C, ALLE, SEQ // ST + 8)
        base = p.cfgv(CFG_B_C, ALLE, SEQ)

    x0bT_src = p.x0bT_d.rearrange("(dt q) n -> q dt n", q=128)
    loT_src = p.loT_d.rearrange("(dt q) n -> q dt n", q=128)
    spx_dst = p.sp_x1.rearrange("(dt q) n -> q dt n", q=128)
    x0bT_dst = a["x0bT"][:, :].rearrange("q (dt n) -> q dt n", n=ST)
    lo_dst = a["qhat"][:, :].rearrange("q (dt n) -> q dt n", n=ST)
    x0T_src3 = a["x0T"][:, :].rearrange("q (dt n) -> q dt n", n=ST)

    def load_x0bT(off_expr, stg_wait=None):
        if stg_wait is not None:
            nc.sync.wait_ge(p.stg_sem, stg_wait)
        cL.inc(nc.sync.dma_start(out=x0bT_dst, in_=x0bT_src[:, :, ds(off_expr, ST)]), 16)

    

    def load_loT(off_expr):
        cL.inc(nc.scalar.dma_start(out=lo_dst, in_=loT_src[:, :, ds(off_expr, ST)]), 16)

    p.load_x0bT, p.load_loT = load_x0bT, load_loT
    p.spx_dst, p.x0T_src3 = spx_dst, x0T_src3

    if feed_only:
        load_x0bT(0, stg_wait=2)
    else:
        nc.sync.wait_ge(p.stg_sem, p.stg_all)
        b0 = rcalc(p, E.SP, SEQ, add=(base,))
        p.c_base = b0
        load_x0bT(b0)
        load_loT(0)

    if DEBUG and not feed_only:
        p.sdbg_s = nc.semaphore("sdbgs").__enter__()
        p.sdbg_g = nc.semaphore("sdbgg").__enter__()
    from contextlib import nullcontext
    guard = nc.If(n > 0) if feed_only else nullcontext()
    with guard:
        with kfori(p, n) as i:
            for c in ctrs:
                c.begin_body()
            begin_snaps(p)
            cur = Cursors(p, i, ctrs)
            body_start_waits(cur, ctrs)
            emit_perf_body(p, a, i, feed_only, cL, cS, cV, cT, cur)
            for c in ctrs:
                c.end_body()
            cur.finish()
            end_snaps(p)
    drain_all(p, ctrs, n)
    pro_free(p, mark)
    if DEBUG and not feed_only:
        nc.sync.wait_ge(p.sdbg_s.h if hasattr(p.sdbg_s, "h") else p.sdbg_s, n * 32)
        nc.gpsimd.wait_ge(p.sdbg_g.h if hasattr(p.sdbg_g, "h") else p.sdbg_g, n * 32)


def emit_perf_body(p, a, i, feed_only, cL, cS, cV, cT, cur):
    nc = p.nc
    ps = p.ps
    rot = MMRot(p, [0, 1, 2, 3], cur)

    cur.wait(E.Activation, cL)
    cS.inc(nc.scalar.activation(a["sqb"][:, :], a["x0bT"][:, :], AF.Square))

    cur.wait(E.PE, cL)
    for kt in range(8):
        mm = nc.tensor.matmul(ps[6][0:1, :], p.ones_b[:, :],
                              a["x0bT"][:, kt * ST:(kt + 1) * ST],
                              start=(kt == 0), stop=(kt == 7))
    cT.inc(mm)
    cur.wait(E.PE, cS)
    for kt in range(8):
        mm = nc.tensor.matmul(ps[7][0:1, :], p.ones_b[:, :],
                              a["sqb"][:, kt * ST:(kt + 1) * ST],
                              start=(kt == 0), stop=(kt == 7))
    cT.inc(mm)

    cur.wait(E.DVE, cT)
    nc.vector.tensor_scalar_mul(a["r_mu"][:, :], ps[6][0:1, :], 1.0 / DM)
    nc.vector.tensor_scalar_mul(a["r_t1"][:, :], ps[7][0:1, :], 1.0 / DM)
    nc.vector.tensor_mul(a["r_t2"][:, :], a["r_mu"][:, :], a["r_mu"][:, :])
    cV.inc(nc.vector.tensor_sub(a["r_t1"][:, :], a["r_t1"][:, :], a["r_t2"][:, :]))
    cur.wait(E.Activation, cV)
    nc.scalar.activation(a["r_t2"][:, :], a["r_t1"][:, :], AF.Ln, bias=1e-5)
    cS.inc(nc.scalar.activation(a["r_rstd"][:, :], a["r_t2"][:, :], AF.Exp, scale=-0.5))

    cur.wait(E.PE, cS)
    cT.inc(nc.tensor.matmul(ps[4][:, :], p.ones_r32[:, :], a["r_mu"][:, :], start=True, stop=True))
    cT.inc(nc.tensor.matmul(ps[5][:, :], p.ones_r32[:, :], a["r_rstd"][:, :], start=True, stop=True))

    cur.wait(E.DVE, cT)
    if feed_only:
        for kt in range(8):
            nc.vector.tensor_sub(a["ylnb"][:, kt * ST:(kt + 1) * ST],
                                 a["x0bT"][:, kt * ST:(kt + 1) * ST], ps[4][:, :])
        for kt in range(8):
            v = nc.vector.tensor_mul(a["ylnb"][:, kt * ST:(kt + 1) * ST],
                                     a["ylnb"][:, kt * ST:(kt + 1) * ST], ps[5][:, :])
        cV.inc(v)
    else:
        for kt in range(8):
            nc.vector.tensor_add(a["x0T"][:, kt * ST:(kt + 1) * ST],
                                 a["x0bT"][:, kt * ST:(kt + 1) * ST],
                                 a["qhat"][:, kt * ST:(kt + 1) * ST])
        for kt in range(8):
            nc.vector.tensor_sub(a["yln"][:, kt * ST:(kt + 1) * ST],
                                 a["x0T"][:, kt * ST:(kt + 1) * ST], ps[4][:, :])
        for kt in range(8):
            nc.vector.tensor_mul(a["yln"][:, kt * ST:(kt + 1) * ST],
                                 a["yln"][:, kt * ST:(kt + 1) * ST], ps[5][:, :])
        cV.inc(nc.vector.tensor_copy(a["ylnb"][:, :], a["yln"][:, :]))

    def proj_block(Wname, bias_row, dst):
        cur.wait(E.PE, cV)
        for dt in range(8):
            bank, bi = rot.group()
            for kt in range(8):
                nc.tensor.matmul(bank[:, :],
                                 a[Wname][:, kt * DM + dt * 128: kt * DM + (dt + 1) * 128],
                                 a["ylnb"][:, kt * ST:(kt + 1) * ST],
                                 start=(kt == 0), stop=False)
            cT.inc(nc.tensor.matmul(bank[:, :], a[bias_row][0:1, dt * 128:(dt + 1) * 128],
                                    p.ones_r[0:1, :], start=False, stop=True))
            cur.wait(E.Activation, cT)
            cS.inc(nc.scalar.activation(dst[:, dt * ST:(dt + 1) * ST], bank[:, :],
                                        AF.Copy, scale=0.125))
            rot.done(bi, cS)

    proj_block("Wk", "kbr", a["khat"])
    if not feed_only:
        proj_block("Wq", "qbr", a["qhat"])

    cur.wait(E.PE, cV)
    for tt in range(4):
        for vc in range(2):
            bank, bi = rot.group()
            for kt in range(8):
                nc.tensor.matmul(bank[:, :],
                                 a["ylnb"][:, kt * ST + tt * 128: kt * ST + (tt + 1) * 128],
                                 a["Wv"][:, kt * DM + vc * 512: kt * DM + (vc + 1) * 512],
                                 start=(kt == 0), stop=False)
            cT.inc(nc.tensor.matmul(bank[:, :], p.ones_r[0:1, 0:128],
                                    a["vb_row"][0:1, vc * 512:(vc + 1) * 512],
                                    start=False, stop=True))
            cur.wait(E.Activation, cT)
            cS.inc(nc.scalar.copy(a["vnat"][:, tt * DM + vc * 512: tt * DM + (vc + 1) * 512],
                                  bank[:, :]))
            rot.done(bi, cS)

    for h in range(PH):
        dt, row = h // 2, (h % 2) * 64
        cur.wait(E.Activation, cV)
        cS.inc(nc.scalar.copy(a["ka"][0:64, :], a["khat"][row:row + 64, dt * ST:(dt + 1) * ST]))
        cS.inc(nc.scalar.activation(a["qsq"][0:64, :], a["ka"][0:64, :], AF.Square))
        cur.wait(E.PE, cS)
        cur.wait(E.PE, cV)
        cT.inc(nc.tensor.matmul(ps[6][0:1, :], p.ones_b[0:64, :], a["qsq"][0:64, :],
                                start=True, stop=True))
        cur.wait(E.Activation, cT)
        cS.inc(nc.scalar.activation(a["ka"][64:65, :], ps[6][0:1, :], AF.Copy,
                                    scale=0.5, bias=LN16))
        if not feed_only:
            cS.inc(nc.scalar.copy(a["qa"][0:64, :], a["qhat"][row:row + 64, dt * ST:(dt + 1) * ST]))
            cS.inc(nc.scalar.activation(a["qsq"][0:64, :], a["qa"][0:64, :], AF.Square))
            cur.wait(E.PE, cS)
            cT.inc(nc.tensor.matmul(ps[7][0:1, :], p.ones_b[0:64, :], a["qsq"][0:64, :],
                                    start=True, stop=True))
            cur.wait(E.Activation, cT)
            cS.inc(nc.scalar.activation(a["qa"][64:65, :], ps[7][0:1, :], AF.Copy,
                                        scale=0.5, bias=LN16))

        cur.wait(E.PE, cS)
        for tt in range(4):
            bank, bi = rot.group()
            cT.inc(nc.tensor.matmul(bank[:, 0:PM], a["ka"][0:65, tt * 128:(tt + 1) * 128],
                                    a["projT"][0:65, :], start=True, stop=True))
            cur.wait(E.Activation, cT)
            cS.inc(nc.scalar.activation(a["kpn"][:, tt * PM:(tt + 1) * PM], bank[:, 0:PM], AF.Exp))
            rot.done(bi, cS)
        if not feed_only:
            for mt in range(2):
                bank, bi = rot.group()
                cT.inc(nc.tensor.matmul(bank[:, :], a["projT"][0:65, mt * 128:(mt + 1) * 128],
                                        a["qa"][0:65, :], start=True, stop=True))
                cur.wait(E.Activation, cT)
                cS.inc(nc.scalar.activation(a["qpT"][:, mt * ST:(mt + 1) * ST], bank[:, :], AF.Exp))
                rot.done(bi, cS)
            for mt in range(2):
                bank, bi = rot.group()
                cT.inc(nc.tensor.matmul(bank[:, :], a["projT"][0:65, mt * 128:(mt + 1) * 128],
                                        a["ka"][0:65, :], start=True, stop=True))
                cur.wait(E.Activation, cT)
                cS.inc(nc.scalar.activation(a["kpT"][:, mt * ST:(mt + 1) * ST], bank[:, :], AF.Exp))
                rot.done(bi, cS)

            for jt in range(4):
                bank, bi = rot.group()
                cur.wait(E.PE, cS)
                for mt in range(2):
                    mm = nc.tensor.matmul(bank[:, :],
                                          a["kpT"][:, mt * ST + jt * 128: mt * ST + (jt + 1) * 128],
                                          a["qpT"][:, mt * ST:(mt + 1) * ST],
                                          start=(mt == 0), stop=(mt == 1))
                cT.inc(mm)
                cur.wait(E.DVE, cT)
                cV.inc(nc.vector.tensor_mask(a["Am"][:, jt * ST:(jt + 1) * ST], bank[:, :],
                                             a["alen_s"][:, jt:jt + 1], a["negi_s"][:, :], 0))
                rot.done(bi, cV)

            bank_n, bi_n = rot.group()
            cur.wait(E.PE, cV)
            for jt in range(4):
                nc.tensor.matmul(bank_n[0:64, :], a["vnat"][:, jt * DM + h * 64: jt * DM + (h + 1) * 64],
                                 a["Am"][:, jt * ST:(jt + 1) * ST], start=(jt == 0), stop=False)
            for mt in range(2):
                slot = (2 * h + mt) * 65
                nc.tensor.matmul(bank_n[0:64, :], a["stateb"][:, slot:slot + 64],
                                 a["qpT"][:, mt * ST:(mt + 1) * ST], start=False, stop=(mt == 1))
            for jt in range(4):
                nc.tensor.matmul(bank_n[64:65, :], p.ones_b[:, :],
                                 a["Am"][:, jt * ST:(jt + 1) * ST], start=(jt == 0), stop=False)
            for mt in range(2):
                slot = (2 * h + mt) * 65
                mm = nc.tensor.matmul(bank_n[64:65, :], a["stateb"][:, slot + 64:slot + 65],
                                      a["qpT"][:, mt * ST:(mt + 1) * ST], start=False, stop=(mt == 1))
            cT.inc(mm)

            cur.wait(E.DVE, cT)
            cV.inc(nc.vector.reciprocal(a["r_t2"][:, :], bank_n[64:65, :]))
            cur.wait(E.PE, cV)
            cT.inc(nc.tensor.matmul(ps[5][0:64, :], p.ones_r32[0:1, 0:64], a["r_t2"][:, :],
                                    start=True, stop=True))
            cur.wait(E.Activation, cT)
            cS.inc(nc.scalar.copy(a["dbc"][:, :], ps[5][0:64, :]))
            cur.wait(E.DVE, cS)
            cV.inc(nc.vector.tensor_mul(a["x0bT"][row:row + 64, dt * ST:(dt + 1) * ST],
                                        bank_n[0:64, :], a["dbc"][:, :]))
            rot.done(bi_n, cV)

        bank_u, bi_u = rot.group()
        cur.wait(E.PE, cS)
        first = True
        for c in range(4):
            for mt in range(2):
                nc.tensor.matmul(bank_u[:, mt * 65: mt * 65 + 64],
                                 a["kpn"][:, c * PM + mt * 128: c * PM + (mt + 1) * 128],
                                 a["vnat"][:, c * DM + h * 64: c * DM + (h + 1) * 64],
                                 start=first, stop=False)
                first = False
                mm = nc.tensor.matmul(bank_u[:, mt * 65 + 64: mt * 65 + 65],
                                      a["kpn"][:, c * PM + mt * 128: c * PM + (mt + 1) * 128],
                                      p.ones_b[:, :], start=False,
                                      stop=(c == 3 and mt == 1))
        cT.inc(mm)
        slot0 = 2 * h * 65
        cur.wait(E.DVE, cT)
        cV.inc(nc.vector.tensor_add(a["state"][:, slot0:slot0 + 130],
                                    a["state"][:, slot0:slot0 + 130], bank_u[:, 0:130]))
        cV.inc(nc.vector.tensor_copy(a["stateb"][:, slot0:slot0 + 130],
                                     a["state"][:, slot0:slot0 + 130]))
        rot.done(bi_u, cV)

    if feed_only:
        cur.wait(E.SP, cS)
        cur.wait(E.SP, cT)
        cur.wait(E.SP, cV)
        p.load_x0bT(rcalc(p, E.SP, SEQ, imm=ST, mul=(i, ST)), stg_wait=i + 3)
        return

    for dt in range(8):
        bank, bi = rot.group()
        cur.wait(E.PE, cV)
        for et in range(8):
            nc.tensor.matmul(bank[:, :], a["Wo"][:, et * DM + dt * 128: et * DM + (dt + 1) * 128],
                             a["x0bT"][:, et * ST:(et + 1) * ST], start=(et == 0), stop=False)
        cT.inc(nc.tensor.matmul(bank[:, :], a["ob_row"][0:1, dt * 128:(dt + 1) * 128],
                                p.ones_r[0:1, :], start=False, stop=True))
        cur.wait(E.DVE, cT)
        cV.inc(nc.vector.tensor_add(a["gb1"][:, dt * ST:(dt + 1) * ST], bank[:, :],
                                    a["yln"][:, dt * ST:(dt + 1) * ST]))
        rot.done(bi, cV)

    emit_ln_tail(p, a, i, cS, cV, cT, cur, src="gb1", scratch="yln", dst="gb1",
                 eps=1e-6, affine=True)
    emit_ln_tail(p, a, i, cS, cV, cT, cur, src="gb1", scratch="yln", dst=None,
                 eps=1e-5, affine=False)
    cur.wait(E.DVE, cT)
    for kt in range(8):
        v = nc.vector.tensor_add(a["x0T"][:, kt * ST:(kt + 1) * ST],
                                 a["x0T"][:, kt * ST:(kt + 1) * ST],
                                 a["yln"][:, kt * ST:(kt + 1) * ST])
    cV.inc(v)

    if DEBUG:
        dco = snapo(p, i * 4096, 11 * 4096)
        dcp = snapo(p, i * 4096, 11 * 4096, engines=(E.Pool,))
        cur.wait(E.SP, cV)
        cur.wait(E.SP, cS)
        nc.sync.dma_start(out=p.dbg1[:, ds(dco, 4096)], in_=a["yln"][:, :]).then_inc(p.sdbg_s, 16)
        nc.gpsimd.dma_start(out=p.dbg2[:, ds(dcp, 4096)], in_=a["qhat"][:, :]).then_inc(p.sdbg_g, 16)
        nc.gpsimd.dma_start(out=p.dbg3[:, ds(dcp, 4096)], in_=a["x0bT"][:, :]).then_inc(p.sdbg_g, 16)
        nc.sync.dma_start(out=p.dbg4[:, ds(dco, 4096)], in_=a["gb1"][:, :]).then_inc(p.sdbg_s, 16)

    cur.wait(E.SP, cV)
    cur.wait(E.SP, cS)
    cur.wait(E.SP, cT)
    cur.wait(E.Activation, cV)
    spo = rcalc(p, E.Activation, SPX1_COLS, imm=128, mul=(i, ST))
    cL.inc(nc.scalar.dma_start(out=p.spx_dst[:, :, ds(spo, ST)], in_=p.x0T_src3), 16)
    p.load_x0bT(rcalc(p, E.SP, SEQ, imm=ST, mul=(i, ST), add=(p.c_base,)))
    p.load_loT(rcalc(p, E.Activation, XC_TOK - ST, imm=ST, mul=(i, ST)))


def emit_ln_tail(p, a, i, cS, cV, cT, cur, src, scratch, dst, eps, affine):
    nc = p.nc
    ps = p.ps
    cur.wait(E.Activation, cV)
    cS.inc(nc.scalar.copy(a["sqb"][:, :], a[src][:, :]))
    cur.wait(E.PE, cS)
    cur.wait(E.PE, cV)
    for kt in range(8):
        mm = nc.tensor.matmul(ps[6][0:1, :], p.ones_b[:, :],
                              a["sqb"][:, kt * ST:(kt + 1) * ST],
                              start=(kt == 0), stop=(kt == 7))
    cT.inc(mm)
    cur.wait(E.Activation, cT)
    cS.inc(nc.scalar.activation(a["sqb"][:, :], a["sqb"][:, :], AF.Square))
    cur.wait(E.PE, cS)
    for kt in range(8):
        mm = nc.tensor.matmul(ps[7][0:1, :], p.ones_b[:, :],
                              a["sqb"][:, kt * ST:(kt + 1) * ST],
                              start=(kt == 0), stop=(kt == 7))
    cT.inc(mm)
    cur.wait(E.DVE, cT)
    nc.vector.tensor_scalar_mul(a["r_mu"][:, :], ps[6][0:1, :], 1.0 / DM)
    nc.vector.tensor_scalar_mul(a["r_t1"][:, :], ps[7][0:1, :], 1.0 / DM)
    nc.vector.tensor_mul(a["r_t2"][:, :], a["r_mu"][:, :], a["r_mu"][:, :])
    cV.inc(nc.vector.tensor_sub(a["r_t1"][:, :], a["r_t1"][:, :], a["r_t2"][:, :]))
    cur.wait(E.Activation, cV)
    nc.scalar.activation(a["r_t2"][:, :], a["r_t1"][:, :], AF.Ln, bias=eps)
    cS.inc(nc.scalar.activation(a["r_rstd"][:, :], a["r_t2"][:, :], AF.Exp, scale=-0.5))
    cur.wait(E.PE, cS)
    cur.wait(E.PE, cV)
    cT.inc(nc.tensor.matmul(ps[4][:, :], p.ones_r32[:, :], a["r_mu"][:, :], start=True, stop=True))
    cT.inc(nc.tensor.matmul(ps[5][:, :], p.ones_r32[:, :], a["r_rstd"][:, :], start=True, stop=True))
    cur.wait(E.DVE, cT)
    for kt in range(8):
        nc.vector.tensor_sub(a[scratch][:, kt * ST:(kt + 1) * ST],
                             a[src][:, kt * ST:(kt + 1) * ST], ps[4][:, :])
    for kt in range(8):
        v = nc.vector.tensor_mul(a[scratch][:, kt * ST:(kt + 1) * ST],
                                 a[scratch][:, kt * ST:(kt + 1) * ST], ps[5][:, :])
    cV.inc(v)
    if affine:
        for dt in range(8):
            v = nc.vector.tensor_scalar(a[dst][:, dt * ST:(dt + 1) * ST],
                                        a[scratch][:, dt * ST:(dt + 1) * ST],
                                        a["gam"][:, dt:dt + 1], a["bet"][:, dt:dt + 1],
                                        ALU.mult, ALU.add)
        cV.inc(v)



FA_SPAN = 8 * (STD + 1)


def alloc_fa(p, stk, with_ln, layer):
    nc = p.nc
    a = {}

    def sb(name, shape, dt):
        a[name] = stk.enter_context(nc.sbuf_tensor(f"fa{layer}_" + name, shape, dt))
        return a[name]

    sb("Wqkv", [128, 8 * 1536], BF16)
    sb("Wo", [128, 4 * DM], BF16)
    sb("rotpb", [128, 128], BF16)
    sb("identb", [128, 128], BF16)
    sb("xsp", [128, FA_SPAN], F32)
    sb("xb16", [128, FA_SPAN], BF16)
    sb("sqb", [128, FA_SPAN], BF16)
    if with_ln:
        sb("yln", [128, FA_SPAN], F32)
    sb("hb", [128, 8 * STD], BF16)
    sb("r_mu", [1, STD + 1], F32)
    sb("r_t1", [1, STD + 1], F32)
    sb("r_t2", [1, STD + 1], F32)
    sb("r_rstd", [1, STD + 1], F32)
    sb("sqt", [128, STD], BF16)
    sb("rr", [2, 2 * STD], F32)
    sb("qsb", [128, 4 * STD], BF16)
    sb("tr2", [128, STD], F32)
    sb("tr3", [128, STD], F32)
    sb("tr4", [128, STD], F32)
    sb("rsqs", [128, STD], F32)
    sb("qfb", [128, 4 * STD], BF16)
    sb("kwin", [128, 4 * 768], BF16)
    sb("vwin", [128, 6 * FIN], BF16)
    sb("rt4", [128, 4 * STD], F32)
    sb("simm", [128, 640], F32)
    sb("simx", [128, 640], F32)
    sb("attb", [128, 640], BF16)
    sb("attT", [128, 640], BF16)
    sb("mxc", [128, 1], F32)
    sb("nmx", [128, 1], F32)
    sb("denc", [128, 1], F32)
    sb("dencs", [128, 1], F32)
    sb("simxn", [128, 640], F32)
    sb("identf", [128, 128], F32)
    sb("rden", [128, 1], F32)
    sb("qmc", [128, 2], F32)
    sb("e1c", [128, 1], F32)
    sb("z0c", [128, 1], F32)
    sb("rp1", [128, 1], F32)
    sb("sclq", [128, 1], F32)
    sb("sclk", [128, 1], F32)
    sb("bd2", [128, 2], BF16)
    sb("sel2", [2, 128], F32)
    sb("oTb", [128, 2 * FIN], BF16)
    sb("xout", [128, 8 * STD], F32)
    return a


def phase_FA(p, layer):
    nc = p.nc
    with_ln = (layer == 0)
    with ExitStack() as stk:
        a = alloc_fa(p, stk, with_ln, layer)
        s_w = nc.semaphore(f"fa{layer}_w").__enter__()
        s_w2 = nc.semaphore(f"fa{layer}_w2").__enter__()
        s_ms = nc.semaphore(f"fa{layer}_ms").__enter__()
        g = nc.gpsimd
        wq_src = p.wqkv[layer:layer + 1, :, :].rearrange("u (kt q) o -> q (u kt) o", q=128)
        g.dma_start(out=a["Wqkv"][:, :].rearrange("q (kt o) -> q kt o", o=1536),
                    in_=wq_src).then_inc(s_w, 16)
        wo_src = p.wo[layer:layer + 1, :, :].rearrange("u (kt q) o -> q (u kt) o", q=128)
        g.dma_start(out=a["Wo"][:, :].rearrange("q (kt o) -> q kt o", o=DM),
                    in_=wo_src).then_inc(s_w, 16)
        g.dma_start(out=a["rotpb"][:, :], in_=p.rotp).then_inc(s_w, 16)
        g.dma_start(out=a["identb"][:, :], in_=p.ident).then_inc(s_w, 16)
        nc.sync.dma_start(out=a["identf"][:, :], in_=p.ident).then_inc(s_w2, 16)
        nc.sync.dma_start(out=a["sclq"][:, :],
                          in_=p.ffn_scl[:, 2 * layer:2 * layer + 1], allow_slow_non_contiguous=True).then_inc(s_w2, 16)
        nc.sync.dma_start(out=a["sclk"][:, :],
                          in_=p.ffn_scl[:, 2 * layer + 1:2 * layer + 2], allow_slow_non_contiguous=True).then_inc(s_w2, 16)
        nc.sync.dma_start(out=a["rp1"][:, :], in_=p.rowp1, allow_slow_non_contiguous=True).then_inc(s_w2, 16)
        g.memset(a["bd2"][:, :], 0.0).then_inc(s_ms, 1)
        nc.sync.dma_start(out=a["sel2"][:, :], in_=p.sel2m).then_inc(s_w2, 16)
        g.memset(a["kwin"][:, :], 0.0).then_inc(s_ms, 1)
        g.memset(a["vwin"][:, :], 0.0).then_inc(s_ms, 1)
        g.memset(a["e1c"][:, :], 512.0).then_inc(s_ms, 1)
        g.memset(a["z0c"][:, :], 0.0).then_inc(s_ms, 1)
        g.wait_ge(s_ms, 5)
        g.memset(a["bd2"][0:64, 0:1], 1.0).then_inc(s_ms, 1)
        g.memset(a["bd2"][64:128, 1:2], 1.0).then_inc(s_ms, 1)
        for e in ALLE:
            nc.engines[e].wait_ge(s_w, 4 * 16)
            nc.engines[e].wait_ge(s_w2, 5 * 16)
            nc.engines[e].wait_ge(s_ms, 7)

        emit_fa_loop(p, a, layer, warm=True)
        p.barrier()
        emit_fa_loop(p, a, layer, warm=False)
        p.barrier()



def _spdbg(p, label):
    import os, itertools
    if not os.environ.get("REGDBG"):
        return
    if not hasattr(_spdbg, "c"):
        _spdbg.c = itertools.count()
    nc = p.nc
    got = []
    while True:
        try:
            got.append(nc.sync.alloc_register(f"dbg_{next(_spdbg.c)}"))
        except ValueError:
            break
    for r in got:
        nc.sync.free_register(r)
    print(f"[regdbg] {label}: SP free {len(got)}", flush=True)

def emit_fa_loop(p, a, layer, warm):
    nc = p.nc
    mark = pro_mark(p)
    tag = f"fa{layer}{'w' if warm else 'm'}"
    cL = Ctr(nc, tag + "L", p.deltas)
    cA = Ctr(nc, tag + "A", p.deltas)
    cG = Ctr(nc, tag + "G", p.deltas)
    cP = Ctr(nc, tag + "P", p.deltas)
    cS = Ctr(nc, tag + "S", p.deltas)
    cV = Ctr(nc, tag + "V", p.deltas)
    cT = Ctr(nc, tag + "T", p.deltas)
    ctrs = (cL, cA, cG, cP, cS, cV, cT)

    if layer == 0:
        n = p.cfgv(CFG_N_D1W if warm else CFG_N_D1, ALLE, 64)
        base = p.cfgv(CFG_B_D1W if warm else CFG_B_D1, [E.SP, E.Activation], SEQ)
        src_sp, src_base_cfg = p.sp_x1, CFG_B_C
        dst_sp = p.sp_bm
        qm_in, qm_max = p.qm_d1, 32
    else:
        n = p.cfgv(CFG_N_E1W if warm else CFG_N_E1, ALLE, 64)
        base = p.cfgv(CFG_B_E1W if warm else CFG_B_E1, [E.SP, E.Activation], SEQ)
        src_sp, src_base_cfg = p.sp_l0, CFG_B_D2
        dst_sp = p.sp_l1
        qm_in, qm_max = p.qm_e1, 24
    src_base = p.cfgv(src_base_cfg, [E.SP], SEQ)
    _spdbg(p, tag + " after cfgv")

    xsp3 = a["xsp"][:, :].rearrange("q (dt n) -> q dt n", n=STD + 1)
    src3 = src_sp.rearrange("(dt q) n -> q dt n", q=128)
    dst3 = dst_sp.rearrange("(dt q) n -> q dt n", q=128)
    xout3 = a["xout"][:, :].rearrange("q (dt n) -> q dt n", n=STD)
    sp_cols = {id(p.sp_x1): SPX1_COLS + ST, id(p.sp_bm): 4096 + STD,
               id(p.sp_l0): 4096 + STD, id(p.sp_l1): 3072 + STD}
    src_cols = sp_cols[id(src_sp)]

    xmax = src_cols - (STD + 1)
    smax = sp_cols[id(dst_sp)] - STD
    p._fa_smax = smax

    def load_iter(i_rv):
        if i_rv is None:
            offX = rcalc(p, E.SP, xmax, imm=127, add=(base,), sub=(src_base,))
            offR = rcalc(p, E.Activation, SEQ, add=(base,))
            offQ = 0
        else:
            offX = rcalc(p, E.SP, xmax, imm=127 + STD, mul=(i_rv, STD),
                         add=(base,), sub=(src_base,))
            offR = rcalc(p, E.Activation, SEQ, imm=STD, mul=(i_rv, STD), add=(base,))
            offQ = rcalc(p, E.SP, qm_max - 2, imm=2, mul=(i_rv, 2))
        cL.inc(nc.sync.dma_start(out=xsp3, in_=src3[:, :, ds(offX, STD + 1)],
                                  bounds_check="skip_entire_dma"), 16)
        rt43 = a["rt4"][:, :].rearrange("q (f n) -> q f n", n=STD)
        rall3 = p.rot_all.rearrange("q (f n) -> q f n", f=4)
        cA.inc(nc.scalar.dma_start(out=rt43, in_=rall3[:, :, ds(offR, STD)]), 16)
        if not warm:
            cL.inc(nc.sync.dma_start(out=a["qmc"][:, :], in_=qm_in[:, ds(offQ, 2)]), 16)

    from contextlib import nullcontext
    load_iter(None)
    _spdbg(p, tag + " after prologue")
    guard = nc.If(n > 0) if warm else nullcontext()
    with guard:
        with kfori(p, n) as i:
            _spdbg(p, tag + " in fori")
            for c in ctrs:
                c.begin_body()
            begin_snaps(p)
            cur = Cursors(p, i, ctrs)
            body_start_waits(cur, ctrs, engines=ALLE)
            _spdbg(p, tag + " after cursors")
            emit_fa_body(p, a, layer, warm, i, cL, cA, cG, cP, cS, cV, cT, cur,
                         load_iter, dst3)
            for c in ctrs:
                c.end_body()
            cur.finish()
            end_snaps(p)
            _spdbg(p, tag + " after body frees")
    drain_all(p, ctrs, n)
    pro_free(p, mark)
    _spdbg(p, tag + " after loop")


def emit_fa_body(p, a, layer, warm, i, cL, cA, cG, cP, cS, cV, cT, cur,
                 load_iter, dst3):
    nc = p.nc
    ps = p.ps
    ps_t = p.ps_t
    with_ln = (layer == 0)

    vw = a["vwin"]
    for mt in range(4):
        kw = a["kwin"][:, mt * 768:(mt + 1) * 768]
        cG.inc(nc.gpsimd.dma_start(out=kw[:, 0:256], in_=kw[:, 256:512]), 16)
    cG.inc(nc.gpsimd.dma_start(out=vw[:, 0:1024], in_=vw[:, 1024:2048]), 16)
    cur.wait(E.Pool, cG)
    for mt in range(4):
        kw = a["kwin"][:, mt * 768:(mt + 1) * 768]
        cG.inc(nc.gpsimd.dma_start(out=kw[:, 256:512], in_=kw[:, 512:768]), 16)
    cG.inc(nc.gpsimd.dma_start(out=vw[:, 1024:2048], in_=vw[:, 2048:3072]), 16)

    cur.wait(E.DVE, cL)
    cV.inc(nc.vector.tensor_copy(a["xb16"][:, :], a["xsp"][:, :]))
    if with_ln:
        cur.wait(E.Activation, cV)
        cS.inc(nc.scalar.activation(a["sqb"][:, :], a["xb16"][:, :], AF.Square))
        cur.wait(E.PE, cV)
        for dt in range(8):
            mm = nc.tensor.matmul(ps[6][0:1, 0:STD + 1], p.ones_b[:, :],
                                  a["xb16"][:, dt * 257:(dt + 1) * 257],
                                  start=(dt == 0), stop=(dt == 7))
        cT.inc(mm)
        cur.wait(E.PE, cS)
        for dt in range(8):
            mm = nc.tensor.matmul(ps[7][0:1, 0:STD + 1], p.ones_b[:, :],
                                  a["sqb"][:, dt * 257:(dt + 1) * 257],
                                  start=(dt == 0), stop=(dt == 7))
        cT.inc(mm)
        cur.wait(E.DVE, cT)
        nc.vector.tensor_scalar_mul(a["r_mu"][:, :], ps[6][0:1, 0:STD + 1], 1.0 / DM)
        nc.vector.tensor_scalar_mul(a["r_t1"][:, :], ps[7][0:1, 0:STD + 1], 1.0 / DM)
        nc.vector.tensor_mul(a["r_t2"][:, :], a["r_mu"][:, :], a["r_mu"][:, :])
        cV.inc(nc.vector.tensor_sub(a["r_t1"][:, :], a["r_t1"][:, :], a["r_t2"][:, :]))
        cur.wait(E.Activation, cV)
        nc.scalar.activation(a["r_t2"][:, :], a["r_t1"][:, :], AF.Ln, bias=1e-5)
        cS.inc(nc.scalar.activation(a["r_rstd"][:, :], a["r_t2"][:, :], AF.Exp, scale=-0.5))
        cur.wait(E.PE, cS)
        cT.inc(nc.tensor.matmul(ps[4][:, 0:STD + 1], p.ones_r32[:, :], a["r_mu"][:, :],
                                start=True, stop=True))
        cT.inc(nc.tensor.matmul(ps[5][:, 0:STD + 1], p.ones_r32[:, :], a["r_rstd"][:, :],
                                start=True, stop=True))
        cur.wait(E.DVE, cT)
        for dt in range(8):
            nc.vector.tensor_sub(a["yln"][:, dt * 257:(dt + 1) * 257],
                                 a["xsp"][:, dt * 257:(dt + 1) * 257],
                                 ps[4][:, 0:STD + 1])
        for dt in range(8):
            v = nc.vector.tensor_mul(a["yln"][:, dt * 257:(dt + 1) * 257],
                                     a["yln"][:, dt * 257:(dt + 1) * 257],
                                     ps[5][:, 0:STD + 1])
        cV.inc(v)
        cur.wait(E.Pool, cV)
        for dt in range(8):
            sl = 1 if dt < 4 else 0
            pp = nc.gpsimd.tensor_copy(a["hb"][:, dt * STD:(dt + 1) * STD],
                                       a["yln"][:, dt * 257 + sl: dt * 257 + sl + STD])
        cP.inc(pp)
    else:
        cur.wait(E.Pool, cV)
        for dt in range(8):
            sl = 1 if dt < 4 else 0
            pp = nc.gpsimd.tensor_copy(a["hb"][:, dt * STD:(dt + 1) * STD],
                                       a["xb16"][:, dt * 257 + sl: dt * 257 + sl + STD])
        cP.inc(pp)

    def qk_tiles(kind):
        is_q = (kind == "q")
        wbase = 0 if is_q else FIN
        scl = a["sclq"] if is_q else a["sclk"]
        sfac = 8.0 if is_q else 1.0
        rc = a["rt4"][:, 0 * STD:1 * STD] if is_q else a["rt4"][:, 2 * STD:3 * STD]
        rs = a["rt4"][:, 1 * STD:2 * STD] if is_q else a["rt4"][:, 3 * STD:4 * STD]
        for bb in range(2):
            mts = [2 * bb, 2 * bb + 1]
            for j, mt in enumerate(mts):
                bq = ps[j]
                cur.wait(E.PE, cP)
                cur.wait(E.PE, cS)
                cur.wait(E.PE, cV)
                for kt in range(8):
                    mm = nc.tensor.matmul(bq[:, 0:STD],
                                          a["Wqkv"][:, kt * 1536 + wbase + mt * 128:
                                                    kt * 1536 + wbase + (mt + 1) * 128],
                                          a["hb"][:, kt * STD:(kt + 1) * STD],
                                          start=(kt == 0), stop=(kt == 7))
                cT.inc(mm)
                cur.wait(E.Activation, cT)
                cS.inc(nc.scalar.activation(a["sqt"][:, :], bq[:, 0:STD], AF.Square))
                cur.wait(E.PE, cS)
                cT.inc(nc.tensor.matmul(ps_t[0:2, 2 * 512 + j * STD: 2 * 512 + (j + 1) * STD],
                                        a["bd2"][:, :], a["sqt"][:, :],
                                        start=True, stop=True))
            cur.wait(E.DVE, cT)
            nc.vector.tensor_scalar_max(a["rr"][:, 0:2 * STD],
                                        ps_t[0:2, 2 * 512: 2 * 512 + 2 * STD], 1e-24)
            cV.inc(nc.vector.reciprocal(a["rr"][:, 0:2 * STD], a["rr"][:, 0:2 * STD]))
            cur.wait(E.Activation, cV)
            cS.inc(nc.scalar.activation(a["rr"][:, 0:2 * STD], a["rr"][:, 0:2 * STD],
                                        AF.Sqrt))
            for j, mt in enumerate(mts):
                bq = ps[j]
                cur.wait(E.PE, cS)
                cur.wait(E.PE, cV)
                cT.inc(nc.tensor.matmul(ps[4 + j][:, 0:STD], a["sel2"][:, :],
                                        a["rr"][:, j * STD:(j + 1) * STD],
                                        start=True, stop=True))
                cur.wait(E.Activation, cT)
                cS.inc(nc.scalar.copy(a["rsqs"][:, :], ps[4 + j][:, 0:STD]))
                cur.wait(E.DVE, cS)
                nc.vector.tensor_mul(a["tr4"][:, :], bq[:, 0:STD], a["rsqs"][:, :])
                cV.inc(nc.vector.tensor_scalar(a["qsb"][:, mt * STD:(mt + 1) * STD],
                                               a["tr4"][:, :], scl[:, 0:1], sfac,
                                               ALU.mult, ALU.mult))
                cur.wait(E.PE, cV)
                cT.inc(nc.tensor.matmul(ps[6 + j][:, 0:STD], a["rotpb"][:, :],
                                        a["qsb"][:, mt * STD:(mt + 1) * STD],
                                        start=True, stop=True))
                cur.wait(E.DVE, cT)
                cur.wait(E.DVE, cP)
                cV.inc(nc.vector.tensor_mul(a["tr2"][:, :], ps[6 + j][:, 0:STD],
                                            rs))
                cur.wait(E.Pool, cV)
                cur.wait(E.Pool, cG)
                cP.inc(nc.gpsimd.tensor_mul(a["tr3"][:, :],
                                            a["qsb"][:, mt * STD:(mt + 1) * STD],
                                            rc))
                cur.wait(E.Pool, cP)
                if is_q:
                    dst = a["qfb"][:, mt * STD:(mt + 1) * STD]
                else:
                    dst = a["kwin"][:, mt * 768 + 512: mt * 768 + 768]
                cP.inc(nc.gpsimd.tensor_add(dst, a["tr3"][:, :], a["tr2"][:, :]))

    qk_tiles("k")
    if not warm:
        qk_tiles("q")

    for s in range(2):
        bv = ps[0]
        cur.wait(E.PE, cS)
        cur.wait(E.PE, cV)
        cur.wait(E.PE, cP)
        for kt in range(8):
            mm = nc.tensor.matmul(bv[:, :],
                                  a["hb"][:, kt * STD + s * 128: kt * STD + (s + 1) * 128],
                                  a["Wqkv"][:, kt * 1536 + 1024: kt * 1536 + 1536],
                                  start=(kt == 0), stop=(kt == 7))
        cT.inc(mm)
        cur.wait(E.Activation, cT)
        cur.wait(E.Activation, cG)
        cS.inc(nc.scalar.copy(a["vwin"][:, (4 + s) * FIN:(5 + s) * FIN], bv[:, :]))

    if warm:
        cur.wait(E.SP, cV)
        cur.wait(E.SP, cS)
        cur.wait(E.SP, cT)
        cur.wait(E.SP, cP)
        cur.wait(E.Activation, cP)
        load_iter(i)
        return

    for s in range(2):
        oT = ps[6 + s]
        for h in range(8):
            mt, hrow = h // 2, (h % 2) * 64
            simv = ps_t[:, (h % 2) * 1024: (h % 2) * 1024 + 640]
            sim1 = ps_t[:, (h % 2) * 1024: (h % 2) * 1024 + 512]
            sim2 = ps_t[:, (h % 2) * 1024 + 512: (h % 2) * 1024 + 640]
            lhs_q = a["qfb"][hrow:hrow + 64, mt * STD + s * 128: mt * STD + (s + 1) * 128]
            kw = a["kwin"][hrow:hrow + 64, mt * 768 + s * 128: mt * 768 + s * 128 + 640]
            cur.wait(E.PE, cP)
            cur.wait(E.PE, cG)
            cur.wait(E.PE, cV)
            cur.wait(E.PE, cS)
            cT.inc(nc.tensor.matmul(sim1, lhs_q, kw[:, 0:512], start=True, stop=True))
            cT.inc(nc.tensor.matmul(sim2, lhs_q, kw[:, 512:640], start=True, stop=True))
            cur.wait(E.DVE, cT)
            nc.vector.tensor_mask_reduce(a["simm"][:, 0:512], sim1,
                                         a["qmc"][:, s:s + 1], a["e1c"][:, 0:1],
                                         1.0, -3.0e38, ALU.max, accum_out=a["mxc"][:, :])
            cV.inc(nc.vector.tensor_mask_reduce(a["simm"][:, 512:640], sim2,
                                                a["z0c"][:, 0:1], a["rp1"][:, 0:1],
                                                1.0, a["mxc"][:, 0:1], ALU.max,
                                                accum_out=a["mxc"][:, :]))
            cV.inc(nc.vector.tensor_scalar(a["nmx"][:, :], a["mxc"][:, :], -1.0, 80.0,
                                           ALU.mult, ALU.min))
            cur.wait(E.Activation, cV)
            nc.scalar.activation(a["simx"][:, :], a["simm"][:, :], AF.Exp,
                                 bias=a["nmx"][:, 0:1], accum_out=a["denc"][:, :])
            cS.inc(nc.scalar.copy(a["dencs"][:, :], a["denc"][:, :]))
            cur.wait(E.DVE, cS)
            cV.inc(nc.vector.reciprocal(a["rden"][:, :], a["dencs"][:, :]))
            cur.wait(E.Activation, cV)
            cur.wait(E.Activation, cT)
            cS.inc(nc.scalar.activation(a["simxn"][:, :], a["simx"][:, :],
                                        AF.Copy, scale=a["rden"][:, 0:1]))
            cur.wait(E.PE, cS)
            cur.wait(E.PE, cV)
            for j in range(5):
                tt = nc.tensor.transpose(
                    ps_t[:, 4 * 512 + j * 128: 4 * 512 + (j + 1) * 128],
                    a["simxn"][:, j * 128:(j + 1) * 128], a["identf"][:, :])
            cT.inc(tt)
            cur.wait(E.DVE, cT)
            cV.inc(nc.vector.tensor_copy(a["attT"][:, :],
                                         ps_t[:, 4 * 512: 4 * 512 + 640]))
            cur.wait(E.PE, cV)
            for j in range(5):
                mm = nc.tensor.matmul(
                    oT[hrow:hrow + 64, mt * 128:(mt + 1) * 128],
                    a["vwin"][:, (s + j) * FIN + h * 64: (s + j) * FIN + (h + 1) * 64],
                    a["attT"][:, j * 128:(j + 1) * 128],
                    start=(j == 0), stop=(j == 4))
            cT.inc(mm)
        cur.wait(E.DVE, cT)
        oTb3 = a["oTb"][:, :].rearrange("q (kt two n) -> q kt two n", two=2, n=128)
        cV.inc(nc.vector.tensor_copy(oTb3[:, :, s, :],
                                     oT[:, :].rearrange("q (kt n) -> q kt n", n=128)))

    for dt in range(8):
        bo = ps[dt % 4]
        cur.wait(E.PE, cV)
        for kt in range(4):
            mm = nc.tensor.matmul(bo[:, 0:STD],
                                  a["Wo"][:, kt * DM + dt * 128: kt * DM + (dt + 1) * 128],
                                  a["oTb"][:, kt * STD:(kt + 1) * STD],
                                  start=(kt == 0), stop=(kt == 3))
        cT.inc(mm)
        cur.wait(E.DVE, cT)
        resid = a["yln"] if with_ln else a["xsp"]
        cV.inc(nc.vector.tensor_add(a["xout"][:, dt * STD:(dt + 1) * STD],
                                    bo[:, 0:STD],
                                    resid[:, dt * 257 + 1:dt * 257 + 1 + STD]))

    cur.wait(E.Activation, cV)
    cur.wait(E.Activation, cP)
    spo = rcalc(p, E.Activation, p._fa_smax, imm=128, mul=(i, STD))
    cA.inc(nc.scalar.dma_start(out=dst3[:, :, ds(spo, STD)], in_=xout3_of(p, a)), 16)
    cur.wait(E.SP, cV)
    cur.wait(E.SP, cS)
    cur.wait(E.SP, cT)
    cur.wait(E.SP, cP)
    load_iter(i)


def xout3_of(p, a):
    return a["xout"][:, :].rearrange("q (dt n) -> q dt n", n=STD)



W1_TILES = build_w1_perm()[1]


def alloc_fm(p, stk, final, layer):
    nc = p.nc
    a = {}

    def sb(name, shape, dt):
        a[name] = stk.enter_context(nc.sbuf_tensor(f"fm{layer}_" + name, shape, dt))
        return a[name]

    sb("W1", [128, 8 * 2 * HID], BF16)
    sb("W2", [128, 22 * DM], BF16)
    sb("xsp", [128, FA_SPAN], F32)
    sb("xb16", [128, FA_SPAN], BF16)
    sb("sqb", [128, FA_SPAN], BF16)
    sb("zb", [128, 8 * STD], BF16)
    sb("r_mu", [1, STD], F32)
    sb("r_t1", [1, STD], F32)
    sb("r_t2", [1, STD], F32)
    sb("r_rstd", [1, STD], F32)
    sb("gam", [128, 8], F32)
    sb("bet", [128, 8], F32)
    sb("b1a", [128, 22], F32)
    sb("b1g", [128, 22], F32)
    sb("b2c", [128, 8], F32)
    sb("gel", [128, 2 * STD], F32)
    sb("glub", [128, 22 * STD], BF16)
    sb("xout", [128, 8 * STD], F32)
    if final:
        sb("x1sp", [128, 8 * STD], F32)
        sb("hi", [128, 8 * STD], BF16)
        sb("lo", [128, 8 * STD], BF16)
    return a


def phase_FM(p, layer):
    nc = p.nc
    final = (layer == 1)
    with ExitStack() as stk:
        a = alloc_fm(p, stk, final, layer)
        s_w = nc.semaphore(f"fm{layer}_w").__enter__()
        s_w2 = nc.semaphore(f"fm{layer}_w2").__enter__()
        g = nc.gpsimd
        w1_src = p.w1[layer:layer + 1, :, :].rearrange("u (kt q) o -> q (u kt) o", q=128)
        g.dma_start(out=a["W1"][:, :].rearrange("q (kt o) -> q kt o", o=2 * HID),
                    in_=w1_src).then_inc(s_w, 16)
        w2_src = p.w2[layer:layer + 1, :, :].rearrange("u (kt q) o -> q (u kt) o", q=128)
        g.dma_start(out=a["W2"][:, :].rearrange("q (kt o) -> q kt o", o=DM),
                    in_=w2_src).then_inc(s_w, 16)
        nc.sync.dma_start(out=a["gam"][:, :],
                          in_=p.ffn_ln[:, layer * 16:layer * 16 + 8]).then_inc(s_w2, 16)
        nc.sync.dma_start(out=a["bet"][:, :],
                          in_=p.ffn_ln[:, layer * 16 + 8:layer * 16 + 16]).then_inc(s_w2, 16)
        nc.sync.dma_start(out=a["b1a"][:, :],
                          in_=p.ffn_b1c[:, layer * 44:layer * 44 + 22]).then_inc(s_w2, 16)
        nc.sync.dma_start(out=a["b1g"][:, :],
                          in_=p.ffn_b1c[:, layer * 44 + 22:layer * 44 + 44]).then_inc(s_w2, 16)
        nc.sync.dma_start(out=a["b2c"][:, :],
                          in_=p.ffn_b2c[:, layer * 8:layer * 8 + 8]).then_inc(s_w2, 16)
        for e in ALLE:
            nc.engines[e].wait_ge(s_w, 2 * 16)
            nc.engines[e].wait_ge(s_w2, 5 * 16)
        emit_fm_loop(p, a, layer)
        p.barrier()


def emit_fm_loop(p, a, layer):
    nc = p.nc
    mark = pro_mark(p)
    final = (layer == 1)
    tag = f"fm{layer}"
    cL = Ctr(nc, tag + "L", p.deltas)
    cA = Ctr(nc, tag + "A", p.deltas)
    cS = Ctr(nc, tag + "S", p.deltas)
    cV = Ctr(nc, tag + "V", p.deltas)
    cT = Ctr(nc, tag + "T", p.deltas)
    cP = Ctr(nc, tag + "P", p.deltas)
    ctrs = (cL, cA, cS, cV, cT, cP)

    if layer == 0:
        n = p.cfgv(CFG_N_D2, ALLE, 64)
        base = p.cfgv(CFG_B_D2, [E.SP], SEQ)
        src_sp, src_base_cfg = p.sp_bm, CFG_B_D1
        src_cols = 4096 + STD
    else:
        n = p.cfgv(CFG_N_E2, ALLE, 64)
        base = p.cfgv(CFG_S, [E.SP], SEQ)
        src_sp, src_base_cfg = p.sp_l1, CFG_B_E1
        src_cols = 3072 + STD
    src_base = p.cfgv(src_base_cfg, [E.SP], SEQ)
    xsp3 = a["xsp"][:, :].rearrange("q (dt n) -> q dt n", n=STD + 1)
    src3 = src_sp.rearrange("(dt q) n -> q dt n", q=128)
    if layer == 0:
        dst3 = p.sp_l0.rearrange("(dt q) n -> q dt n", q=128)
    if final:
        x1b_cfg = p.cfgv(CFG_B_C, [E.SP], SEQ)
        x1sp3 = a["x1sp"][:, :].rearrange("q (dt n) -> q dt n", n=STD)
        x1src3 = p.sp_x1.rearrange("(dt q) n -> q dt n", q=128)
        oh3 = p.sp_oh.rearrange("(dt q) n -> q dt n", q=128)
        ol3 = p.sp_ol.rearrange("(dt q) n -> q dt n", q=128)

    xmax = src_cols - (STD + 1)

    def load_iter(i_rv):
        if i_rv is None:
            offX = rcalc(p, E.SP, xmax, imm=127, add=(base,), sub=(src_base,))
        else:
            offX = rcalc(p, E.SP, xmax, imm=127 + STD, mul=(i_rv, STD),
                         add=(base,), sub=(src_base,))
        cL.inc(nc.sync.dma_start(out=xsp3, in_=src3[:, :, ds(offX, STD + 1)],
                                  bounds_check="skip_entire_dma"), 16)
        if final:
            if i_rv is None:
                off1 = rcalc(p, E.SP, SPX1_COLS, imm=128, add=(base,), sub=(x1b_cfg,))
            else:
                off1 = rcalc(p, E.SP, SPX1_COLS, imm=128 + STD, mul=(i_rv, STD),
                             add=(base,), sub=(x1b_cfg,))
            cL.inc(nc.sync.dma_start(out=x1sp3, in_=x1src3[:, :, ds(off1, STD)],
                                      bounds_check="skip_entire_dma"), 16)

    load_iter(None)
    with kfori(p, n) as i:
        for c in ctrs:
            c.begin_body()
        begin_snaps(p)
        cur = Cursors(p, i, ctrs)
        body_start_waits(cur, ctrs, engines=ALLE)
        emit_fm_body(p, a, layer, i, cL, cA, cS, cV, cT, cP, cur, load_iter,
                     dst3 if layer == 0 else None,
                     (oh3, ol3) if final else None)
        for c in ctrs:
            c.end_body()
        cur.finish()
        end_snaps(p)
    drain_all(p, ctrs, n)
    pro_free(p, mark)


def emit_fm_body(p, a, layer, i, cL, cA, cS, cV, cT, cP, cur, load_iter,
                 dst3, ohl):
    nc = p.nc
    ps = p.ps
    final = (layer == 1)

    cur.wait(E.DVE, cL)
    cV.inc(nc.vector.tensor_copy(a["xb16"][:, :], a["xsp"][:, :]))
    cur.wait(E.Activation, cV)
    cS.inc(nc.scalar.activation(a["sqb"][:, :], a["xb16"][:, :], AF.Square))
    cur.wait(E.PE, cV)
    for dt in range(8):
        sl = 1 if dt < 4 else 0
        mm = nc.tensor.matmul(ps[6][0:1, 0:STD], p.ones_b[:, :],
                              a["xb16"][:, dt * 257 + sl: dt * 257 + sl + STD],
                              start=(dt == 0), stop=(dt == 7))
    cT.inc(mm)
    cur.wait(E.PE, cS)
    for dt in range(8):
        sl = 1 if dt < 4 else 0
        mm = nc.tensor.matmul(ps[7][0:1, 0:STD], p.ones_b[:, :],
                              a["sqb"][:, dt * 257 + sl: dt * 257 + sl + STD],
                              start=(dt == 0), stop=(dt == 7))
    cT.inc(mm)
    cur.wait(E.DVE, cT)
    nc.vector.tensor_scalar_mul(a["r_mu"][:, :], ps[6][0:1, 0:STD], 1.0 / DM)
    nc.vector.tensor_scalar_mul(a["r_t1"][:, :], ps[7][0:1, 0:STD], 1.0 / DM)
    nc.vector.tensor_mul(a["r_t2"][:, :], a["r_mu"][:, :], a["r_mu"][:, :])
    cV.inc(nc.vector.tensor_sub(a["r_t1"][:, :], a["r_t1"][:, :], a["r_t2"][:, :]))
    cur.wait(E.Activation, cV)
    nc.scalar.activation(a["r_t2"][:, :], a["r_t1"][:, :], AF.Ln, bias=1e-5)
    cS.inc(nc.scalar.activation(a["r_rstd"][:, :], a["r_t2"][:, :], AF.Exp, scale=-0.5))
    cur.wait(E.PE, cS)
    cT.inc(nc.tensor.matmul(ps[4][:, 0:STD], p.ones_r32[:, :], a["r_mu"][:, :],
                            start=True, stop=True))
    cT.inc(nc.tensor.matmul(ps[5][:, 0:STD], p.ones_r32[:, :], a["r_rstd"][:, :],
                            start=True, stop=True))
    cur.wait(E.DVE, cT)
    for dt in range(8):
        sl = 1 if dt < 4 else 0
        g0 = a["gel"][:, 0:STD]
        nc.vector.tensor_sub(g0,
                             a["xsp"][:, dt * 257 + sl: dt * 257 + sl + STD],
                             ps[4][:, 0:STD])
        nc.vector.tensor_mul(g0, g0, ps[5][:, 0:STD])
        v = nc.vector.tensor_scalar(a["zb"][:, dt * STD:(dt + 1) * STD],
                                    g0, a["gam"][:, dt:dt + 1],
                                    a["bet"][:, dt:dt + 1], ALU.mult, ALU.add)
    cV.inc(v)

    rotW = MMRot(p, [0, 1, 2, 3], cur)
    gluK = {}
    for pi, (a_off, g_off, w) in enumerate(W1_TILES):
        bankap, bi = rotW.group()
        bank = ps[bi]
        if pi == 0:
            cur.wait(E.PE, cV)
            cur.wait(E.PE, cS)
        for kt in range(8):
            nc.tensor.matmul(bank[0:w, 0:STD],
                             a["W1"][:, kt * 2 * HID + a_off: kt * 2 * HID + a_off + w],
                             a["zb"][:, kt * STD:(kt + 1) * STD],
                             start=(kt == 0), stop=False)
        for kt in range(8):
            mm = nc.tensor.matmul(bank[0:w, 256:256 + STD],
                                  a["W1"][:, kt * 2 * HID + g_off: kt * 2 * HID + g_off + w],
                                  a["zb"][:, kt * STD:(kt + 1) * STD],
                                  start=False, stop=(kt == 7))
        cT.inc(mm)
        gph = a["gel"][0:w, (pi % 2) * STD:(pi % 2) * STD + STD]
        if pi >= 2:
            cur.wait(E.Activation, cV, K=gluK[pi - 2])
        cur.wait(E.Activation, cT)
        cS.inc(nc.scalar.activation(gph, bank[0:w, 256:256 + STD],
                                    AF.Gelu, bias=a["b1g"][0:w, pi:pi + 1]))
        cur.wait(E.DVE, cS)
        cV.inc(nc.vector.scalar_tensor_tensor(
            a["glub"][0:w, pi * STD:(pi + 1) * STD],
            bank[0:w, 0:STD], a["b1a"][0:w, pi:pi + 1],
            gph, ALU.add, ALU.mult))
        gluK[pi] = cV.static + cV.cur
        rotW.done(bi, cV)

    rot2 = MMRot(p, [4, 5, 6, 7], cur)
    for dt in range(8):
        bankap, bi2 = rot2.group()
        bank = ps[bi2]
        if dt == 0:
            cur.wait(E.PE, cV)
        for kt in range(22):
            kk = 128 if kt < 21 else HID - 21 * 128
            mm = nc.tensor.matmul(bank[:, 0:STD],
                                  a["W2"][0:kk, kt * DM + dt * 128: kt * DM + (dt + 1) * 128],
                                  a["glub"][0:kk, kt * STD:(kt + 1) * STD],
                                  start=(kt == 0), stop=(kt == 21))
        cT.inc(mm)
        cur.wait(E.DVE, cT)
        cV.inc(nc.vector.scalar_tensor_tensor(
            a["xout"][:, dt * STD:(dt + 1) * STD], bank[:, 0:STD],
            a["b2c"][:, dt:dt + 1],
            a["xsp"][:, dt * 257 + 1: dt * 257 + 1 + STD], ALU.add, ALU.add))
        rot2.done(bi2, cV)

    if not final:
        cur.wait(E.Activation, cV)
        spo = rcalc(p, E.Activation, 4096, imm=128, mul=(i, STD))
        cA.inc(nc.scalar.dma_start(out=dst3[:, :, ds(spo, STD)],
                                   in_=xout3_of(p, a)), 16)
        cur.wait(E.SP, cV)
        cur.wait(E.SP, cS)
        cur.wait(E.SP, cT)
        load_iter(i)
        return

    cur.wait(E.DVE, cV)
    cur.wait(E.DVE, cT)
    cV.inc(nc.vector.tensor_copy(a["xb16"][:, 0:8 * STD], a["xout"][:, :]))
    cur.wait(E.Activation, cV)
    cS.inc(nc.scalar.activation(a["sqb"][:, 0:8 * STD], a["xb16"][:, 0:8 * STD],
                                AF.Square))
    cur.wait(E.PE, cV)
    for dt in range(8):
        mm = nc.tensor.matmul(ps[2][0:1, 0:STD], p.ones_b[:, :],
                              a["xb16"][:, dt * STD:(dt + 1) * STD],
                              start=(dt == 0), stop=(dt == 7))
    cT.inc(mm)
    cur.wait(E.PE, cS)
    for dt in range(8):
        mm = nc.tensor.matmul(ps[3][0:1, 0:STD], p.ones_b[:, :],
                              a["sqb"][:, dt * STD:(dt + 1) * STD],
                              start=(dt == 0), stop=(dt == 7))
    cT.inc(mm)
    cur.wait(E.DVE, cT)
    nc.vector.tensor_scalar_mul(a["r_mu"][:, :], ps[2][0:1, 0:STD], 1.0 / DM)
    nc.vector.tensor_scalar_mul(a["r_t1"][:, :], ps[3][0:1, 0:STD], 1.0 / DM)
    nc.vector.tensor_mul(a["r_t2"][:, :], a["r_mu"][:, :], a["r_mu"][:, :])
    cV.inc(nc.vector.tensor_sub(a["r_t1"][:, :], a["r_t1"][:, :], a["r_t2"][:, :]))
    cur.wait(E.Activation, cV)
    nc.scalar.activation(a["r_t2"][:, :], a["r_t1"][:, :], AF.Ln, bias=1e-5)
    cS.inc(nc.scalar.activation(a["r_rstd"][:, :], a["r_t2"][:, :], AF.Exp, scale=-0.5))
    cur.wait(E.PE, cS)
    cT.inc(nc.tensor.matmul(ps[0][:, 0:STD], p.ones_r32[:, :], a["r_mu"][:, :],
                            start=True, stop=True))
    cT.inc(nc.tensor.matmul(ps[1][:, 0:STD], p.ones_r32[:, :], a["r_rstd"][:, :],
                            start=True, stop=True))
    cur.wait(E.DVE, cT)
    for dt in range(8):
        d = a["xout"][:, dt * STD:(dt + 1) * STD]
        nc.vector.tensor_sub(d, d, ps[0][:, 0:STD])
        nc.vector.tensor_mul(d, d, ps[1][:, 0:STD])
        v = nc.vector.tensor_add(d, d, a["x1sp"][:, dt * STD:(dt + 1) * STD])
    cV.inc(v)
    nc.vector.tensor_copy(a["hi"][:, :], a["xout"][:, :])
    nc.vector.tensor_sub(a["xout"][:, :], a["xout"][:, :], a["hi"][:, :])
    cV.inc(nc.vector.tensor_copy(a["lo"][:, :], a["xout"][:, :]))
    cur.wait(E.SP, cV)
    oh3, ol3 = ohl
    oc = rcalc(p, E.SP, OUT_ROWS - STD, mul=(i, STD))
    cL.inc(nc.sync.dma_start(
        out=oh3[:, :, ds(oc, STD)],
        in_=a["hi"][:, :].rearrange("q (dt n) -> q dt n", n=STD)), 16)
    cL.inc(nc.sync.dma_start(
        out=ol3[:, :, ds(oc, STD)],
        in_=a["lo"][:, :].rearrange("q (dt n) -> q dt n", n=STD)), 16)
    cur.wait(E.SP, cS)
    cur.wait(E.SP, cT)
    load_iter(i)



def phase_F(p):
    nc = p.nc
    with (
        nc.sbuf_tensor("f_th", [128, DM], BF16) as fth,
        nc.sbuf_tensor("f_tl", [128, DM], BF16) as ftl,
        nc.sbuf_tensor("f_o", [128, DM], F32) as fo,
    ):
        cL2 = Ctr(nc, "f2L", p.deltas)
        cV2 = Ctr(nc, "f2V", p.deltas)
        nT128b = p.cfgv(CFG_T128, ALLE, SEQ // 128)

        with kfori(p, nT128b) as i:
            for c in (cL2, cV2):
                c.begin_body()
            cur2 = Cursors(p, i, (cL2, cV2))
            begin_snaps(p)
            body_start_waits(cur2, (cL2, cV2), engines=[E.Activation, E.DVE])
            oc2 = rcalc(p, E.Activation, OUT_ROWS - 128, mul=(i, 128))
            cL2.inc(nc.scalar.dma_start(out=fth[:, :], in_=p.sp_oh[:, ds(oc2, 128)],
                                      transpose=True), 16)
            cL2.inc(nc.scalar.dma_start(out=ftl[:, :], in_=p.sp_ol[:, ds(oc2, 128)],
                                      transpose=True), 16)
            cur2.wait(E.DVE, cL2)
            cV2.inc(nc.vector.tensor_add(fo[:, :], fth[:, :], ftl[:, :]))
            cur2.wait(E.Activation, cV2)
            cL2.inc(nc.scalar.dma_start(out=p.out[ds(oc2, 128), :], in_=fo[:, :]), 16)
            for c in (cL2, cV2):
                c.end_body()
            cur2.finish()
            end_snaps(p)
        nf = p.cfgv(CFG_T128, [E.Activation], SEQ // 128)
        nc.scalar.wait_ge(cL2.h, nf * cL2.delta + cL2.static)



def make_in_maps(inputs):
    x0 = np.ascontiguousarray(np.asarray(inputs["L0_attn_x"], np.float32).reshape(SEQ, DM))
    x0 = np.concatenate([x0, np.zeros((XC_TOK - ST, DM), np.float32)], 0)
    plans = plan_shards()
    rqc, rqs, rkc, rks = build_rotary_tables()
    rot_all = np.ascontiguousarray(np.concatenate([rqc, rqs, rkc, rks], axis=1))
    rotp = build_rotp()
    w1perm, _w1tiles = build_w1_perm()
    ident = np.eye(128, dtype=np.float32)
    projT_aug = np.concatenate([
        np.ascontiguousarray(np.asarray(inputs["L0_attn_proj"], np.float32).T),
        -np.ones((1, PM), np.float32)], 0)
    negi = np.broadcast_to(-np.arange(ST, dtype=np.float32)[None, :], (128, ST)).copy()
    alen = np.zeros((128, 4), np.float32)
    for j in range(4):
        alen[:, j] = 1.0 - (j * 128 + np.arange(128))
    rowp1 = (np.arange(128, dtype=np.float32) + 1.0).reshape(128, 1)
    sel2m = np.zeros((2, 128), np.float32)
    sel2m[0, 0:64] = 1.0
    sel2m[1, 64:128] = 1.0
    qsc = np.asarray(inputs["L0_ffn_q_scale"], np.float32)
    ksc = np.asarray(inputs["L0_ffn_k_scale"], np.float32)
    ffn_scl = np.zeros((128, 4), np.float32)
    for l in range(2):
        ffn_scl[:, 2 * l + 0] = np.concatenate([qsc[l], qsc[l]])
        ffn_scl[:, 2 * l + 1] = np.concatenate([ksc[l], ksc[l]])
    lg = np.asarray(inputs["L0_ffn_ln_g"], np.float32)
    lb = np.asarray(inputs["L0_ffn_ln_b"], np.float32)
    ffn_ln = np.zeros((128, 32), np.float32)
    for l in range(2):
        ffn_ln[:, l * 16:l * 16 + 8] = col_layout(lg[l])
        ffn_ln[:, l * 16 + 8:l * 16 + 16] = col_layout(lb[l])
    w1p = np.ascontiguousarray(np.asarray(inputs["L0_ffn_w1"], np.float32)[:, :, w1perm])
    b1p = np.ascontiguousarray(np.asarray(inputs["L0_ffn_b1"], np.float32)[:, w1perm])
    w2_ = np.asarray(inputs["L0_ffn_w2"], np.float32)
    w2p = np.zeros((2, HID_PAD, DM), np.float32)
    w2p[:, :HID, :] = w2_
    b2_ = np.asarray(inputs["L0_ffn_b2"], np.float32)
    ffn_b1c = np.zeros((128, 88), np.float32)
    for l in range(2):
        for pi, (a_off, g_off, w) in enumerate(_w1tiles):
            ffn_b1c[:w, l * 44 + pi] = b1p[l, a_off:a_off + w]
            ffn_b1c[:w, l * 44 + 22 + pi] = b1p[l, g_off:g_off + w]
    ffn_b2c = np.zeros((128, 16), np.float32)
    for l in range(2):
        ffn_b2c[:, l * 8:l * 8 + 8] = col_layout(b2_[l])

    def qm_table(plan, base_key, ncols):
        t = np.zeros((128, ncols), np.float32)
        r = np.arange(128, dtype=np.float32)
        B = plan[base_key]
        for j in range(ncols):
            c0s = B + j * 128
            t[:, j] = np.maximum(r, 512.0 - c0s)
        return t

    in_maps = []
    for c, pl in enumerate(plans):
        cfg = np.zeros((1, 32), np.uint32)
        vals = {CFG_B_C: pl["B_C"], CFG_N_LO: pl["n_lo"], CFG_N_FEED: pl["n_feed"],
                CFG_N_C: pl["n_c"], CFG_S: pl["S"], CFG_B_D1: pl["B_D1"],
                CFG_N_D1W: pl["n_d1w"], CFG_N_D1: pl["n_d1"], CFG_B_D2: pl["B_D2"],
                CFG_N_D2: pl["n_d2"], CFG_B_E1: pl["B_E1"], CFG_N_E1W: pl["n_e1w"],
                CFG_N_E1: pl["n_e1"], CFG_N_E2: pl["n_e2"], CFG_B_D1W: pl["B_D1W"],
                CFG_B_E1W: pl["B_E1W"], CFG_T128: pl["T"] // 128,
                CFG_OFFX1: pl["S"] - pl["B_C"]}
        for k, v in vals.items():
            cfg[0, k] = v
        m = dict(
            L0_attn_x=x0,
            L0_attn_q_w=np.asarray(inputs["L0_attn_q_w"], np.float32),
            L0_attn_q_b=np.asarray(inputs["L0_attn_q_b"], np.float32).reshape(1, DM),
            L0_attn_k_w=np.asarray(inputs["L0_attn_k_w"], np.float32),
            L0_attn_k_b=np.asarray(inputs["L0_attn_k_b"], np.float32).reshape(1, DM),
            L0_attn_v_w=np.asarray(inputs["L0_attn_v_w"], np.float32),
            L0_attn_v_b=np.asarray(inputs["L0_attn_v_b"], np.float32).reshape(1, DM),
            L0_attn_out_w=np.asarray(inputs["L0_attn_out_w"], np.float32),
            L0_attn_out_b=np.asarray(inputs["L0_attn_out_b"], np.float32).reshape(1, DM),
            projT_aug=projT_aug,
            L0_attn_gamma=col_layout(inputs["L0_attn_gamma"]),
            L0_attn_beta=col_layout(inputs["L0_attn_beta"]),
            L0_ffn_wqkv=np.asarray(inputs["L0_ffn_wqkv"], np.float32),
            L0_ffn_wo=np.asarray(inputs["L0_ffn_wo"], np.float32),
            L0_ffn_w1=w1p, L0_ffn_b1=b1p, L0_ffn_w2=w2p,
            L0_ffn_b2=np.asarray(inputs["L0_ffn_b2"], np.float32),
            cfg=cfg, negi=negi, alen=alen,
            rot_all=rot_all, rotp=rotp, ident=ident,
            rowp1=rowp1, ffn_scl=ffn_scl, ffn_ln=ffn_ln, sel2m=sel2m,
            ffn_b1c=ffn_b1c, ffn_b2c=ffn_b2c,
            qm_d1=qm_table(pl, "B_D1", 32), qm_e1=qm_table(pl, "B_E1", 24),
        )
        in_maps.append(m)
    return in_maps, plans


def kernel(**inputs):
    in_maps, plans = make_in_maps(inputs)
    deltas = {}
    build_program(deltas)
    pr = build_program(deltas)
    res = run_bass_kernel_spmd(pr.nc, in_maps, core_ids=list(range(NCORES)))
    out = np.empty((SEQ, DM), np.float32)
    for c, pl in enumerate(plans):
        out[pl["S"]:pl["E"], :] = res.results[c]["out"][: pl["T"], :]
    return out.reshape(1, SEQ, DM)
